# version 18
# baseline (speedup 1.0000x reference)
import contextlib
import numpy as np
import ml_dtypes

import concourse.bass as bass
import concourse.mybir as mybir
from concourse.bass_utils import run_bass_kernel_spmd

F32 = mybir.dt.float32
BF16 = mybir.dt.bfloat16
AF = mybir.ActivationFunctionType
ALU = mybir.AluOpType
AX = mybir.AxisListType

NCORES = 8
NB = 4
T = 2048
D = 1024
H = 16
HD = 64
NDC = 8
SEG = 1024
NSEG = 2
NTC = 8
QTL = 512
NQT = 2
TOKQ = 256
NTQ = 4
NE = 8
IE = 512
NU = 12
DBG_NB = None
DBG_NSEG = None
DBG_PHASES = "abc"
DBG_A_PASSES = 3
DBG_A_LN = 9
DBG_B_NH = 16
DBG_B_LVL = 9
DBG_PV_BANK = 4
DBG_PRO_BCAST = 1
EPS = 1e-5

ENGINES = ("tensor", "scalar", "vector", "sync", "gpsimd")
SAFE_SYNC = False
SH_A = 4


class Sem:
    def __init__(self, handle, name):
        self.h = handle
        self.name = name
        self.n = 0


class Plan:

    def __init__(self):
        self.items = {e: [] for e in ENGINES}

    def emit(self, eng, fn, incs=(), nodrain=False):
        incs = tuple(incs)
        if (eng == "vector" and not nodrain) or (eng == "scalar" and SAFE_SYNC):
            inner = fn
            sep = "drain"
            if eng == "scalar":
                sep = "drain"

            def fn(v, _inner=inner, sep=sep):
                getattr(v, sep)()
                return _inner(v)
        self.items[eng].append(("op", fn, incs))
        for sem, k in incs:
            sem.n += k

    def wait(self, eng, sem, val=None):
        v = sem.n if val is None else val
        if v > 0:
            self.items[eng].append(("wait", sem, v))

    def run(self, eng_name, eng):
        for it in self.items[eng_name]:
            if it[0] == "op":
                _, fn, incs = it
                ins = fn(eng)
                for sem, k in incs:
                    ins.then_inc(sem.h, k)
            else:
                _, sem, v = it
                eng.wait_ge(sem.h, v)


def build_core():
    nc = bass.Bass("TRN2", target_bir_lowering=False, debug=False)

    dx = nc.dram_tensor("x", [NB, T, D], F32, kind="ExternalInput").ap()
    dwq = nc.dram_tensor("wq", [128, NDC, D], F32, kind="ExternalInput").ap()
    dwk = nc.dram_tensor("wk", [128, NDC, D], F32, kind="ExternalInput").ap()
    dwv = nc.dram_tensor("wv", [128, NDC, D], F32, kind="ExternalInput").ap()
    dwo = nc.dram_tensor("wo", [128, NDC, D], F32, kind="ExternalInput").ap()
    dbo = nc.dram_tensor("bo", [1, D], F32, kind="ExternalInput").ap()
    dklw = nc.dram_tensor("klw", [1, HD], F32, kind="ExternalInput").ap()
    dklb = nc.dram_tensor("klb", [1, HD], F32, kind="ExternalInput").ap()
    dqlw = nc.dram_tensor("qlw", [1, HD], F32, kind="ExternalInput").ap()
    dqlb = nc.dram_tensor("qlb", [1, HD], F32, kind="ExternalInput").ap()
    drw = nc.dram_tensor("rw", [128, NDC, NE], F32, kind="ExternalInput").ap()
    drb = nc.dram_tensor("rb", [1, 1], F32, kind="ExternalInput").ap()
    dgw = nc.dram_tensor("gw", [NE, 128, NDC, IE], F32, kind="ExternalInput").ap()
    duw = nc.dram_tensor("uw", [NE, 128, NDC, IE], F32, kind="ExternalInput").ap()
    ddw = nc.dram_tensor("dw", [NE, 128, 4, D], F32, kind="ExternalInput").ap()
    dsg = nc.dram_tensor("sg", [4, 128, NDC, IE], F32, kind="ExternalInput").ap()
    dsu = nc.dram_tensor("su", [4, 128, NDC, IE], F32, kind="ExternalInput").ap()
    dsd = nc.dram_tensor("sd", [4, 128, 4, D], F32, kind="ExternalInput").ap()
    cwq = nc.dram_tensor("cwq", [128, NDC * D], BF16, kind="Internal").ap()
    cwk = nc.dram_tensor("cwk", [128, NDC * D], BF16, kind="Internal").ap()
    cwv = nc.dram_tensor("cwv", [128, NDC * D], BF16, kind="Internal").ap()
    cwo = nc.dram_tensor("cwo", [128, NDC * D], BF16, kind="Internal").ap()
    cgw = nc.dram_tensor("cgw", [NE, 128, NDC * IE], BF16, kind="Internal").ap()
    cuw = nc.dram_tensor("cuw", [NE, 128, NDC * IE], BF16, kind="Internal").ap()
    cdw = nc.dram_tensor("cdw", [NE, 128, 4 * D], BF16, kind="Internal").ap()
    csg = nc.dram_tensor("csg", [4, 128, NDC * IE], BF16, kind="Internal").ap()
    csu = nc.dram_tensor("csu", [4, 128, NDC * IE], BF16, kind="Internal").ap()
    csd = nc.dram_tensor("csd", [4, 128, 4 * D], BF16, kind="Internal").ap()
    dident = nc.dram_tensor("ident", [128, 128], BF16, kind="ExternalInput").ap()
    dtri = nc.dram_tensor("tri", [128, 1280], BF16, kind="ExternalInput").ap()
    dout = nc.dram_tensor("out", [NB, T, D], F32, kind="ExternalOutput").ap()

    es = contextlib.ExitStack()
    S = lambda n, sh, dt: es.enter_context(nc.sbuf_tensor(n, sh, dt))

    ident_b = S("ident_b", [128, 128], BF16)
    tri_b = S("tri_b", [128, 1280], BF16)
    ones_b = S("ones_b", [1, 128], BF16)
    ones_f = S("ones_f", [1, 128], F32)
    bo_row_b = S("bo_row_b", [1, D], BF16)
    lnrows = S("lnrows", [1, 4 * HD], F32)
    klw_r = lnrows.ap()[:, 0:HD]
    klb_r = lnrows.ap()[:, HD:2 * HD]
    qlw_r = lnrows.ap()[:, 2 * HD:3 * HD]
    qlb_r = lnrows.ap()[:, 3 * HD:4 * HD]
    rb11 = S("rb11", [1, 1], F32)
    rbias_col = S("rbias_col", [128, 1], F32)
    eps_col = S("eps_col", [128, 1], F32)
    qkcol = S("qkcol", [128, 4], F32)
    e64_col = S("e64_col", [128, 1], F32)
    ln8_col = S("ln8_col", [128, 1], F32)
    rw_sb = S("rw_sb", [128, NDC * NE], BF16)
    ktall = S("ktall", [128, NDC * T], BF16)
    vsb = S("vsb", [128, 16 * (H * 65)], BF16)
    h1f = S("h1f", [128, NTC * D], F32)
    xf = [S("xf0", [128, D], F32), S("xf1", [128, D], F32)]
    sqf2 = [S("sqf0", [128, D], F32), S("sqf1", [128, D], F32)]
    sqf = sqf2[0]
    st16p = [{k: S(f"st16_{k}{i}", [128, H], F32)
              for k in ("sum", "sq", "mean", "var", "rstd", "nmr")}
             for i in range(2)]
    st1 = {k: S("st1_" + k, [128, 1], F32)
           for k in ("sum", "sq", "mean", "var", "rstd")}
    w_seg = S("w_seg", [128, NTC * NE], F32)
    rrow_f = S("rrow_f", [1, QTL], F32)
    rrow_b = S("rrow_b", [1, 2 * QTL], BF16)
    bc_sb = S("bc_sb", [64, 2 * QTL], BF16)
    r8 = {k: S("r8_" + k, [128, NTC * NE], F32)
          for k in ("logit", "score", "sel", "msk")}
    r8["corr"] = w_seg
    m8w = r8["logit"]
    rcol = {k: S("rcol_" + k, [128, NTC], F32)
            for k in ("nmax", "den", "rden", "den4", "rden4")}

    wcur = [S("wcur0", [128, NDC * D], BF16), S("wcur1", [128, NDC * D], BF16)]
    scrab = S("scrab", [128, 8192], BF16)
    qtseg = S("qtseg", [128, NDC * SEG], BF16)
    aot = S("aot", [128, NDC * QTL], BF16)
    bb = [aot.ap()[:, 0:D], aot.ap()[:, D:2 * D]]
    nscr = [S("nscr0", [128, D], F32), S("nscr1", [128, D], F32)]

    h1T = wcur[0]
    gub = [qtseg, scrab]
    down_sb = aot
    silu_sb = nscr[0].ap().bitcast(BF16)
    actw = nscr[1].ap().bitcast(BF16)

    ps = es.enter_context(nc.psum_tensor("ps", [128, 4096], F32))

    def bank(b, w=512):
        return ps[:, 512 * b: 512 * b + w]

    def bank_b16(b, off, cols):
        return ps[:, 512 * b + off // 2: 512 * b + (off + cols) // 2].bitcast(BF16)

    sems = {}
    for name in ("scs", "sx0", "sx1", "sw", "sgu0", "sgu1", "sdn", "sgp",
                 "so0", "so1", "st", "ss", "sv"):
        sems[name] = Sem(es.enter_context(nc.semaphore(name)), name)
    scs = sems["scs"]
    sxp = [sems["sx0"], sems["sx1"]]
    sw = sems["sw"]
    sgu = [sems["sgu0"], sems["sgu1"]]
    sdn = sems["sdn"]
    sgp = sems["sgp"]
    sop = [sems["so0"], sems["so1"]]
    st_, ss, sv = sems["st"], sems["ss"], sems["sv"]

    P = Plan()
    EV = {}

    def xTv(t, d):
        return scrab[:, SEG * d + 128 * t: SEG * d + 128 * (t + 1)]

    def tr_part(src_bb, par, producer_key):
        trb = 4 + par
        P.wait("tensor", ss, EV.get((producer_key, par), 0))
        P.wait("tensor", sv, EV.get("ps47_dve", 0))
        for d_ in range(NDC):
            P.emit("tensor", lambda e, d_=d_, trb=trb, src_bb=src_bb:
                   e.transpose(bank_b16(trb, 128 * d_, 128),
                               src_bb[:, 128 * d_:128 * (d_ + 1)], ident_b[:]),
                   [(st_, 1)] if d_ == NDC - 1 else ())
        EV[("bb_rd", par)] = st_.n
        return st_.n, trb

    def ev_part(dst3, trb, tr_done, par, producer_key, scale=None, bias=None):
        P.wait("scalar", st_, tr_done)
        src3 = bank_b16(trb, 0, 1024).rearrange("p (d c) -> p d c", c=128)
        if scale is None:
            P.emit("scalar", lambda e, dst3=dst3, src3=src3:
                   e.copy(dst3, src3), [(ss, 1)])
        else:
            P.emit("scalar", lambda e, dst3=dst3, src3=src3, scale=scale,
                   bias=bias: e.activation(dst3, src3, AF.Identity,
                                           scale=scale, bias=bias), [(ss, 1)])
        EV[(producer_key, par)] = ss.n

    def tr_evac(src_bb, dst3, par, producer_key, scale=None, bias=None):
        tr_done, trb = tr_part(src_bb, par, producer_key)
        ev_part(dst3, trb, tr_done, par, producer_key, scale, bias)

    for ap_in, tile in [(dident, ident_b[:]), (dtri, tri_b[:]),
                        (dklw, klw_r), (dklb, klb_r), (dqlw, qlw_r),
                        (dqlb, qlb_r), (drb, rb11[:])]:
        P.emit("sync", lambda e, a=ap_in, t=tile: e.dma_start(out=t, in_=a),
               [(scs, 16)])
    for j, srow in enumerate([dqlw, dqlb, dklw, dklb]):
        for half in range(2):
            P.emit("sync", lambda e, srow=srow, j=j, half=half: e.dma_start(
                out=qkcol[64 * half:64 * (half + 1), j:j + 1], in_=srow),
                [(scs, 16)])
    P.wait("scalar", scs)
    P.emit("gpsimd", lambda e: e.dma_start(
        out=rw_sb.ap().rearrange("p (c e) -> p c e", e=NE), in_=drw),
        [(sgp, 16)])
    P.emit("gpsimd", lambda e: e.dma_start(out=bo_row_b[:], in_=dbo), [(sgp, 16)])
    cast_done = {}
    for nm, csrc, cdst in [("wq", dwq, cwq), ("wk", dwk, cwk), ("wv", dwv, cwv),
                           ("wo", dwo, cwo), ("gw", dgw, cgw), ("uw", duw, cuw),
                           ("dw", ddw, cdw), ("sg", dsg, csg), ("su", dsu, csu),
                           ("sd", dsd, csd)]:
        P.emit("gpsimd", lambda e, a=csrc, b=cdst: e.dma_start(
            out=b, in_=a.rearrange("... p c f -> ... p (c f)")
            if len(a.shape) == 4 else a.rearrange("p c f -> p (c f)")),
            [(sgp, 16)])
    for nm in ("wq", "wk", "wv", "wo", "gw", "uw", "dw", "sg", "su", "sd"):
        cast_done[nm] = sgp.n
    P.wait("vector", scs)
    P.wait("vector", sgp)
    P.emit("vector", lambda e: e.memset(ones_b[:], 1.0), [(sv, 1)])
    P.emit("vector", lambda e: e.memset(ones_f[:], 1.0), [(sv, 1)])
    P.emit("vector", lambda e: e.memset(eps_col[:], EPS), [(sv, 1)])
    P.emit("vector", lambda e: e.memset(e64_col[:], HD * EPS), [(sv, 1)])
    P.emit("vector", lambda e: e.memset(ln8_col[:], 2.0794415416798357), [(sv, 1)])
    P.emit("vector", lambda e: e.memset(
        vsb.ap().rearrange("p (g h c) -> p g h c", h=H, c=65)[:, :, :, 64:65],
        1.0), [(sv, 1)])
    P.wait("tensor", sv)
    for i, (dst, srow, cols) in enumerate([(rbias_col, rb11, 1)]):
        if DBG_PRO_BCAST:
            P.emit("tensor", lambda e, srow=srow, cols=cols, i=i:
                   e.matmul(bank(i, cols), ones_f[0:1, :], srow[0:1, :],
                            start=True, stop=True), [(st_, 1)])
            P.wait("vector", st_)
            P.emit("vector", lambda e, dst=dst, cols=cols, i=i:
                   e.tensor_copy(dst[:, 0:cols], bank(i, cols)), [(sv, 1)])
        else:
            P.emit("vector", lambda e, dst=dst, cols=cols:
                   e.memset(dst[:, 0:cols], 0.5), [(sv, 1)])

    for b in range(NB if DBG_NB is None else DBG_NB):
        for s in range(NSEG if DBG_NSEG is None else DBG_NSEG):

            def flush_tr(pas, t, pend_tr):
                if t not in pend_tr:
                    return None
                par, g, ttb_done, app_done = pend_tr.pop(t)
                P.wait("tensor", sv, ttb_done)
                P.wait("tensor", ss, app_done)
                tr_done, trb = tr_part(bb[par], par, "trev")
                if pas == 0:
                    dst3 = qtseg.ap().rearrange("p (d c) -> p d c", c=SEG)[
                        :, :, 128 * t:128 * (t + 1)]
                    sc, bi = qkcol[:, 0:1], qkcol[:, 1:2]
                else:
                    dst3 = ktall.ap().rearrange("p (d c) -> p d c", c=T)[
                        :, :, 128 * g:128 * (g + 1)]
                    sc, bi = qkcol[:, 2:3], qkcol[:, 3:4]
                return lambda: ev_part(dst3, trb, tr_done, par, "trev", sc, bi)

            for t in range(NTC):
                g = NTC * s + t
                par = t % 2
                P.wait("sync", sv, EV.get(("xfree", par), 0))
                P.wait("sync", sop[par], EV.get(("ofree", par), 0))
                P.emit("sync", lambda e, b=b, g=g, par=par: e.dma_start(
                    out=xf[par][:], in_=dx[b, 128 * g:128 * (g + 1), :]),
                    [(sxp[par], 16)])
                x_ld = sxp[par].n
                P.wait("vector", sxp[par], x_ld)
                P.wait("vector", st_)
                P.emit("vector", lambda e, par=par: e.tensor_copy(
                    bb[par][:], xf[par][:]), [(sv, 1)])
                EV[("xfree", par)] = sv.n
                P.wait("tensor", sv)
                tr_evac(bb[par],
                        scrab.ap().rearrange("p (d c) -> p d c", c=SEG)[
                            :, :, 128 * t:128 * (t + 1)], par, "trev")
                EV[("xt", t)] = ss.n

            def stage2a(tp):
                if tp < 0:
                    return
                par = tp % 2
                P.wait("scalar", sv, EV[("var", tp)])
                P.emit("scalar", lambda e, par=par: e.activation(
                    st16p[par]["sq"][:], st16p[par]["var"][:], AF.Ln,
                    bias=e64_col[:, 0:1]), [(ss, 1)])
                P.emit("scalar", lambda e, par=par: e.activation(
                    st16p[par]["rstd"][:], st16p[par]["sq"][:], AF.Exp,
                    scale=-0.5, bias=ln8_col[:, 0:1]), [(ss, 1)])
                P.wait("vector", ss)
                P.emit("vector", lambda e, par=par: e.scalar_tensor_tensor(
                    st16p[par]["nmr"][:], st16p[par]["sum"][:], -1.0 / HD,
                    st16p[par]["rstd"][:], ALU.mult, ALU.mult), [(sv, 1)])
                EV[("nmr", tp)] = sv.n
                VH = H - SH_A
                P.wait("vector", st_, EV.get(("bb_rd", par), 0))
                P.emit("vector", lambda e, par=par, VH=VH: e.tensor_tensor(
                    sqf2[par].ap()[:, HD * SH_A:].rearrange(
                        "p (h c) -> p h c", c=HD),
                    nscr[par].ap()[:, HD * SH_A:].rearrange(
                        "p (h c) -> p h c", c=HD),
                    st16p[par]["rstd"].ap()[:, SH_A:].rearrange(
                        "p (h o) -> p h o", o=1).to_broadcast([128, VH, HD]),
                    ALU.mult), [(sv, 1)])
                P.emit("vector", lambda e, par=par, VH=VH: e.tensor_tensor(
                    bb[par][:, HD * SH_A:].rearrange("p (h c) -> p h c", c=HD),
                    sqf2[par].ap()[:, HD * SH_A:].rearrange(
                        "p (h c) -> p h c", c=HD),
                    st16p[par]["nmr"].ap()[:, SH_A:].rearrange(
                        "p (h o) -> p h o", o=1).to_broadcast([128, VH, HD]),
                    ALU.add), [(sv, 1)])
                EV[("ttb", tp)] = sv.n
                EV[("nscr_free", par)] = sv.n
                EV[("sqfA_free", par)] = sv.n

            def stage2b(tp, pend_tr):
                if tp < 0:
                    return
                par = tp % 2
                g = NTC * s + tp
                P.wait("scalar", sv, EV[("nmr", tp)])
                P.wait("scalar", st_, EV.get(("bb_rd", par), 0))
                for h in range(SH_A):
                    P.emit("scalar", lambda e, h=h, par=par: e.activation(
                        bb[par][:, HD * h:HD * (h + 1)],
                        nscr[par][:, HD * h:HD * (h + 1)],
                        AF.Identity, bias=st16p[par]["nmr"][:, h:h + 1],
                        scale=st16p[par]["rstd"][:, h:h + 1]),
                        [(ss, 1)] if h == SH_A - 1 else ())
                pend_tr[tp] = (par, g, EV[("ttb", tp)], ss.n)

            for pas, (wnm, wdram) in enumerate(
                    [("wq", cwq), ("wk", cwk), ("wv", cwv)][:DBG_A_PASSES]):
                wslot = pas % 2
                pend_tr = {}
                P.wait("sync", st_)
                P.wait("sync", sgp, cast_done[wnm])
                P.emit("sync", lambda e, wd=wdram, wslot=wslot: e.dma_start(
                    out=wcur[wslot][:], in_=wd), [(sw, 16)])
                w_ready = sw.n
                for t in range(NTC):
                    g = NTC * s + t
                    par = t % 2
                    qb0 = 2 * par
                    P.wait("tensor", sw, w_ready)
                    P.wait("tensor", ss, EV[("xt", t)])
                    fr = EV.get(("qkvfree", par), (0, 0))
                    P.wait("tensor", ss, fr[0])
                    P.wait("tensor", sv, fr[1])
                    for d_ in range(NDC):
                        for nh in range(2):
                            P.emit("tensor",
                                   lambda e, t=t, d_=d_, nh=nh, qb0=qb0, ws=wslot:
                                   e.matmul(bank(qb0 + nh), xTv(t, d_),
                                            wcur[ws][:, D * d_ + 512 * nh:
                                                     D * d_ + 512 * (nh + 1)],
                                            start=(d_ == 0), stop=(d_ == NDC - 1)),
                                   [(st_, 1)] if (d_ == NDC - 1 and nh == 1)
                                   else ())
                    mm_done = st_.n
                    pso = ps[:, 512 * qb0:512 * qb0 + 1024]
                    if pas < 2 and DBG_A_LN > 0:
                        stage2a(t - 1)
                        ev_fn = flush_tr(pas, t - 2, pend_tr)
                        P.wait("scalar", st_, mm_done)
                        P.wait("scalar", sv, EV.get(("nscr_free", par), 0))
                        P.emit("scalar", lambda e, pso=pso, par=par: e.copy(
                            nscr[par][:], pso), [(ss, 1)])
                        EV[("qkvfree", par)] = (ss.n, 0)
                        P.wait("scalar", sv, EV.get(("sqfA_free", par), 0))
                        P.emit("scalar", lambda e, par=par: e.activation(
                            sqf2[par][:], nscr[par][:], AF.Square), [(ss, 1)])
                        sq_done = ss.n
                        P.wait("vector", ss, sq_done - 1)
                        P.emit("vector", lambda e, par=par: e.reduce_sum(
                            st16p[par]["sum"][:],
                            nscr[par].ap().rearrange("p (h c) -> p h c", c=HD),
                            axis=AX.X), [(sv, 1)])
                        P.wait("vector", ss, sq_done)
                        P.emit("vector", lambda e, par=par: e.reduce_sum(
                            st16p[par]["sq"][:],
                            sqf2[par].ap().rearrange("p (h c) -> p h c", c=HD),
                            axis=AX.X), [(sv, 1)])
                        P.emit("vector", lambda e, par=par: e.scalar_tensor_tensor(
                            st16p[par]["mean"][:], st16p[par]["sum"][:], -1.0 / HD,
                            st16p[par]["sum"][:], ALU.mult, ALU.mult), [(sv, 1)])
                        P.emit("vector", lambda e, par=par: e.tensor_tensor(
                            st16p[par]["var"][:], st16p[par]["mean"][:],
                            st16p[par]["sq"][:], ALU.add), [(sv, 1)])
                        EV[("var", t)] = sv.n
                        if ev_fn is not None:
                            ev_fn()
                        stage2b(t - 1, pend_tr)
                    elif pas == 2:
                        P.wait("scalar", st_, mm_done)
                        P.emit("scalar", lambda e, g=g, pso=pso: e.copy(
                            vsb[:, 1040 * g:1040 * (g + 1)]
                            .rearrange("p (h c) -> p h c", c=65)[:, :, 0:64],
                            pso.rearrange("p (h c) -> p h c", c=HD)), [(ss, 1)])
                        EV[("qkvfree", par)] = (ss.n, sv.n)

                if pas < 2:
                    stage2a(NTC - 1)
                    stage2b(NTC - 1, pend_tr)
                    for tt_ in (NTC - 2, NTC - 1):
                        ev_fn = flush_tr(pas, tt_, pend_tr)
                        if ev_fn is not None:
                            ev_fn()

            if "b" not in DBG_PHASES:
                continue
            P.wait("sync", st_)
            P.wait("sync", sgp, cast_done["wo"])
            P.emit("sync", lambda e: e.dma_start(
                out=wcur[1][:], in_=cwo), [(sw, 16)])
            wo_ready = sw.n
            P.wait("tensor", sgp, 192)

            for qt in range(NQT):
                Q = NQT * s + qt
                jmax = 4 * (Q + 1)
                P.wait("tensor", ss)
                P.wait("tensor", sv)
                exp_at = {}
                pv_red_at = 0
                ao_start_guard = sv.n

                def kw(bi):
                    P.emit("tensor", lambda e, bi=bi: e.matmul(
                        ps[0:128, 512 * bi:512 * (bi + 1)], ones_b[0:1, :],
                        tri_b[0:1, 0:512], start=True, stop=True))

                def flush_norm(i):
                    pvb = (4, 6, 7)[i % 3]
                    P.wait("tensor", ss, EV.get(("rr", i), 0))
                    P.wait("tensor", sv, EV.get("bc_evac", 0))
                    kw(5)
                    hb = QTL * (i % 2)
                    P.emit("tensor", lambda e, hb=hb: e.matmul(
                        ps[0:64, 512 * 5:512 * 6], ones_b[0:1, 0:64],
                        rrow_b[0:1, hb:hb + QTL],
                        start=True, stop=True), [(st_, 1)])
                    P.wait("vector", st_)
                    P.emit("vector", lambda e, hb=hb: e.tensor_copy(
                        bc_sb[:, hb:hb + QTL], ps[0:64, 512 * 5:512 * 6]),
                        [(sv, 1)])
                    EV["bc_evac"] = sv.n
                    P.emit("vector", lambda e, i=i, pvb=pvb, hb=hb:
                           e.tensor_tensor(
                               aot[0:128, QTL * (i // 2):QTL * (i // 2 + 1)]
                               [64 * (i % 2):64 * (i % 2) + 64, :],
                               ps[0:64, 512 * pvb:512 * pvb + QTL],
                               bc_sb[:, hb:hb + QTL], ALU.mult), [(sv, 1)])
                    EV[("aoTT", i)] = sv.n

                def emit_pv(i, js):
                    pvb = (4, 6, 7)[i % 3]
                    for j in js:
                        if j == 0:
                            P.wait("tensor", sv, EV.get(("aoTT", i - 3), 0))
                            kw(pvb)
                        P.wait("tensor", ss, exp_at.get((i, j // 2), 0))
                        if j >= 4 * Q:
                            P.wait("tensor", sv,
                                   EV.get(("mask", i, j - 4 * Q), 0))
                        P.emit("tensor", lambda e, j=j, i=i, pvb=pvb: e.matmul(
                            ps[0:65, 512 * pvb:512 * pvb + QTL],
                            vsb[:, 1040 * j + 65 * i:1040 * j + 65 * (i + 1)],
                            scrab[:, 512 * j:512 * (j + 1)],
                            start=(j == 0), stop=(j == jmax - 1)),
                            [(st_, 1)] if j == js[-1] else ())
                    if js and js[-1] == jmax - 1:
                        EV[("pvdone", i)] = st_.n

                exp_at = {}
                for i in range(DBG_B_NH + 1):
                    if i >= 3:
                        flush_norm(i - 3)
                    if i < DBG_B_NH:
                        hc, hp = i // 2, 64 * (i % 2)
                        npair = (jmax + 1) // 2
                        for jp in range(npair):
                            js = [2 * jp] + ([2 * jp + 1]
                                             if 2 * jp + 1 < jmax else [])
                            pb0 = (jp % 2) * 2
                            P.wait("tensor", ss, EV.get(("exp_bank", pb0), 0))
                            if i == 0 and jp == 0:
                                P.wait("tensor", ss)
                                P.wait("tensor", sv)
                            kw(pb0)
                            for idx, j in enumerate(js):
                                P.emit("tensor",
                                       lambda e, j=j, idx=idx, pb0=pb0, hc=hc,
                                       hp=hp, qt=qt: e.matmul(
                                           bank(pb0 + idx),
                                           ktall[hp:hp + HD,
                                                 T * hc + 128 * j:
                                                 T * hc + 128 * (j + 1)],
                                           qtseg[hp:hp + HD,
                                                 SEG * hc + QTL * qt:
                                                 SEG * hc + QTL * (qt + 1)],
                                           start=True, stop=True),
                                       [(st_, 1)] if idx == len(js) - 1 else ())
                            if i >= 1:
                                emit_pv(i - 1, [2 * jp] +
                                        ([2 * jp + 1] if 2 * jp + 1 < jmax
                                         else []))
                            P.wait("scalar", st_)
                            ncols = 512 * len(js)
                            P.emit("scalar",
                                   lambda e, jp=jp, pb0=pb0, ncols=ncols:
                                   e.activation(
                                       scrab[:, 1024 * jp:1024 * jp + ncols],
                                       ps[:, 512 * pb0:512 * pb0 + ncols],
                                       AF.Exp, scale=0.125), [(ss, 1)])
                            exp_at[(i, jp)] = ss.n
                            EV[("exp_bank", pb0)] = ss.n
                        EV[("expdone", i)] = ss.n
                        P.wait("vector", ss, EV[("expdone", i)])
                        moff = [0, 128, 384, 768]
                        for di in range(4):
                            j = 4 * Q + di
                            w = 128 * (di + 1)
                            P.emit("vector", lambda e, j=j, di=di, w=w:
                                   e.tensor_tensor(
                                       scrab[:, 512 * j:512 * j + w],
                                       scrab[:, 512 * j:512 * j + w],
                                       tri_b[:, moff[di]:moff[di] + w],
                                       ALU.mult), [(sv, 1)])
                            EV[("mask", i, di)] = sv.n
                        EV[("masks", i)] = sv.n
                    else:
                        emit_pv(i - 1, list(range(jmax)))
                    if i >= 1 and ("rr", i - 1) not in EV:
                        pvb1 = (4, 6, 7)[(i - 1) % 3]
                        P.wait("scalar", st_, EV[("pvdone", i - 1)])
                        hb1 = QTL * ((i - 1) % 2)
                        P.emit("scalar", lambda e, pvb1=pvb1: e.activation(
                            rrow_f[0:1, :],
                            ps[64:65, 512 * pvb1:512 * pvb1 + QTL],
                            AF.Ln), [(ss, 1)])
                        P.emit("scalar", lambda e, hb1=hb1: e.activation(
                            rrow_b[0:1, hb1:hb1 + QTL],
                            rrow_f[0:1, :], AF.Exp, scale=-1.0), [(ss, 1)])
                        EV[("rr", i - 1)] = ss.n
                flush_norm(DBG_B_NH - 2)
                flush_norm(DBG_B_NH - 1)
                for i in range(DBG_B_NH + 1):
                    EV.pop(("rr", i), None)
                    EV.pop(("masks", i), None)
                    EV.pop(("expdone", i), None)
                ao_done = sv.n

                for tc in range(4 if DBG_B_LVL >= 6 else 0):
                    t = 4 * qt + tc
                    g = NTC * s + t
                    par = t % 2
                    P.wait("sync", sv, EV.get(("xfree", par), 0))
                    P.wait("sync", sop[par], EV.get(("ofree", par), 0))
                    P.emit("sync", lambda e, b=b, g=g, par=par: e.dma_start(
                        out=xf[par][:], in_=dx[b, 128 * g:128 * (g + 1), :]),
                        [(sxp[par], 16)])
                    xr_ld = sxp[par].n
                    P.wait("tensor", sv, ao_done)
                    P.wait("tensor", sw, wo_ready)
                    P.wait("tensor", sv, EV.get("oev", 0))
                    for c in range(NDC):
                        for nh in range(2):
                            P.emit("tensor", lambda e, c=c, nh=nh, tc=tc:
                                   e.matmul(bank(6 + nh),
                                            aot[:, QTL * c + 128 * tc:
                                                QTL * c + 128 * (tc + 1)],
                                            wcur[1][:, D * c + 512 * nh:
                                                    D * c + 512 * (nh + 1)],
                                            start=(c == 0), stop=False))
                    for nh in range(2):
                        P.emit("tensor", lambda e, nh=nh: e.matmul(
                            bank(6 + nh), ones_b[0:1, :],
                            bo_row_b[0:1, 512 * nh:512 * (nh + 1)],
                            start=False, stop=True),
                            [(st_, 1)] if nh == 1 else ())
                    o_done = st_.n
                    P.wait("vector", st_, o_done)
                    P.wait("vector", sxp[par], xr_ld)
                    P.emit("vector", lambda e, par=par: e.scalar_tensor_tensor(
                        nscr[par][:], ps[:, 512 * 6:512 * 8], 1.0, xf[par][:],
                        ALU.mult, ALU.add, accum_out=st1["sum"][:, 0:1]),
                        [(sv, 1)])
                    EV[("xfree", par)] = sv.n
                    EV["oev"] = sv.n
                    P.emit("vector", lambda e, par=par: e.scalar_tensor_tensor(
                        sqf[:], nscr[par][:], 1.0, nscr[par][:],
                        ALU.mult, ALU.mult, accum_out=st1["sq"][:, 0:1]),
                        [(sv, 1)])
                    P.emit("vector", lambda e: e.tensor_scalar_mul(
                        st1["mean"][:], st1["sum"][:], 1.0 / D), [(sv, 1)])
                    P.emit("vector", lambda e: e.tensor_tensor(
                        st1["var"][:], st1["mean"][:], st1["mean"][:], ALU.mult),
                        [(sv, 1)])
                    P.emit("vector", lambda e: e.scalar_tensor_tensor(
                        st1["var"][:], st1["sq"][:], 1.0 / D, st1["var"][:],
                        ALU.mult, ALU.subtract), [(sv, 1)])
                    P.wait("scalar", sv)
                    P.emit("scalar", lambda e: e.activation(
                        st1["sq"][:], st1["var"][:], AF.Ln,
                        bias=eps_col[:, 0:1]), [(ss, 1)])
                    P.emit("scalar", lambda e: e.activation(
                        st1["rstd"][:], st1["sq"][:], AF.Exp, scale=-0.5),
                        [(ss, 1)])
                    P.wait("vector", ss)
                    P.emit("vector", lambda e, t=t, par=par: e.tensor_scalar(
                        h1f[:, D * t:D * (t + 1)], nscr[par][:],
                        st1["mean"][:, 0:1], st1["rstd"][:, 0:1],
                        ALU.subtract, ALU.mult), [(sv, 1)])
                    EV[("h1", t)] = sv.n

            EV["ps47_dve"] = sems["sv"].n

            if "c" not in DBG_PHASES:
                continue

            def unit_srcs(u):
                if u < NE:
                    return (cgw[u], cuw[u], cdw[u], True)
                qd = u - NE
                return (csg[qd], csu[qd], csd[qd], False)

            def load_unit_gu(u):
                gsrc, usrc, dsrc, _ = unit_srcs(u)
                slot = u % 2
                P.wait("sync", sgp, cast_done["uw" if u < NE else "su"])
                P.emit("sync", lambda e, gsrc=gsrc, slot=slot: e.dma_start(
                    out=gub[slot][:, 0:4096], in_=gsrc), [(sgu[slot], 16)])
                P.emit("sync", lambda e, usrc=usrc, slot=slot: e.dma_start(
                    out=gub[slot][:, 4096:8192], in_=usrc), [(sgu[slot], 16)])
                return sgu[slot].n

            def load_unit_dn(u):
                gsrc, usrc, dsrc, _ = unit_srcs(u)
                P.wait("sync", sgp, cast_done["dw" if u < NE else "sd"])
                P.emit("sync", lambda e, dsrc=dsrc: e.dma_start(
                    out=down_sb.ap()[:, :], in_=dsrc), [(sdn, 16)])
                return sdn.n

            pend_c = {}

            def flush_c(t):
                if t not in pend_c:
                    return
                par, cast_done_c = pend_c.pop(t)
                P.wait("tensor", sv, cast_done_c)
                tr_evac(bb[par],
                        wcur[0].ap().rearrange("p (d c) -> p d c", c=SEG)[
                            :, :, 128 * t:128 * (t + 1)], par, "trev")
                EV[("h1t", t)] = ss.n

            for t in range(NTC):
                par = t % 2
                P.wait("vector", sv, EV[("h1", t)])
                P.wait("vector", st_)
                P.emit("vector", lambda e, t=t, par=par: e.tensor_copy(
                    bb[par][:], h1f[:, D * t:D * (t + 1)]), [(sv, 1)])
                pend_c[t] = (par, sv.n)
                flush_c(t - 1)
                if t < NTC - 1:
                    continue
                flush_c(t)
            P.wait("tensor", sgp, 192)
            P.wait("tensor", sv, EV.get("router_read", 0))
            for t in range(NTC):
                P.wait("tensor", ss, EV[("h1t", t)])
                for d_ in range(NDC):
                    P.emit("tensor", lambda e, t=t, d_=d_: e.matmul(
                        ps[:, NE * t:NE * (t + 1)],
                        h1T[:, SEG * d_ + 128 * t:SEG * d_ + 128 * (t + 1)],
                        rw_sb[:, NE * d_:NE * (d_ + 1)],
                        start=(d_ == 0), stop=(d_ == NDC - 1)),
                        [(st_, 1)] if d_ == NDC - 1 else ())
            rt_done = st_.n

            def r3(t_):
                return t_.ap().rearrange("p (t e) -> p t e", e=NE)

            def c3(t_):
                return t_.ap().rearrange("p (t o) -> p t o", o=1).to_broadcast(
                    [128, NTC, NE])

            P.wait("vector", st_, rt_done)
            P.emit("vector", lambda e: e.tensor_copy(
                r8["logit"][:], bank(0, NTC * NE)), [(sv, 1)])
            EV["router_read"] = sv.n
            P.emit("vector", lambda e: e.reduce_max(
                rcol["nmax"][:], r3(r8["logit"]), axis=AX.X, negate=True),
                [(sv, 1)])
            P.emit("vector", lambda e: e.tensor_tensor(
                r3(r8["logit"]), r3(r8["logit"]), c3(rcol["nmax"]), ALU.add),
                [(sv, 1)])
            P.wait("scalar", sv)
            P.emit("scalar", lambda e: e.activation(
                r8["score"][:], r8["logit"][:], AF.Exp), [(ss, 1)])
            P.wait("vector", ss)
            P.emit("vector", lambda e: e.reduce_sum(
                rcol["den"][:], r3(r8["score"]), axis=AX.X), [(sv, 1)])
            P.emit("vector", lambda e: e.reciprocal(
                rcol["rden"][:], rcol["den"][:]), [(sv, 1)])
            P.emit("vector", lambda e: e.tensor_tensor(
                r3(r8["score"]), r3(r8["score"]), c3(rcol["rden"]), ALU.mult),
                [(sv, 1)])
            P.emit("vector", lambda e: e.tensor_scalar_add(
                r8["corr"][:], r8["score"][:], rbias_col[:, 0:1]), [(sv, 1)])
            for t in range(NTC):
                P.emit("vector", lambda e, t=t: e.max(
                    out=m8w[:, NE * t:NE * (t + 1)],
                    in_=r8["corr"][:, NE * t:NE * (t + 1)]), [(sv, 1)],
                    nodrain=(t > 0))
            thr3 = m8w.ap().rearrange("p (t e) -> p t e", e=NE)[
                :, :, 3:4].to_broadcast([128, NTC, NE])
            P.emit("vector", lambda e, thr3=thr3: e.tensor_tensor(
                r3(r8["sel"]), r3(r8["corr"]), thr3, ALU.is_ge), [(sv, 1)])
            P.emit("vector", lambda e: e.tensor_mul(
                r8["msk"][:], r8["sel"][:], r8["score"][:]), [(sv, 1)])
            P.emit("vector", lambda e: e.reduce_sum(
                rcol["den4"][:], r3(r8["msk"]), axis=AX.X), [(sv, 1)])
            P.emit("vector", lambda e: e.tensor_scalar_add(
                rcol["den4"][:], rcol["den4"][:], 1e-20), [(sv, 1)])
            P.emit("vector", lambda e: e.reciprocal(
                rcol["rden4"][:], rcol["den4"][:]), [(sv, 1)])
            P.emit("vector", lambda e: e.tensor_tensor(
                w_seg.ap().rearrange("p (t e) -> p t e", e=NE),
                r3(r8["msk"]), c3(rcol["rden4"]), ALU.mult), [(sv, 1)])
            for t in range(NTC):
                EV[("w", t)] = sv.n

            P.wait("sync", st_)
            u_gu = {0: load_unit_gu(0)}
            u_dn = {0: load_unit_dn(0)}
            EV["pipeline_prev"] = None

            def emit_down(u, tq, act_done, routed):
                tc0, tc1 = 2 * tq, 2 * tq + 1
                apar = 1024 * (tq % 2)
                P.wait("tensor", sv, act_done)
                P.wait("tensor", sdn, u_dn[u])
                P.wait("tensor", sv, EV.get("dadd_last", 0))
                for tci in range(2):
                    for m in range(4):
                        for nh in range(2):
                            P.emit("tensor",
                                   lambda e, tci=tci, m=m, nh=nh, apar=apar:
                                   e.matmul(
                                       bank(4 + 2 * tci + nh),
                                       actw[:, apar + 256 * m + 128 * tci:
                                            apar + 256 * m + 128 * tci + 128],
                                       down_sb[:, D * m + 512 * nh:
                                               D * m + 512 * (nh + 1)],
                                       start=(m == 0), stop=(m == 3)),
                                   [(st_, 1)]
                                   if (m == 3 and nh == 1 and tci == 1)
                                   else ())
                dn_done = st_.n
                EV[("act_read_t", tq % 2)] = dn_done
                P.wait("vector", st_, dn_done)
                for tci, tcg in enumerate((tc0, tc1)):
                    pd = 512 * (4 + 2 * tci)
                    if routed:
                        P.emit("vector", lambda e, tcg=tcg, pd=pd, u=u:
                               e.scalar_tensor_tensor(
                                   h1f[:, D * tcg:D * (tcg + 1)],
                                   ps[:, pd:pd + 1024],
                                   w_seg[:, NE * tcg + u:NE * tcg + u + 1],
                                   h1f[:, D * tcg:D * (tcg + 1)],
                                   ALU.mult, ALU.add), [(sv, 1)])
                    else:
                        P.emit("vector", lambda e, tcg=tcg, pd=pd: e.tensor_add(
                            h1f[:, D * tcg:D * (tcg + 1)],
                            h1f[:, D * tcg:D * (tcg + 1)],
                            ps[:, pd:pd + 1024]), [(sv, 1)])
                EV["dadd_last"] = sv.n
                EV[("dadd", (u, tq))] = sv.n
            for u in range(NU):
                slot = u % 2
                routed = u < NE
                for tq in range(NTQ):
                    tc0, tc1 = 2 * tq, 2 * tq + 1
                    P.wait("tensor", sgu[u % 2], u_gu[u])
                    P.wait("tensor", sv, EV.get("router_read", 0))
                    P.wait("tensor", ss, EV.get("guev_s", 0))
                    P.wait("tensor", sv, EV.get("guev_v", 0))
                    for mat in range(2):
                        for m in range(4):
                            for d_ in range(NDC):
                                P.emit("tensor",
                                       lambda e, mat=mat, m=m, d_=d_, tq=tq,
                                       slot=slot: e.matmul(
                                           ps[:, 1024 * mat + 256 * m:
                                              1024 * mat + 256 * (m + 1)],
                                           gub[slot][:, 4096 * mat + IE * d_ +
                                                     128 * m:
                                                     4096 * mat + IE * d_ +
                                                     128 * (m + 1)],
                                           h1T[:, SEG * d_ + TOKQ * tq:
                                               SEG * d_ + TOKQ * (tq + 1)],
                                           start=(d_ == 0),
                                           stop=(d_ == NDC - 1)),
                                       [(st_, 1)]
                                       if (d_ == NDC - 1 and m == 3) else ())
                    gu_done = st_.n
                    if tq == 0 and u + 1 < NU:
                        P.wait("sync", st_)
                        u_gu[u + 1] = load_unit_gu(u + 1)
                    P.wait("scalar", st_, gu_done)
                    P.wait("scalar", sv, EV.get("act_read_v", 0))
                    P.emit("scalar", lambda e: e.activation(
                        silu_sb[:, 0:1024], ps[:, 0:1024], AF.Silu), [(ss, 1)])
                    EV["guev_s"] = ss.n
                    P.wait("vector", st_, gu_done)
                    P.wait("vector", ss)
                    P.emit("vector", lambda e: e.tensor_copy(
                        silu_sb[:, 1024:2048], ps[:, 1024:2048]), [(sv, 1)])
                    EV["guev_v"] = sv.n
                    P.wait("vector", st_, EV.get(("act_read_t", tq % 2), 0))
                    apar = 1024 * (tq % 2)
                    P.emit("vector", lambda e, apar=apar: e.tensor_tensor(
                        actw[:, apar:apar + 1024], silu_sb[:, 0:1024],
                        silu_sb[:, 1024:2048], ALU.mult), [(sv, 1)])
                    act_done = sv.n
                    EV["act_read_v"] = act_done
                    prev = EV.get("pipeline_prev")
                    EV["pipeline_prev"] = (u, tq, act_done, routed)
                    if prev is not None:
                        emit_down(*prev)
                        if prev[1] == NTQ - 1 and u not in u_dn:
                            P.wait("sync", st_)
                            u_dn[u] = load_unit_dn(u)


            if EV.get("pipeline_prev") is not None:
                emit_down(*EV["pipeline_prev"])
                EV["pipeline_prev"] = None

            EV["ps47_dve"] = sems["sv"].n
            for t in range(NTC):
                par = t % 2
                P.wait("vector", sv, EV[("dadd", (NU - 1, t // 2))])
                P.emit("vector", lambda e, t=t: e.scalar_tensor_tensor(
                    sqf[:], h1f[:, D * t:D * (t + 1)], 1.0,
                    h1f[:, D * t:D * (t + 1)], ALU.mult, ALU.mult,
                    accum_out=st1["sq"][:, 0:1]), [(sv, 1)])
                P.emit("vector", lambda e, t=t: e.reduce_sum(
                    st1["sum"][:], h1f[:, D * t:D * (t + 1)], axis=AX.X),
                    [(sv, 1)])
                P.emit("vector", lambda e: e.tensor_scalar_mul(
                    st1["mean"][:], st1["sum"][:], 1.0 / D), [(sv, 1)])
                P.emit("vector", lambda e: e.tensor_tensor(
                    st1["var"][:], st1["mean"][:], st1["mean"][:], ALU.mult),
                    [(sv, 1)])
                P.emit("vector", lambda e: e.scalar_tensor_tensor(
                    st1["var"][:], st1["sq"][:], 1.0 / D, st1["var"][:],
                    ALU.mult, ALU.subtract), [(sv, 1)])
                P.wait("scalar", sv)
                P.emit("scalar", lambda e: e.activation(
                    st1["sq"][:], st1["var"][:], AF.Ln, bias=eps_col[:, 0:1]),
                    [(ss, 1)])
                P.emit("scalar", lambda e: e.activation(
                    st1["rstd"][:], st1["sq"][:], AF.Exp, scale=-0.5), [(ss, 1)])
                P.wait("vector", ss)
                P.wait("vector", sop[par], EV.get(("ofree", par), 0))
                P.emit("vector", lambda e, t=t, par=par: e.tensor_scalar(
                    xf[par][:], h1f[:, D * t:D * (t + 1)],
                    st1["mean"][:, 0:1], st1["rstd"][:, 0:1],
                    ALU.subtract, ALU.mult), [(sv, 1)])
                apply_done = sv.n
                P.wait("sync", sv, apply_done)
                g = NTC * s + t
                P.emit("sync", lambda e, b=b, g=g, par=par: e.dma_start(
                    out=dout[b, 128 * g:128 * (g + 1), :], in_=xf[par][:]),
                    [(sop[par], 16)])
                EV[("ofree", par)] = sop[par].n

    P.wait("sync", sop[0])
    P.wait("sync", sop[1])

    with nc.Block() as block:
        @block.sync
        def _(e):
            P.run("sync", e)

        @block.gpsimd
        def _(e):
            P.run("gpsimd", e)

        @block.tensor
        def _(e):
            P.run("tensor", e)

        @block.scalar
        def _(e):
            P.run("scalar", e)

        @block.vector
        def _(e):
            P.run("vector", e)

    es.close()
    return nc


_NC_CACHE = None


def prep_common(L0_attn_Wk, L0_attn_Wq, L0_attn_Wv, L0_attn_Wo, L0_attn_bo,
                L0_attn_kln_w, L0_attn_kln_b, L0_attn_qln_w, L0_attn_qln_b,
                L0_ffn_router_w, L0_ffn_router_bias, L0_ffn_gate_w,
                L0_ffn_up_w, L0_ffn_down_w, L0_ffn_s_gate_w, L0_ffn_s_up_w,
                L0_ffn_s_down_w):
    f32 = lambda a: np.ascontiguousarray(np.asarray(a, dtype=np.float32))
    def lay2(w):
        w = np.asarray(w, np.float32)
        return np.ascontiguousarray(
            w.reshape(-1, 128, w.shape[-1]).transpose(1, 0, 2))

    def lay3(w):
        w = np.asarray(w, np.float32)
        return np.ascontiguousarray(
            w.reshape(w.shape[0], -1, 128, w.shape[-1]).transpose(0, 2, 1, 3))

    sg_q = np.stack([f32(L0_ffn_s_gate_w)[:, 512 * q:512 * (q + 1)]
                     for q in range(4)])
    su_q = np.stack([f32(L0_ffn_s_up_w)[:, 512 * q:512 * (q + 1)]
                     for q in range(4)])
    sd_q = np.stack([f32(L0_ffn_s_down_w)[512 * q:512 * (q + 1), :]
                     for q in range(4)])
    common = dict(
        wq=lay2(L0_attn_Wq), wk=lay2(L0_attn_Wk), wv=lay2(L0_attn_Wv),
        wo=lay2(L0_attn_Wo), bo=f32(L0_attn_bo).reshape(1, D),
        klw=f32(L0_attn_kln_w).reshape(1, HD),
        klb=f32(L0_attn_kln_b).reshape(1, HD),
        qlw=f32(L0_attn_qln_w).reshape(1, HD),
        qlb=f32(L0_attn_qln_b).reshape(1, HD),
        rw=lay2(L0_ffn_router_w), rb=f32(L0_ffn_router_bias).reshape(1, 1),
        gw=lay3(L0_ffn_gate_w), uw=lay3(L0_ffn_up_w), dw=lay3(L0_ffn_down_w),
        sg=lay3(sg_q), su=lay3(su_q), sd=lay3(sd_q),
        ident=np.eye(128, dtype=ml_dtypes.bfloat16),
        tri=_diag_masks(),
    )
    return common


def kernel(L0_attn_x, L0_attn_Wk, L0_attn_Wq, L0_attn_Wv, L0_attn_Wo,
           L0_attn_bo, L0_attn_kln_w, L0_attn_kln_b, L0_attn_qln_w,
           L0_attn_qln_b, L0_ffn_router_w, L0_ffn_router_bias, L0_ffn_gate_w,
           L0_ffn_up_w, L0_ffn_down_w, L0_ffn_s_gate_w, L0_ffn_s_up_w,
           L0_ffn_s_down_w):
    global _NC_CACHE
    x = np.ascontiguousarray(np.asarray(L0_attn_x, dtype=np.float32))
    common = prep_common(L0_attn_Wk, L0_attn_Wq, L0_attn_Wv, L0_attn_Wo,
                         L0_attn_bo, L0_attn_kln_w, L0_attn_kln_b,
                         L0_attn_qln_w, L0_attn_qln_b, L0_ffn_router_w,
                         L0_ffn_router_bias, L0_ffn_gate_w, L0_ffn_up_w,
                         L0_ffn_down_w, L0_ffn_s_gate_w, L0_ffn_s_up_w,
                         L0_ffn_s_down_w)
    in_maps = []
    for c in range(NCORES):
        m = dict(common)
        m["x"] = np.ascontiguousarray(x[NB * c:NB * (c + 1)])
        in_maps.append(m)
    if _NC_CACHE is None:
        _NC_CACHE = build_core()
    res = run_bass_kernel_spmd(_NC_CACHE, in_maps, core_ids=list(range(NCORES)))
    return np.concatenate([np.asarray(r["out"]) for r in res.results], axis=0)


def _diag_masks():
    parts = []
    for i in range(4):
        w = 128 * (i + 1)
        q = np.arange(w)[None, :]
        k = np.arange(128)[:, None]
        parts.append((q >= 128 * i + k).astype(np.float32))
    return np.concatenate(parts, axis=1).astype(ml_dtypes.bfloat16)


if __name__ == "__main__":
    nc = build_core()
    print("built ok; instructions:", nc.n_instructions()
          if hasattr(nc, "n_instructions") else "?")


# version 19
# speedup vs baseline: 1.0699x; 1.0699x over previous
import contextlib
import numpy as np
import ml_dtypes

import concourse.bass as bass
import concourse.mybir as mybir
from concourse.bass_utils import run_bass_kernel_spmd

F32 = mybir.dt.float32
BF16 = mybir.dt.bfloat16
AF = mybir.ActivationFunctionType
ALU = mybir.AluOpType
AX = mybir.AxisListType

NCORES = 8
NB = 4
T = 2048
D = 1024
H = 16
HD = 64
NDC = 8
SEG = 1024
NSEG = 2
NTC = 8
QTL = 512
NQT = 2
TOKQ = 256
NTQ = 4
NE = 8
IE = 512
NU = 12
DBG_NB = None
DBG_NSEG = None
DBG_PHASES = "abc"
DBG_A_PASSES = 3
DBG_A_LN = 9
DBG_B_NH = 16
DBG_B_LVL = 9
DBG_PV_BANK = 4
DBG_PRO_BCAST = 1
EPS = 1e-5

ENGINES = ("tensor", "scalar", "vector", "sync", "gpsimd")
SAFE_SYNC = False
SH_A = 4
KEEP_WARM = False


class Sem:
    def __init__(self, handle, name):
        self.h = handle
        self.name = name
        self.n = 0


class Plan:

    def __init__(self):
        self.items = {e: [] for e in ENGINES}

    def emit(self, eng, fn, incs=(), nodrain=False):
        incs = tuple(incs)
        if (eng == "vector" and not nodrain) or (eng == "scalar" and SAFE_SYNC):
            inner = fn
            sep = "drain"
            if eng == "scalar":
                sep = "drain"

            def fn(v, _inner=inner, sep=sep):
                getattr(v, sep)()
                return _inner(v)
        self.items[eng].append(("op", fn, incs))
        for sem, k in incs:
            sem.n += k

    def wait(self, eng, sem, val=None):
        v = sem.n if val is None else val
        if v > 0:
            self.items[eng].append(("wait", sem, v))

    def run(self, eng_name, eng):
        for it in self.items[eng_name]:
            if it[0] == "op":
                _, fn, incs = it
                ins = fn(eng)
                for sem, k in incs:
                    ins.then_inc(sem.h, k)
            else:
                _, sem, v = it
                eng.wait_ge(sem.h, v)


def build_core():
    nc = bass.Bass("TRN2", target_bir_lowering=False, debug=False)

    dx = nc.dram_tensor("x", [NB, T, D], F32, kind="ExternalInput").ap()
    dwq = nc.dram_tensor("wq", [128, NDC, D], F32, kind="ExternalInput").ap()
    dwk = nc.dram_tensor("wk", [128, NDC, D], F32, kind="ExternalInput").ap()
    dwv = nc.dram_tensor("wv", [128, NDC, D], F32, kind="ExternalInput").ap()
    dwo = nc.dram_tensor("wo", [128, NDC, D], F32, kind="ExternalInput").ap()
    dbo = nc.dram_tensor("bo", [1, D], F32, kind="ExternalInput").ap()
    dklw = nc.dram_tensor("klw", [1, HD], F32, kind="ExternalInput").ap()
    dklb = nc.dram_tensor("klb", [1, HD], F32, kind="ExternalInput").ap()
    dqlw = nc.dram_tensor("qlw", [1, HD], F32, kind="ExternalInput").ap()
    dqlb = nc.dram_tensor("qlb", [1, HD], F32, kind="ExternalInput").ap()
    drw = nc.dram_tensor("rw", [128, NDC, NE], F32, kind="ExternalInput").ap()
    drb = nc.dram_tensor("rb", [1, 1], F32, kind="ExternalInput").ap()
    dgw = nc.dram_tensor("gw", [NE, 128, NDC, IE], F32, kind="ExternalInput").ap()
    duw = nc.dram_tensor("uw", [NE, 128, NDC, IE], F32, kind="ExternalInput").ap()
    ddw = nc.dram_tensor("dw", [NE, 128, 4, D], F32, kind="ExternalInput").ap()
    dsg = nc.dram_tensor("sg", [4, 128, NDC, IE], F32, kind="ExternalInput").ap()
    dsu = nc.dram_tensor("su", [4, 128, NDC, IE], F32, kind="ExternalInput").ap()
    dsd = nc.dram_tensor("sd", [4, 128, 4, D], F32, kind="ExternalInput").ap()
    cwq = nc.dram_tensor("cwq", [128, NDC * D], BF16, kind="Internal").ap()
    cwk = nc.dram_tensor("cwk", [128, NDC * D], BF16, kind="Internal").ap()
    cwv = nc.dram_tensor("cwv", [128, NDC * D], BF16, kind="Internal").ap()
    cwo = nc.dram_tensor("cwo", [128, NDC * D], BF16, kind="Internal").ap()
    cgw = nc.dram_tensor("cgw", [NE, 128, NDC * IE], BF16, kind="Internal").ap()
    cuw = nc.dram_tensor("cuw", [NE, 128, NDC * IE], BF16, kind="Internal").ap()
    cdw = nc.dram_tensor("cdw", [NE, 128, 4 * D], BF16, kind="Internal").ap()
    csg = nc.dram_tensor("csg", [4, 128, NDC * IE], BF16, kind="Internal").ap()
    csu = nc.dram_tensor("csu", [4, 128, NDC * IE], BF16, kind="Internal").ap()
    csd = nc.dram_tensor("csd", [4, 128, 4 * D], BF16, kind="Internal").ap()
    dident = nc.dram_tensor("ident", [128, 128], BF16, kind="ExternalInput").ap()
    dtri = nc.dram_tensor("tri", [128, 1280], BF16, kind="ExternalInput").ap()
    dout = nc.dram_tensor("out", [NB, T, D], F32, kind="ExternalOutput").ap()

    es = contextlib.ExitStack()
    S = lambda n, sh, dt: es.enter_context(nc.sbuf_tensor(n, sh, dt))

    ident_b = S("ident_b", [128, 128], BF16)
    tri_b = S("tri_b", [128, 1280], BF16)
    ones_b = S("ones_b", [1, 128], BF16)
    ones_f = S("ones_f", [1, 128], F32)
    bo_row_b = S("bo_row_b", [1, D], BF16)
    lnrows = S("lnrows", [1, 4 * HD], F32)
    klw_r = lnrows.ap()[:, 0:HD]
    klb_r = lnrows.ap()[:, HD:2 * HD]
    qlw_r = lnrows.ap()[:, 2 * HD:3 * HD]
    qlb_r = lnrows.ap()[:, 3 * HD:4 * HD]
    rb11 = S("rb11", [1, 1], F32)
    rbias_col = S("rbias_col", [128, 1], F32)
    eps_col = S("eps_col", [128, 1], F32)
    qkcol = S("qkcol", [128, 4], F32)
    e64_col = S("e64_col", [128, 1], F32)
    ln8_col = S("ln8_col", [128, 1], F32)
    rw_sb = S("rw_sb", [128, NDC * NE], BF16)
    ktall = S("ktall", [128, NDC * T], BF16)
    vsb = S("vsb", [128, 16 * (H * 65)], BF16)
    h1f = S("h1f", [128, NTC * D], F32)
    xf = [S("xf0", [128, D], F32), S("xf1", [128, D], F32)]
    sqf2 = [S("sqf0", [128, D], F32), S("sqf1", [128, D], F32)]
    sqf = sqf2[0]
    st16p = [{k: S(f"st16_{k}{i}", [128, H], F32)
              for k in ("sum", "sq", "mean", "var", "rstd", "nmr")}
             for i in range(2)]
    st1 = {k: S("st1_" + k, [128, 1], F32)
           for k in ("sum", "sq", "mean", "var", "rstd")}
    w_seg = S("w_seg", [128, NTC * NE], F32)
    rrow_f = S("rrow_f", [1, QTL], F32)
    rrow_b = S("rrow_b", [1, 2 * QTL], BF16)
    bc_sb = S("bc_sb", [64, 2 * QTL], BF16)
    r8 = {k: S("r8_" + k, [128, NTC * NE], F32)
          for k in ("logit", "score", "sel", "msk")}
    r8["corr"] = w_seg
    m8w = r8["logit"]
    rcol = {k: S("rcol_" + k, [128, NTC], F32)
            for k in ("nmax", "den", "rden", "den4", "rden4")}

    wcur = [S("wcur0", [128, NDC * D], BF16), S("wcur1", [128, NDC * D], BF16)]
    scrab = S("scrab", [128, 8192], BF16)
    qtseg = S("qtseg", [128, NDC * SEG], BF16)
    aot = S("aot", [128, NDC * QTL], BF16)
    bb = [aot.ap()[:, 0:D], aot.ap()[:, D:2 * D]]
    nscr = [S("nscr0", [128, D], F32), S("nscr1", [128, D], F32)]

    h1T = wcur[0]
    gub = [qtseg, scrab]
    down_sb = aot
    silu_sb = nscr[0].ap().bitcast(BF16)
    actw = nscr[1].ap().bitcast(BF16)

    ps = es.enter_context(nc.psum_tensor("ps", [128, 4096], F32))

    def bank(b, w=512):
        return ps[:, 512 * b: 512 * b + w]

    def bank_b16(b, off, cols):
        return ps[:, 512 * b + off // 2: 512 * b + (off + cols) // 2].bitcast(BF16)

    sems = {}
    for name in ("scs", "sx0", "sx1", "sw", "sgu0", "sgu1", "sdn", "sgp",
                 "so0", "so1", "st", "ss", "sv"):
        sems[name] = Sem(es.enter_context(nc.semaphore(name)), name)
    scs = sems["scs"]
    sxp = [sems["sx0"], sems["sx1"]]
    sw = sems["sw"]
    sgu = [sems["sgu0"], sems["sgu1"]]
    sdn = sems["sdn"]
    sgp = sems["sgp"]
    sop = [sems["so0"], sems["so1"]]
    st_, ss, sv = sems["st"], sems["ss"], sems["sv"]

    P = Plan()
    EV = {}

    def xTv(t, d):
        return scrab[:, SEG * d + 128 * t: SEG * d + 128 * (t + 1)]

    def tr_part(src_bb, par, producer_key):
        trb = 4 + par
        P.wait("tensor", ss, EV.get((producer_key, par), 0))
        P.wait("tensor", sv, EV.get("ps47_dve", 0))
        for d_ in range(NDC):
            P.emit("tensor", lambda e, d_=d_, trb=trb, src_bb=src_bb:
                   e.transpose(bank_b16(trb, 128 * d_, 128),
                               src_bb[:, 128 * d_:128 * (d_ + 1)], ident_b[:]),
                   [(st_, 1)] if d_ == NDC - 1 else ())
        EV[("bb_rd", par)] = st_.n
        return st_.n, trb

    def ev_part(dst3, trb, tr_done, par, producer_key, scale=None, bias=None):
        P.wait("scalar", st_, tr_done)
        src3 = bank_b16(trb, 0, 1024).rearrange("p (d c) -> p d c", c=128)
        if scale is None:
            P.emit("scalar", lambda e, dst3=dst3, src3=src3:
                   e.copy(dst3, src3), [(ss, 1)])
        else:
            P.emit("scalar", lambda e, dst3=dst3, src3=src3, scale=scale,
                   bias=bias: e.activation(dst3, src3, AF.Identity,
                                           scale=scale, bias=bias), [(ss, 1)])
        EV[(producer_key, par)] = ss.n

    def tr_evac(src_bb, dst3, par, producer_key, scale=None, bias=None):
        tr_done, trb = tr_part(src_bb, par, producer_key)
        ev_part(dst3, trb, tr_done, par, producer_key, scale, bias)

    for ap_in, tile in [(dident, ident_b[:]), (dtri, tri_b[:]),
                        (dklw, klw_r), (dklb, klb_r), (dqlw, qlw_r),
                        (dqlb, qlb_r), (drb, rb11[:])]:
        P.emit("sync", lambda e, a=ap_in, t=tile: e.dma_start(out=t, in_=a),
               [(scs, 16)])
    for j, srow in enumerate([dqlw, dqlb, dklw, dklb]):
        for half in range(2):
            P.emit("sync", lambda e, srow=srow, j=j, half=half: e.dma_start(
                out=qkcol[64 * half:64 * (half + 1), j:j + 1], in_=srow),
                [(scs, 16)])
    P.wait("scalar", scs)
    P.emit("gpsimd", lambda e: e.dma_start(
        out=rw_sb.ap().rearrange("p (c e) -> p c e", e=NE), in_=drw),
        [(sgp, 16)])
    P.emit("gpsimd", lambda e: e.dma_start(out=bo_row_b[:], in_=dbo), [(sgp, 16)])
    cast_done = {}
    for nm, csrc, cdst in [("wq", dwq, cwq), ("wk", dwk, cwk), ("wv", dwv, cwv),
                           ("wo", dwo, cwo), ("gw", dgw, cgw), ("uw", duw, cuw),
                           ("dw", ddw, cdw), ("sg", dsg, csg), ("su", dsu, csu),
                           ("sd", dsd, csd)]:
        P.emit("gpsimd", lambda e, a=csrc, b=cdst: e.dma_start(
            out=b, in_=a.rearrange("... p c f -> ... p (c f)")
            if len(a.shape) == 4 else a.rearrange("p c f -> p (c f)")),
            [(sgp, 16)])
    for nm in ("wq", "wk", "wv", "wo", "gw", "uw", "dw", "sg", "su", "sd"):
        cast_done[nm] = sgp.n
    P.wait("vector", scs)
    P.wait("vector", sgp)
    P.emit("vector", lambda e: e.memset(ones_b[:], 1.0), [(sv, 1)])
    P.emit("vector", lambda e: e.memset(ones_f[:], 1.0), [(sv, 1)])
    P.emit("vector", lambda e: e.memset(eps_col[:], EPS), [(sv, 1)])
    P.emit("vector", lambda e: e.memset(e64_col[:], HD * EPS), [(sv, 1)])
    P.emit("vector", lambda e: e.memset(ln8_col[:], 2.0794415416798357), [(sv, 1)])
    P.emit("vector", lambda e: e.memset(
        vsb.ap().rearrange("p (g h c) -> p g h c", h=H, c=65)[:, :, :, 64:65],
        1.0), [(sv, 1)])
    P.wait("tensor", sv)
    for i, (dst, srow, cols) in enumerate([(rbias_col, rb11, 1)]):
        if DBG_PRO_BCAST:
            P.emit("tensor", lambda e, srow=srow, cols=cols, i=i:
                   e.matmul(bank(i, cols), ones_f[0:1, :], srow[0:1, :],
                            start=True, stop=True), [(st_, 1)])
            P.wait("vector", st_)
            P.emit("vector", lambda e, dst=dst, cols=cols, i=i:
                   e.tensor_copy(dst[:, 0:cols], bank(i, cols)), [(sv, 1)])
        else:
            P.emit("vector", lambda e, dst=dst, cols=cols:
                   e.memset(dst[:, 0:cols], 0.5), [(sv, 1)])

    for b in range(NB if DBG_NB is None else DBG_NB):
        for s in range(NSEG if DBG_NSEG is None else DBG_NSEG):

            def flush_tr(pas, t, pend_tr):
                if t not in pend_tr:
                    return None
                par, g, ttb_done, app_done = pend_tr.pop(t)
                P.wait("tensor", sv, ttb_done)
                P.wait("tensor", ss, app_done)
                tr_done, trb = tr_part(bb[par], par, "trev")
                if pas == 0:
                    dst3 = qtseg.ap().rearrange("p (d c) -> p d c", c=SEG)[
                        :, :, 128 * t:128 * (t + 1)]
                    sc, bi = qkcol[:, 0:1], qkcol[:, 1:2]
                else:
                    dst3 = ktall.ap().rearrange("p (d c) -> p d c", c=T)[
                        :, :, 128 * g:128 * (g + 1)]
                    sc, bi = qkcol[:, 2:3], qkcol[:, 3:4]
                return lambda: ev_part(dst3, trb, tr_done, par, "trev", sc, bi)

            for t in range(NTC):
                g = NTC * s + t
                par = t % 2
                P.wait("sync", sv, EV.get(("xfree", par), 0))
                P.wait("sync", sop[par], EV.get(("ofree", par), 0))
                P.emit("sync", lambda e, b=b, g=g, par=par: e.dma_start(
                    out=xf[par][:], in_=dx[b, 128 * g:128 * (g + 1), :]),
                    [(sxp[par], 16)])
                x_ld = sxp[par].n
                P.wait("vector", sxp[par], x_ld)
                P.wait("vector", st_)
                P.emit("vector", lambda e, par=par: e.tensor_copy(
                    bb[par][:], xf[par][:]), [(sv, 1)])
                EV[("xfree", par)] = sv.n
                P.wait("tensor", sv)
                tr_evac(bb[par],
                        scrab.ap().rearrange("p (d c) -> p d c", c=SEG)[
                            :, :, 128 * t:128 * (t + 1)], par, "trev")
                EV[("xt", t)] = ss.n

            def stage2a(tp):
                if tp < 0:
                    return
                par = tp % 2
                P.wait("scalar", sv, EV[("var", tp)])
                P.emit("scalar", lambda e, par=par: e.activation(
                    st16p[par]["sq"][:], st16p[par]["var"][:], AF.Ln,
                    bias=e64_col[:, 0:1]), [(ss, 1)])
                P.emit("scalar", lambda e, par=par: e.activation(
                    st16p[par]["rstd"][:], st16p[par]["sq"][:], AF.Exp,
                    scale=-0.5, bias=ln8_col[:, 0:1]), [(ss, 1)])
                P.wait("vector", ss)
                P.emit("vector", lambda e, par=par: e.scalar_tensor_tensor(
                    st16p[par]["nmr"][:], st16p[par]["sum"][:], -1.0 / HD,
                    st16p[par]["rstd"][:], ALU.mult, ALU.mult), [(sv, 1)])
                EV[("nmr", tp)] = sv.n
                VH = H - SH_A
                P.wait("vector", st_, EV.get(("bb_rd", par), 0))
                P.emit("vector", lambda e, par=par, VH=VH: e.tensor_tensor(
                    sqf2[par].ap()[:, HD * SH_A:].rearrange(
                        "p (h c) -> p h c", c=HD),
                    nscr[par].ap()[:, HD * SH_A:].rearrange(
                        "p (h c) -> p h c", c=HD),
                    st16p[par]["rstd"].ap()[:, SH_A:].rearrange(
                        "p (h o) -> p h o", o=1).to_broadcast([128, VH, HD]),
                    ALU.mult), [(sv, 1)])
                P.emit("vector", lambda e, par=par, VH=VH: e.tensor_tensor(
                    bb[par][:, HD * SH_A:].rearrange("p (h c) -> p h c", c=HD),
                    sqf2[par].ap()[:, HD * SH_A:].rearrange(
                        "p (h c) -> p h c", c=HD),
                    st16p[par]["nmr"].ap()[:, SH_A:].rearrange(
                        "p (h o) -> p h o", o=1).to_broadcast([128, VH, HD]),
                    ALU.add), [(sv, 1)])
                EV[("ttb", tp)] = sv.n
                EV[("nscr_free", par)] = sv.n
                EV[("sqfA_free", par)] = sv.n

            def stage2b(tp, pend_tr):
                if tp < 0:
                    return
                par = tp % 2
                g = NTC * s + tp
                P.wait("scalar", sv, EV[("nmr", tp)])
                P.wait("scalar", st_, EV.get(("bb_rd", par), 0))
                for h in range(SH_A):
                    P.emit("scalar", lambda e, h=h, par=par: e.activation(
                        bb[par][:, HD * h:HD * (h + 1)],
                        nscr[par][:, HD * h:HD * (h + 1)],
                        AF.Identity, bias=st16p[par]["nmr"][:, h:h + 1],
                        scale=st16p[par]["rstd"][:, h:h + 1]),
                        [(ss, 1)] if h == SH_A - 1 else ())
                pend_tr[tp] = (par, g, EV[("ttb", tp)], ss.n)

            for pas, (wnm, wdram) in enumerate(
                    [("wq", cwq), ("wk", cwk), ("wv", cwv)][:DBG_A_PASSES]):
                wslot = pas % 2
                pend_tr = {}
                P.wait("sync", st_)
                P.wait("sync", sgp, cast_done[wnm])
                P.emit("sync", lambda e, wd=wdram, wslot=wslot: e.dma_start(
                    out=wcur[wslot][:], in_=wd), [(sw, 16)])
                w_ready = sw.n
                for t in range(NTC):
                    g = NTC * s + t
                    par = t % 2
                    qb0 = 2 * par
                    P.wait("tensor", sw, w_ready)
                    P.wait("tensor", ss, EV[("xt", t)])
                    fr = EV.get(("qkvfree", par), (0, 0))
                    P.wait("tensor", ss, fr[0])
                    P.wait("tensor", sv, fr[1])
                    for d_ in range(NDC):
                        for nh in range(2):
                            P.emit("tensor",
                                   lambda e, t=t, d_=d_, nh=nh, qb0=qb0, ws=wslot:
                                   e.matmul(bank(qb0 + nh), xTv(t, d_),
                                            wcur[ws][:, D * d_ + 512 * nh:
                                                     D * d_ + 512 * (nh + 1)],
                                            start=(d_ == 0), stop=(d_ == NDC - 1)),
                                   [(st_, 1)] if (d_ == NDC - 1 and nh == 1)
                                   else ())
                    mm_done = st_.n
                    pso = ps[:, 512 * qb0:512 * qb0 + 1024]
                    if pas < 2 and DBG_A_LN > 0:
                        stage2a(t - 1)
                        ev_fn = flush_tr(pas, t - 2, pend_tr)
                        P.wait("scalar", st_, mm_done)
                        P.wait("scalar", sv, EV.get(("nscr_free", par), 0))
                        P.emit("scalar", lambda e, pso=pso, par=par: e.copy(
                            nscr[par][:], pso), [(ss, 1)])
                        EV[("qkvfree", par)] = (ss.n, 0)
                        P.wait("scalar", sv, EV.get(("sqfA_free", par), 0))
                        P.emit("scalar", lambda e, par=par: e.activation(
                            sqf2[par][:], nscr[par][:], AF.Square), [(ss, 1)])
                        sq_done = ss.n
                        P.wait("vector", ss, sq_done - 1)
                        P.emit("vector", lambda e, par=par: e.reduce_sum(
                            st16p[par]["sum"][:],
                            nscr[par].ap().rearrange("p (h c) -> p h c", c=HD),
                            axis=AX.X), [(sv, 1)])
                        P.wait("vector", ss, sq_done)
                        P.emit("vector", lambda e, par=par: e.reduce_sum(
                            st16p[par]["sq"][:],
                            sqf2[par].ap().rearrange("p (h c) -> p h c", c=HD),
                            axis=AX.X), [(sv, 1)])
                        P.emit("vector", lambda e, par=par: e.scalar_tensor_tensor(
                            st16p[par]["mean"][:], st16p[par]["sum"][:], -1.0 / HD,
                            st16p[par]["sum"][:], ALU.mult, ALU.mult), [(sv, 1)])
                        P.emit("vector", lambda e, par=par: e.tensor_tensor(
                            st16p[par]["var"][:], st16p[par]["mean"][:],
                            st16p[par]["sq"][:], ALU.add), [(sv, 1)])
                        EV[("var", t)] = sv.n
                        if ev_fn is not None:
                            ev_fn()
                        stage2b(t - 1, pend_tr)
                    elif pas == 2:
                        P.wait("scalar", st_, mm_done)
                        P.emit("scalar", lambda e, g=g, pso=pso: e.copy(
                            vsb[:, 1040 * g:1040 * (g + 1)]
                            .rearrange("p (h c) -> p h c", c=65)[:, :, 0:64],
                            pso.rearrange("p (h c) -> p h c", c=HD)), [(ss, 1)])
                        EV[("qkvfree", par)] = (ss.n, sv.n)

                if pas < 2:
                    stage2a(NTC - 1)
                    stage2b(NTC - 1, pend_tr)
                    for tt_ in (NTC - 2, NTC - 1):
                        ev_fn = flush_tr(pas, tt_, pend_tr)
                        if ev_fn is not None:
                            ev_fn()

            if "b" not in DBG_PHASES:
                continue
            P.wait("sync", st_)
            P.wait("sync", sgp, cast_done["wo"])
            P.emit("sync", lambda e: e.dma_start(
                out=wcur[1][:], in_=cwo), [(sw, 16)])
            wo_ready = sw.n
            P.wait("tensor", sgp, 192)

            for qt in range(NQT):
                Q = NQT * s + qt
                jmax = 4 * (Q + 1)
                P.wait("tensor", ss)
                P.wait("tensor", sv)
                exp_at = {}
                pv_red_at = 0
                ao_start_guard = sv.n

                def kw(bi):
                    if not KEEP_WARM:
                        return
                    P.emit("tensor", lambda e, bi=bi: e.matmul(
                        ps[0:128, 512 * bi:512 * (bi + 1)], ones_b[0:1, :],
                        tri_b[0:1, 0:512], start=True, stop=True))

                def flush_norm(i):
                    pvb = (4, 6, 7)[i % 3]
                    P.wait("tensor", ss, EV.get(("rr", i), 0))
                    P.wait("tensor", sv, EV.get("bc_evac", 0))
                    kw(5)
                    hb = QTL * (i % 2)
                    P.emit("tensor", lambda e, hb=hb: e.matmul(
                        ps[0:64, 512 * 5:512 * 6], ones_b[0:1, 0:64],
                        rrow_b[0:1, hb:hb + QTL],
                        start=True, stop=True), [(st_, 1)])
                    P.wait("vector", st_)
                    P.emit("vector", lambda e, hb=hb: e.tensor_copy(
                        bc_sb[:, hb:hb + QTL], ps[0:64, 512 * 5:512 * 6]),
                        [(sv, 1)])
                    EV["bc_evac"] = sv.n
                    P.emit("vector", lambda e, i=i, pvb=pvb, hb=hb:
                           e.tensor_tensor(
                               aot[0:128, QTL * (i // 2):QTL * (i // 2 + 1)]
                               [64 * (i % 2):64 * (i % 2) + 64, :],
                               ps[0:64, 512 * pvb:512 * pvb + QTL],
                               bc_sb[:, hb:hb + QTL], ALU.mult), [(sv, 1)])
                    EV[("aoTT", i)] = sv.n

                def emit_pv(i, js):
                    pvb = (4, 6, 7)[i % 3]
                    for j in js:
                        if j == 0:
                            P.wait("tensor", sv, EV.get(("aoTT", i - 3), 0))
                            kw(pvb)
                        P.wait("tensor", ss, exp_at.get((i, j // 2), 0))
                        if j >= 4 * Q:
                            P.wait("tensor", sv,
                                   EV.get(("mask", i, j - 4 * Q), 0))
                        P.emit("tensor", lambda e, j=j, i=i, pvb=pvb: e.matmul(
                            ps[0:65, 512 * pvb:512 * pvb + QTL],
                            vsb[:, 1040 * j + 65 * i:1040 * j + 65 * (i + 1)],
                            scrab[:, 512 * j:512 * (j + 1)],
                            start=(j == 0), stop=(j == jmax - 1)),
                            [(st_, 1)] if j == js[-1] else ())
                    if js and js[-1] == jmax - 1:
                        EV[("pvdone", i)] = st_.n

                exp_at = {}
                for i in range(DBG_B_NH + 1):
                    if i >= 3:
                        flush_norm(i - 3)
                    if i < DBG_B_NH:
                        hc, hp = i // 2, 64 * (i % 2)
                        npair = (jmax + 1) // 2
                        for jp in range(npair):
                            js = [2 * jp] + ([2 * jp + 1]
                                             if 2 * jp + 1 < jmax else [])
                            pb0 = (jp % 2) * 2
                            P.wait("tensor", ss, EV.get(("exp_bank", pb0), 0))
                            if i == 0 and jp == 0:
                                P.wait("tensor", ss)
                                P.wait("tensor", sv)
                            kw(pb0)
                            for idx, j in enumerate(js):
                                P.emit("tensor",
                                       lambda e, j=j, idx=idx, pb0=pb0, hc=hc,
                                       hp=hp, qt=qt: e.matmul(
                                           bank(pb0 + idx),
                                           ktall[hp:hp + HD,
                                                 T * hc + 128 * j:
                                                 T * hc + 128 * (j + 1)],
                                           qtseg[hp:hp + HD,
                                                 SEG * hc + QTL * qt:
                                                 SEG * hc + QTL * (qt + 1)],
                                           start=True, stop=True),
                                       [(st_, 1)] if idx == len(js) - 1 else ())
                            if i >= 1:
                                emit_pv(i - 1, [2 * jp] +
                                        ([2 * jp + 1] if 2 * jp + 1 < jmax
                                         else []))
                            P.wait("scalar", st_)
                            ncols = 512 * len(js)
                            P.emit("scalar",
                                   lambda e, jp=jp, pb0=pb0, ncols=ncols:
                                   e.activation(
                                       scrab[:, 1024 * jp:1024 * jp + ncols],
                                       ps[:, 512 * pb0:512 * pb0 + ncols],
                                       AF.Exp, scale=0.125), [(ss, 1)])
                            exp_at[(i, jp)] = ss.n
                            EV[("exp_bank", pb0)] = ss.n
                        EV[("expdone", i)] = ss.n
                        P.wait("vector", ss, EV[("expdone", i)])
                        moff = [0, 128, 384, 768]
                        for di in range(4):
                            j = 4 * Q + di
                            w = 128 * (di + 1)
                            P.emit("vector", lambda e, j=j, di=di, w=w:
                                   e.tensor_tensor(
                                       scrab[:, 512 * j:512 * j + w],
                                       scrab[:, 512 * j:512 * j + w],
                                       tri_b[:, moff[di]:moff[di] + w],
                                       ALU.mult), [(sv, 1)])
                            EV[("mask", i, di)] = sv.n
                        EV[("masks", i)] = sv.n
                    else:
                        emit_pv(i - 1, list(range(jmax)))
                    if i >= 1 and ("rr", i - 1) not in EV:
                        pvb1 = (4, 6, 7)[(i - 1) % 3]
                        P.wait("scalar", st_, EV[("pvdone", i - 1)])
                        hb1 = QTL * ((i - 1) % 2)
                        P.emit("scalar", lambda e, pvb1=pvb1: e.activation(
                            rrow_f[0:1, :],
                            ps[64:65, 512 * pvb1:512 * pvb1 + QTL],
                            AF.Ln), [(ss, 1)])
                        P.emit("scalar", lambda e, hb1=hb1: e.activation(
                            rrow_b[0:1, hb1:hb1 + QTL],
                            rrow_f[0:1, :], AF.Exp, scale=-1.0), [(ss, 1)])
                        EV[("rr", i - 1)] = ss.n
                flush_norm(DBG_B_NH - 2)
                flush_norm(DBG_B_NH - 1)
                for i in range(DBG_B_NH + 1):
                    EV.pop(("rr", i), None)
                    EV.pop(("masks", i), None)
                    EV.pop(("expdone", i), None)
                ao_done = sv.n

                for tc in range(4 if DBG_B_LVL >= 6 else 0):
                    t = 4 * qt + tc
                    g = NTC * s + t
                    par = t % 2
                    P.wait("sync", sv, EV.get(("xfree", par), 0))
                    P.wait("sync", sop[par], EV.get(("ofree", par), 0))
                    P.emit("sync", lambda e, b=b, g=g, par=par: e.dma_start(
                        out=xf[par][:], in_=dx[b, 128 * g:128 * (g + 1), :]),
                        [(sxp[par], 16)])
                    xr_ld = sxp[par].n
                    P.wait("tensor", sv, ao_done)
                    P.wait("tensor", sw, wo_ready)
                    P.wait("tensor", sv, EV.get("oev", 0))
                    for c in range(NDC):
                        for nh in range(2):
                            P.emit("tensor", lambda e, c=c, nh=nh, tc=tc:
                                   e.matmul(bank(6 + nh),
                                            aot[:, QTL * c + 128 * tc:
                                                QTL * c + 128 * (tc + 1)],
                                            wcur[1][:, D * c + 512 * nh:
                                                    D * c + 512 * (nh + 1)],
                                            start=(c == 0), stop=False))
                    for nh in range(2):
                        P.emit("tensor", lambda e, nh=nh: e.matmul(
                            bank(6 + nh), ones_b[0:1, :],
                            bo_row_b[0:1, 512 * nh:512 * (nh + 1)],
                            start=False, stop=True),
                            [(st_, 1)] if nh == 1 else ())
                    o_done = st_.n
                    P.wait("vector", st_, o_done)
                    P.wait("vector", sxp[par], xr_ld)
                    P.emit("vector", lambda e, par=par: e.scalar_tensor_tensor(
                        nscr[par][:], ps[:, 512 * 6:512 * 8], 1.0, xf[par][:],
                        ALU.mult, ALU.add, accum_out=st1["sum"][:, 0:1]),
                        [(sv, 1)])
                    EV[("xfree", par)] = sv.n
                    EV["oev"] = sv.n
                    P.emit("vector", lambda e, par=par: e.scalar_tensor_tensor(
                        sqf[:], nscr[par][:], 1.0, nscr[par][:],
                        ALU.mult, ALU.mult, accum_out=st1["sq"][:, 0:1]),
                        [(sv, 1)])
                    P.emit("vector", lambda e: e.tensor_scalar_mul(
                        st1["mean"][:], st1["sum"][:], 1.0 / D), [(sv, 1)])
                    P.emit("vector", lambda e: e.tensor_tensor(
                        st1["var"][:], st1["mean"][:], st1["mean"][:], ALU.mult),
                        [(sv, 1)])
                    P.emit("vector", lambda e: e.scalar_tensor_tensor(
                        st1["var"][:], st1["sq"][:], 1.0 / D, st1["var"][:],
                        ALU.mult, ALU.subtract), [(sv, 1)])
                    P.wait("scalar", sv)
                    P.emit("scalar", lambda e: e.activation(
                        st1["sq"][:], st1["var"][:], AF.Ln,
                        bias=eps_col[:, 0:1]), [(ss, 1)])
                    P.emit("scalar", lambda e: e.activation(
                        st1["rstd"][:], st1["sq"][:], AF.Exp, scale=-0.5),
                        [(ss, 1)])
                    P.wait("vector", ss)
                    P.emit("vector", lambda e, t=t, par=par: e.tensor_scalar(
                        h1f[:, D * t:D * (t + 1)], nscr[par][:],
                        st1["mean"][:, 0:1], st1["rstd"][:, 0:1],
                        ALU.subtract, ALU.mult), [(sv, 1)])
                    EV[("h1", t)] = sv.n

            EV["ps47_dve"] = sems["sv"].n

            if "c" not in DBG_PHASES:
                continue

            def unit_srcs(u):
                if u < NE:
                    return (cgw[u], cuw[u], cdw[u], True)
                qd = u - NE
                return (csg[qd], csu[qd], csd[qd], False)

            def load_unit_gu(u):
                gsrc, usrc, dsrc, _ = unit_srcs(u)
                slot = u % 2
                P.wait("sync", sgp, cast_done["uw" if u < NE else "su"])
                P.emit("sync", lambda e, gsrc=gsrc, slot=slot: e.dma_start(
                    out=gub[slot][:, 0:4096], in_=gsrc), [(sgu[slot], 16)])
                P.emit("sync", lambda e, usrc=usrc, slot=slot: e.dma_start(
                    out=gub[slot][:, 4096:8192], in_=usrc), [(sgu[slot], 16)])
                return sgu[slot].n

            def load_unit_dn(u):
                gsrc, usrc, dsrc, _ = unit_srcs(u)
                P.wait("sync", sgp, cast_done["dw" if u < NE else "sd"])
                P.emit("sync", lambda e, dsrc=dsrc: e.dma_start(
                    out=down_sb.ap()[:, :], in_=dsrc), [(sdn, 16)])
                return sdn.n

            pend_c = {}

            def flush_c(t):
                if t not in pend_c:
                    return
                par, cast_done_c = pend_c.pop(t)
                P.wait("tensor", sv, cast_done_c)
                tr_evac(bb[par],
                        wcur[0].ap().rearrange("p (d c) -> p d c", c=SEG)[
                            :, :, 128 * t:128 * (t + 1)], par, "trev")
                EV[("h1t", t)] = ss.n

            for t in range(NTC):
                par = t % 2
                P.wait("vector", sv, EV[("h1", t)])
                P.wait("vector", st_)
                P.emit("vector", lambda e, t=t, par=par: e.tensor_copy(
                    bb[par][:], h1f[:, D * t:D * (t + 1)]), [(sv, 1)])
                pend_c[t] = (par, sv.n)
                flush_c(t - 1)
                if t < NTC - 1:
                    continue
                flush_c(t)
            P.wait("tensor", sgp, 192)
            P.wait("tensor", sv, EV.get("router_read", 0))
            for t in range(NTC):
                P.wait("tensor", ss, EV[("h1t", t)])
                for d_ in range(NDC):
                    P.emit("tensor", lambda e, t=t, d_=d_: e.matmul(
                        ps[:, NE * t:NE * (t + 1)],
                        h1T[:, SEG * d_ + 128 * t:SEG * d_ + 128 * (t + 1)],
                        rw_sb[:, NE * d_:NE * (d_ + 1)],
                        start=(d_ == 0), stop=(d_ == NDC - 1)),
                        [(st_, 1)] if d_ == NDC - 1 else ())
            rt_done = st_.n

            def r3(t_):
                return t_.ap().rearrange("p (t e) -> p t e", e=NE)

            def c3(t_):
                return t_.ap().rearrange("p (t o) -> p t o", o=1).to_broadcast(
                    [128, NTC, NE])

            P.wait("vector", st_, rt_done)
            P.emit("vector", lambda e: e.tensor_copy(
                r8["logit"][:], bank(0, NTC * NE)), [(sv, 1)])
            EV["router_read"] = sv.n
            P.emit("vector", lambda e: e.reduce_max(
                rcol["nmax"][:], r3(r8["logit"]), axis=AX.X, negate=True),
                [(sv, 1)])
            P.emit("vector", lambda e: e.tensor_tensor(
                r3(r8["logit"]), r3(r8["logit"]), c3(rcol["nmax"]), ALU.add),
                [(sv, 1)])
            P.wait("scalar", sv)
            P.emit("scalar", lambda e: e.activation(
                r8["score"][:], r8["logit"][:], AF.Exp), [(ss, 1)])
            P.wait("vector", ss)
            P.emit("vector", lambda e: e.reduce_sum(
                rcol["den"][:], r3(r8["score"]), axis=AX.X), [(sv, 1)])
            P.emit("vector", lambda e: e.reciprocal(
                rcol["rden"][:], rcol["den"][:]), [(sv, 1)])
            P.emit("vector", lambda e: e.tensor_tensor(
                r3(r8["score"]), r3(r8["score"]), c3(rcol["rden"]), ALU.mult),
                [(sv, 1)])
            P.emit("vector", lambda e: e.tensor_scalar_add(
                r8["corr"][:], r8["score"][:], rbias_col[:, 0:1]), [(sv, 1)])
            for t in range(NTC):
                P.emit("vector", lambda e, t=t: e.max(
                    out=m8w[:, NE * t:NE * (t + 1)],
                    in_=r8["corr"][:, NE * t:NE * (t + 1)]), [(sv, 1)],
                    nodrain=(t > 0))
            thr3 = m8w.ap().rearrange("p (t e) -> p t e", e=NE)[
                :, :, 3:4].to_broadcast([128, NTC, NE])
            P.emit("vector", lambda e, thr3=thr3: e.tensor_tensor(
                r3(r8["sel"]), r3(r8["corr"]), thr3, ALU.is_ge), [(sv, 1)])
            P.emit("vector", lambda e: e.tensor_mul(
                r8["msk"][:], r8["sel"][:], r8["score"][:]), [(sv, 1)])
            P.emit("vector", lambda e: e.reduce_sum(
                rcol["den4"][:], r3(r8["msk"]), axis=AX.X), [(sv, 1)])
            P.emit("vector", lambda e: e.tensor_scalar_add(
                rcol["den4"][:], rcol["den4"][:], 1e-20), [(sv, 1)])
            P.emit("vector", lambda e: e.reciprocal(
                rcol["rden4"][:], rcol["den4"][:]), [(sv, 1)])
            P.emit("vector", lambda e: e.tensor_tensor(
                w_seg.ap().rearrange("p (t e) -> p t e", e=NE),
                r3(r8["msk"]), c3(rcol["rden4"]), ALU.mult), [(sv, 1)])
            for t in range(NTC):
                EV[("w", t)] = sv.n

            P.wait("sync", st_)
            u_gu = {0: load_unit_gu(0)}
            u_dn = {0: load_unit_dn(0)}
            EV["pipeline_prev"] = None

            def emit_down(u, tq, act_done, routed):
                tc0, tc1 = 2 * tq, 2 * tq + 1
                apar = 1024 * (tq % 2)
                P.wait("tensor", sv, act_done)
                P.wait("tensor", sdn, u_dn[u])
                P.wait("tensor", sv, EV.get("dadd_last", 0))
                for tci in range(2):
                    for m in range(4):
                        for nh in range(2):
                            P.emit("tensor",
                                   lambda e, tci=tci, m=m, nh=nh, apar=apar:
                                   e.matmul(
                                       bank(4 + 2 * tci + nh),
                                       actw[:, apar + 256 * m + 128 * tci:
                                            apar + 256 * m + 128 * tci + 128],
                                       down_sb[:, D * m + 512 * nh:
                                               D * m + 512 * (nh + 1)],
                                       start=(m == 0), stop=(m == 3)),
                                   [(st_, 1)]
                                   if (m == 3 and nh == 1 and tci == 1)
                                   else ())
                dn_done = st_.n
                EV[("act_read_t", tq % 2)] = dn_done
                P.wait("vector", st_, dn_done)
                for tci, tcg in enumerate((tc0, tc1)):
                    pd = 512 * (4 + 2 * tci)
                    if routed:
                        P.emit("vector", lambda e, tcg=tcg, pd=pd, u=u:
                               e.scalar_tensor_tensor(
                                   h1f[:, D * tcg:D * (tcg + 1)],
                                   ps[:, pd:pd + 1024],
                                   w_seg[:, NE * tcg + u:NE * tcg + u + 1],
                                   h1f[:, D * tcg:D * (tcg + 1)],
                                   ALU.mult, ALU.add), [(sv, 1)])
                    else:
                        P.emit("vector", lambda e, tcg=tcg, pd=pd: e.tensor_add(
                            h1f[:, D * tcg:D * (tcg + 1)],
                            h1f[:, D * tcg:D * (tcg + 1)],
                            ps[:, pd:pd + 1024]), [(sv, 1)])
                EV["dadd_last"] = sv.n
                EV[("dadd", (u, tq))] = sv.n
            for u in range(NU):
                slot = u % 2
                routed = u < NE
                for tq in range(NTQ):
                    tc0, tc1 = 2 * tq, 2 * tq + 1
                    P.wait("tensor", sgu[u % 2], u_gu[u])
                    P.wait("tensor", sv, EV.get("router_read", 0))
                    P.wait("tensor", ss, EV.get("guev_s", 0))
                    P.wait("tensor", sv, EV.get("guev_v", 0))
                    for mat in range(2):
                        for m in range(4):
                            for d_ in range(NDC):
                                P.emit("tensor",
                                       lambda e, mat=mat, m=m, d_=d_, tq=tq,
                                       slot=slot: e.matmul(
                                           ps[:, 1024 * mat + 256 * m:
                                              1024 * mat + 256 * (m + 1)],
                                           gub[slot][:, 4096 * mat + IE * d_ +
                                                     128 * m:
                                                     4096 * mat + IE * d_ +
                                                     128 * (m + 1)],
                                           h1T[:, SEG * d_ + TOKQ * tq:
                                               SEG * d_ + TOKQ * (tq + 1)],
                                           start=(d_ == 0),
                                           stop=(d_ == NDC - 1)),
                                       [(st_, 1)]
                                       if (d_ == NDC - 1 and m == 3) else ())
                    gu_done = st_.n
                    if tq == 0 and u + 1 < NU:
                        P.wait("sync", st_)
                        u_gu[u + 1] = load_unit_gu(u + 1)
                    P.wait("scalar", st_, gu_done)
                    P.wait("scalar", sv, EV.get("act_read_v", 0))
                    P.emit("scalar", lambda e: e.activation(
                        silu_sb[:, 0:1024], ps[:, 0:1024], AF.Silu), [(ss, 1)])
                    EV["guev_s"] = ss.n
                    P.wait("vector", st_, gu_done)
                    P.wait("vector", ss)
                    P.emit("vector", lambda e: e.tensor_copy(
                        silu_sb[:, 1024:2048], ps[:, 1024:2048]), [(sv, 1)])
                    EV["guev_v"] = sv.n
                    P.wait("vector", st_, EV.get(("act_read_t", tq % 2), 0))
                    apar = 1024 * (tq % 2)
                    P.emit("vector", lambda e, apar=apar: e.tensor_tensor(
                        actw[:, apar:apar + 1024], silu_sb[:, 0:1024],
                        silu_sb[:, 1024:2048], ALU.mult), [(sv, 1)])
                    act_done = sv.n
                    EV["act_read_v"] = act_done
                    prev = EV.get("pipeline_prev")
                    EV["pipeline_prev"] = (u, tq, act_done, routed)
                    if prev is not None:
                        emit_down(*prev)
                        if prev[1] == NTQ - 1 and u not in u_dn:
                            P.wait("sync", st_)
                            u_dn[u] = load_unit_dn(u)


            if EV.get("pipeline_prev") is not None:
                emit_down(*EV["pipeline_prev"])
                EV["pipeline_prev"] = None

            EV["ps47_dve"] = sems["sv"].n
            for t in range(NTC):
                par = t % 2
                P.wait("vector", sv, EV[("dadd", (NU - 1, t // 2))])
                P.emit("vector", lambda e, t=t: e.scalar_tensor_tensor(
                    sqf[:], h1f[:, D * t:D * (t + 1)], 1.0,
                    h1f[:, D * t:D * (t + 1)], ALU.mult, ALU.mult,
                    accum_out=st1["sq"][:, 0:1]), [(sv, 1)])
                P.emit("vector", lambda e, t=t: e.reduce_sum(
                    st1["sum"][:], h1f[:, D * t:D * (t + 1)], axis=AX.X),
                    [(sv, 1)])
                P.emit("vector", lambda e: e.tensor_scalar_mul(
                    st1["mean"][:], st1["sum"][:], 1.0 / D), [(sv, 1)])
                P.emit("vector", lambda e: e.tensor_tensor(
                    st1["var"][:], st1["mean"][:], st1["mean"][:], ALU.mult),
                    [(sv, 1)])
                P.emit("vector", lambda e: e.scalar_tensor_tensor(
                    st1["var"][:], st1["sq"][:], 1.0 / D, st1["var"][:],
                    ALU.mult, ALU.subtract), [(sv, 1)])
                P.wait("scalar", sv)
                P.emit("scalar", lambda e: e.activation(
                    st1["sq"][:], st1["var"][:], AF.Ln, bias=eps_col[:, 0:1]),
                    [(ss, 1)])
                P.emit("scalar", lambda e: e.activation(
                    st1["rstd"][:], st1["sq"][:], AF.Exp, scale=-0.5), [(ss, 1)])
                P.wait("vector", ss)
                P.wait("vector", sop[par], EV.get(("ofree", par), 0))
                P.emit("vector", lambda e, t=t, par=par: e.tensor_scalar(
                    xf[par][:], h1f[:, D * t:D * (t + 1)],
                    st1["mean"][:, 0:1], st1["rstd"][:, 0:1],
                    ALU.subtract, ALU.mult), [(sv, 1)])
                apply_done = sv.n
                P.wait("sync", sv, apply_done)
                g = NTC * s + t
                P.emit("sync", lambda e, b=b, g=g, par=par: e.dma_start(
                    out=dout[b, 128 * g:128 * (g + 1), :], in_=xf[par][:]),
                    [(sop[par], 16)])
                EV[("ofree", par)] = sop[par].n

    P.wait("sync", sop[0])
    P.wait("sync", sop[1])

    with nc.Block() as block:
        @block.sync
        def _(e):
            P.run("sync", e)

        @block.gpsimd
        def _(e):
            P.run("gpsimd", e)

        @block.tensor
        def _(e):
            P.run("tensor", e)

        @block.scalar
        def _(e):
            P.run("scalar", e)

        @block.vector
        def _(e):
            P.run("vector", e)

    es.close()
    return nc


_NC_CACHE = None


def prep_common(L0_attn_Wk, L0_attn_Wq, L0_attn_Wv, L0_attn_Wo, L0_attn_bo,
                L0_attn_kln_w, L0_attn_kln_b, L0_attn_qln_w, L0_attn_qln_b,
                L0_ffn_router_w, L0_ffn_router_bias, L0_ffn_gate_w,
                L0_ffn_up_w, L0_ffn_down_w, L0_ffn_s_gate_w, L0_ffn_s_up_w,
                L0_ffn_s_down_w):
    f32 = lambda a: np.ascontiguousarray(np.asarray(a, dtype=np.float32))
    def lay2(w):
        w = np.asarray(w, np.float32)
        return np.ascontiguousarray(
            w.reshape(-1, 128, w.shape[-1]).transpose(1, 0, 2))

    def lay3(w):
        w = np.asarray(w, np.float32)
        return np.ascontiguousarray(
            w.reshape(w.shape[0], -1, 128, w.shape[-1]).transpose(0, 2, 1, 3))

    sg_q = np.stack([f32(L0_ffn_s_gate_w)[:, 512 * q:512 * (q + 1)]
                     for q in range(4)])
    su_q = np.stack([f32(L0_ffn_s_up_w)[:, 512 * q:512 * (q + 1)]
                     for q in range(4)])
    sd_q = np.stack([f32(L0_ffn_s_down_w)[512 * q:512 * (q + 1), :]
                     for q in range(4)])
    common = dict(
        wq=lay2(L0_attn_Wq), wk=lay2(L0_attn_Wk), wv=lay2(L0_attn_Wv),
        wo=lay2(L0_attn_Wo), bo=f32(L0_attn_bo).reshape(1, D),
        klw=f32(L0_attn_kln_w).reshape(1, HD),
        klb=f32(L0_attn_kln_b).reshape(1, HD),
        qlw=f32(L0_attn_qln_w).reshape(1, HD),
        qlb=f32(L0_attn_qln_b).reshape(1, HD),
        rw=lay2(L0_ffn_router_w), rb=f32(L0_ffn_router_bias).reshape(1, 1),
        gw=lay3(L0_ffn_gate_w), uw=lay3(L0_ffn_up_w), dw=lay3(L0_ffn_down_w),
        sg=lay3(sg_q), su=lay3(su_q), sd=lay3(sd_q),
        ident=np.eye(128, dtype=ml_dtypes.bfloat16),
        tri=_diag_masks(),
    )
    return common


def kernel(L0_attn_x, L0_attn_Wk, L0_attn_Wq, L0_attn_Wv, L0_attn_Wo,
           L0_attn_bo, L0_attn_kln_w, L0_attn_kln_b, L0_attn_qln_w,
           L0_attn_qln_b, L0_ffn_router_w, L0_ffn_router_bias, L0_ffn_gate_w,
           L0_ffn_up_w, L0_ffn_down_w, L0_ffn_s_gate_w, L0_ffn_s_up_w,
           L0_ffn_s_down_w):
    global _NC_CACHE
    x = np.ascontiguousarray(np.asarray(L0_attn_x, dtype=np.float32))
    common = prep_common(L0_attn_Wk, L0_attn_Wq, L0_attn_Wv, L0_attn_Wo,
                         L0_attn_bo, L0_attn_kln_w, L0_attn_kln_b,
                         L0_attn_qln_w, L0_attn_qln_b, L0_ffn_router_w,
                         L0_ffn_router_bias, L0_ffn_gate_w, L0_ffn_up_w,
                         L0_ffn_down_w, L0_ffn_s_gate_w, L0_ffn_s_up_w,
                         L0_ffn_s_down_w)
    in_maps = []
    for c in range(NCORES):
        m = dict(common)
        m["x"] = np.ascontiguousarray(x[NB * c:NB * (c + 1)])
        in_maps.append(m)
    if _NC_CACHE is None:
        _NC_CACHE = build_core()
    res = run_bass_kernel_spmd(_NC_CACHE, in_maps, core_ids=list(range(NCORES)))
    return np.concatenate([np.asarray(r["out"]) for r in res.results], axis=0)


def _diag_masks():
    parts = []
    for i in range(4):
        w = 128 * (i + 1)
        q = np.arange(w)[None, :]
        k = np.arange(128)[:, None]
        parts.append((q >= 128 * i + k).astype(np.float32))
    return np.concatenate(parts, axis=1).astype(ml_dtypes.bfloat16)


if __name__ == "__main__":
    nc = build_core()
    print("built ok; instructions:", nc.n_instructions()
          if hasattr(nc, "n_instructions") else "?")


# version 20 (final)
# speedup vs baseline: 1.0972x; 1.0255x over previous
import contextlib
import numpy as np
import ml_dtypes

import concourse.bass as bass
import concourse.mybir as mybir
from concourse.bass_utils import run_bass_kernel_spmd

F32 = mybir.dt.float32
BF16 = mybir.dt.bfloat16
AF = mybir.ActivationFunctionType
ALU = mybir.AluOpType
AX = mybir.AxisListType

NCORES = 8
NB = 4
T = 2048
D = 1024
H = 16
HD = 64
NDC = 8
SEG = 1024
NSEG = 2
NTC = 8
QTL = 512
NQT = 2
TOKQ = 256
NTQ = 4
NE = 8
IE = 512
NU = 12
DBG_NB = None
DBG_NSEG = None
DBG_PHASES = "abc"
DBG_A_PASSES = 3
DBG_A_LN = 9
DBG_B_NH = 16
DBG_B_LVL = 9
DBG_PV_BANK = 4
DBG_PRO_BCAST = 1
EPS = 1e-5

ENGINES = ("tensor", "scalar", "vector", "sync", "gpsimd")
SAFE_SYNC = False
SH_A = 4
KEEP_WARM = False


class Sem:
    def __init__(self, handle, name):
        self.h = handle
        self.name = name
        self.n = 0


class Plan:

    def __init__(self):
        self.items = {e: [] for e in ENGINES}

    def emit(self, eng, fn, incs=(), nodrain=False):
        incs = tuple(incs)
        if (eng == "vector" and not nodrain) or (eng == "scalar" and SAFE_SYNC):
            inner = fn
            sep = "drain"
            if eng == "scalar":
                sep = "drain"

            def fn(v, _inner=inner, sep=sep):
                getattr(v, sep)()
                return _inner(v)
        self.items[eng].append(("op", fn, incs))
        for sem, k in incs:
            sem.n += k

    def wait(self, eng, sem, val=None):
        v = sem.n if val is None else val
        if v > 0:
            self.items[eng].append(("wait", sem, v))

    def run(self, eng_name, eng):
        for it in self.items[eng_name]:
            if it[0] == "op":
                _, fn, incs = it
                ins = fn(eng)
                for sem, k in incs:
                    ins.then_inc(sem.h, k)
            else:
                _, sem, v = it
                eng.wait_ge(sem.h, v)


def build_core():
    nc = bass.Bass("TRN2", target_bir_lowering=False, debug=False)

    dx = nc.dram_tensor("x", [NB, T, D], F32, kind="ExternalInput").ap()
    dwq = nc.dram_tensor("wq", [128, NDC, D], F32, kind="ExternalInput").ap()
    dwk = nc.dram_tensor("wk", [128, NDC, D], F32, kind="ExternalInput").ap()
    dwv = nc.dram_tensor("wv", [128, NDC, D], F32, kind="ExternalInput").ap()
    dwo = nc.dram_tensor("wo", [128, NDC, D], F32, kind="ExternalInput").ap()
    dbo = nc.dram_tensor("bo", [1, D], F32, kind="ExternalInput").ap()
    dklw = nc.dram_tensor("klw", [1, HD], F32, kind="ExternalInput").ap()
    dklb = nc.dram_tensor("klb", [1, HD], F32, kind="ExternalInput").ap()
    dqlw = nc.dram_tensor("qlw", [1, HD], F32, kind="ExternalInput").ap()
    dqlb = nc.dram_tensor("qlb", [1, HD], F32, kind="ExternalInput").ap()
    drw = nc.dram_tensor("rw", [128, NDC, NE], F32, kind="ExternalInput").ap()
    drb = nc.dram_tensor("rb", [1, 1], F32, kind="ExternalInput").ap()
    dgw = nc.dram_tensor("gw", [NE, 128, NDC, IE], F32, kind="ExternalInput").ap()
    duw = nc.dram_tensor("uw", [NE, 128, NDC, IE], F32, kind="ExternalInput").ap()
    ddw = nc.dram_tensor("dw", [NE, 128, 4, D], F32, kind="ExternalInput").ap()
    dsg = nc.dram_tensor("sg", [4, 128, NDC, IE], F32, kind="ExternalInput").ap()
    dsu = nc.dram_tensor("su", [4, 128, NDC, IE], F32, kind="ExternalInput").ap()
    dsd = nc.dram_tensor("sd", [4, 128, 4, D], F32, kind="ExternalInput").ap()
    cwq = nc.dram_tensor("cwq", [128, NDC * D], BF16, kind="Internal").ap()
    cwk = nc.dram_tensor("cwk", [128, NDC * D], BF16, kind="Internal").ap()
    cwv = nc.dram_tensor("cwv", [128, NDC * D], BF16, kind="Internal").ap()
    cwo = nc.dram_tensor("cwo", [128, NDC * D], BF16, kind="Internal").ap()
    cgw = nc.dram_tensor("cgw", [NE, 128, NDC * IE], BF16, kind="Internal").ap()
    cuw = nc.dram_tensor("cuw", [NE, 128, NDC * IE], BF16, kind="Internal").ap()
    cdw = nc.dram_tensor("cdw", [NE, 128, 4 * D], BF16, kind="Internal").ap()
    csg = nc.dram_tensor("csg", [4, 128, NDC * IE], BF16, kind="Internal").ap()
    csu = nc.dram_tensor("csu", [4, 128, NDC * IE], BF16, kind="Internal").ap()
    csd = nc.dram_tensor("csd", [4, 128, 4 * D], BF16, kind="Internal").ap()
    dident = nc.dram_tensor("ident", [128, 128], BF16, kind="ExternalInput").ap()
    dtri = nc.dram_tensor("tri", [128, 1280], BF16, kind="ExternalInput").ap()
    dout = nc.dram_tensor("out", [NB, T, D], F32, kind="ExternalOutput").ap()

    es = contextlib.ExitStack()
    S = lambda n, sh, dt: es.enter_context(nc.sbuf_tensor(n, sh, dt))

    ident_b = S("ident_b", [128, 128], BF16)
    tri_b = S("tri_b", [128, 1280], BF16)
    ones_b = S("ones_b", [1, 128], BF16)
    ones_f = S("ones_f", [1, 128], F32)
    bo_row_b = S("bo_row_b", [1, D], BF16)
    lnrows = S("lnrows", [1, 4 * HD], F32)
    klw_r = lnrows.ap()[:, 0:HD]
    klb_r = lnrows.ap()[:, HD:2 * HD]
    qlw_r = lnrows.ap()[:, 2 * HD:3 * HD]
    qlb_r = lnrows.ap()[:, 3 * HD:4 * HD]
    rb11 = S("rb11", [1, 1], F32)
    rbias_col = S("rbias_col", [128, 1], F32)
    eps_col = S("eps_col", [128, 1], F32)
    qkcol = S("qkcol", [128, 4], F32)
    e64_col = S("e64_col", [128, 1], F32)
    ln8_col = S("ln8_col", [128, 1], F32)
    rw_sb = S("rw_sb", [128, NDC * NE], BF16)
    ktall = S("ktall", [128, NDC * T], BF16)
    vsb = S("vsb", [128, 16 * (H * 65)], BF16)
    h1f = S("h1f", [128, NTC * D], F32)
    xf = [S("xf0", [128, D], F32), S("xf1", [128, D], F32)]
    sqf2 = [S("sqf0", [128, D], F32), S("sqf1", [128, D], F32)]
    sqf = sqf2[0]
    st16p = [{k: S(f"st16_{k}{i}", [128, H], F32)
              for k in ("sum", "sq", "mean", "var", "rstd", "nmr")}
             for i in range(2)]
    st1 = {k: S("st1_" + k, [128, 1], F32)
           for k in ("sum", "sq", "mean", "var", "rstd")}
    w_seg = S("w_seg", [128, NTC * NE], F32)
    rrow_f = S("rrow_f", [1, QTL], F32)
    rrow_b = S("rrow_b", [1, 2 * QTL], BF16)
    bc_sb = S("bc_sb", [64, 2 * QTL], BF16)
    r8 = {k: S("r8_" + k, [128, NTC * NE], F32)
          for k in ("logit", "score", "sel", "msk")}
    r8["corr"] = w_seg
    m8w = r8["logit"]
    rcol = {k: S("rcol_" + k, [128, NTC], F32)
            for k in ("nmax", "den", "rden", "den4", "rden4")}

    wcur = [S("wcur0", [128, NDC * D], BF16), S("wcur1", [128, NDC * D], BF16)]
    scrab = S("scrab", [128, 8192], BF16)
    qtseg = S("qtseg", [128, NDC * SEG], BF16)
    aot = S("aot", [128, NDC * QTL], BF16)
    bb = [aot.ap()[:, 0:D], aot.ap()[:, D:2 * D]]
    nscr = [S("nscr0", [128, D], F32), S("nscr1", [128, D], F32)]

    h1T = wcur[0]
    gub = [qtseg, scrab]
    down_sb = aot
    silu_sb = nscr[0].ap().bitcast(BF16)
    actw = nscr[1].ap().bitcast(BF16)

    ps = es.enter_context(nc.psum_tensor("ps", [128, 4096], F32))

    def bank(b, w=512):
        return ps[:, 512 * b: 512 * b + w]

    def bank_b16(b, off, cols):
        return ps[:, 512 * b + off // 2: 512 * b + (off + cols) // 2].bitcast(BF16)

    sems = {}
    for name in ("scs", "sx0", "sx1", "sw", "sgu0", "sgu1", "sdn", "sgp",
                 "so0", "so1", "st", "ss", "sv"):
        sems[name] = Sem(es.enter_context(nc.semaphore(name)), name)
    scs = sems["scs"]
    sxp = [sems["sx0"], sems["sx1"]]
    sw = sems["sw"]
    sgu = [sems["sgu0"], sems["sgu1"]]
    sdn = sems["sdn"]
    sgp = sems["sgp"]
    sop = [sems["so0"], sems["so1"]]
    st_, ss, sv = sems["st"], sems["ss"], sems["sv"]

    P = Plan()
    EV = {}

    def xTv(t, d):
        return scrab[:, SEG * d + 128 * t: SEG * d + 128 * (t + 1)]

    def tr_part(src_bb, par, producer_key):
        trb = 4 + par
        P.wait("tensor", ss, EV.get((producer_key, par), 0))
        P.wait("tensor", sv, EV.get("ps47_dve", 0))
        for d_ in range(NDC):
            P.emit("tensor", lambda e, d_=d_, trb=trb, src_bb=src_bb:
                   e.transpose(bank_b16(trb, 128 * d_, 128),
                               src_bb[:, 128 * d_:128 * (d_ + 1)], ident_b[:]),
                   [(st_, 1)] if d_ == NDC - 1 else ())
        EV[("bb_rd", par)] = st_.n
        return st_.n, trb

    def ev_part(dst3, trb, tr_done, par, producer_key, scale=None, bias=None):
        P.wait("scalar", st_, tr_done)
        src3 = bank_b16(trb, 0, 1024).rearrange("p (d c) -> p d c", c=128)
        if scale is None:
            P.emit("scalar", lambda e, dst3=dst3, src3=src3:
                   e.copy(dst3, src3), [(ss, 1)])
        else:
            P.emit("scalar", lambda e, dst3=dst3, src3=src3, scale=scale,
                   bias=bias: e.activation(dst3, src3, AF.Identity,
                                           scale=scale, bias=bias), [(ss, 1)])
        EV[(producer_key, par)] = ss.n

    def tr_evac(src_bb, dst3, par, producer_key, scale=None, bias=None):
        tr_done, trb = tr_part(src_bb, par, producer_key)
        ev_part(dst3, trb, tr_done, par, producer_key, scale, bias)

    for ap_in, tile in [(dident, ident_b[:]), (dtri, tri_b[:]),
                        (dklw, klw_r), (dklb, klb_r), (dqlw, qlw_r),
                        (dqlb, qlb_r), (drb, rb11[:])]:
        P.emit("sync", lambda e, a=ap_in, t=tile: e.dma_start(out=t, in_=a),
               [(scs, 16)])
    for j, srow in enumerate([dqlw, dqlb, dklw, dklb]):
        for half in range(2):
            P.emit("sync", lambda e, srow=srow, j=j, half=half: e.dma_start(
                out=qkcol[64 * half:64 * (half + 1), j:j + 1], in_=srow),
                [(scs, 16)])
    P.wait("scalar", scs)
    P.emit("gpsimd", lambda e: e.dma_start(
        out=rw_sb.ap().rearrange("p (c e) -> p c e", e=NE), in_=drw),
        [(sgp, 16)])
    P.emit("gpsimd", lambda e: e.dma_start(out=bo_row_b[:], in_=dbo), [(sgp, 16)])
    cast_done = {}
    for nm, csrc, cdst in [("wq", dwq, cwq), ("wk", dwk, cwk), ("wv", dwv, cwv),
                           ("wo", dwo, cwo), ("gw", dgw, cgw), ("uw", duw, cuw),
                           ("dw", ddw, cdw), ("sg", dsg, csg), ("su", dsu, csu),
                           ("sd", dsd, csd)]:
        P.emit("gpsimd", lambda e, a=csrc, b=cdst: e.dma_start(
            out=b, in_=a.rearrange("... p c f -> ... p (c f)")
            if len(a.shape) == 4 else a.rearrange("p c f -> p (c f)")),
            [(sgp, 16)])
    for nm in ("wq", "wk", "wv", "wo", "gw", "uw", "dw", "sg", "su", "sd"):
        cast_done[nm] = sgp.n
    P.wait("vector", scs)
    P.wait("vector", sgp)
    P.emit("vector", lambda e: e.memset(ones_b[:], 1.0), [(sv, 1)])
    P.emit("vector", lambda e: e.memset(ones_f[:], 1.0), [(sv, 1)])
    P.emit("vector", lambda e: e.memset(eps_col[:], EPS), [(sv, 1)])
    P.emit("vector", lambda e: e.memset(e64_col[:], HD * EPS), [(sv, 1)])
    P.emit("vector", lambda e: e.memset(ln8_col[:], 2.0794415416798357), [(sv, 1)])
    P.emit("vector", lambda e: e.memset(
        vsb.ap().rearrange("p (g h c) -> p g h c", h=H, c=65)[:, :, :, 64:65],
        1.0), [(sv, 1)])
    P.wait("tensor", sv)
    for i, (dst, srow, cols) in enumerate([(rbias_col, rb11, 1)]):
        if DBG_PRO_BCAST:
            P.emit("tensor", lambda e, srow=srow, cols=cols, i=i:
                   e.matmul(bank(i, cols), ones_f[0:1, :], srow[0:1, :],
                            start=True, stop=True), [(st_, 1)])
            P.wait("vector", st_)
            P.emit("vector", lambda e, dst=dst, cols=cols, i=i:
                   e.tensor_copy(dst[:, 0:cols], bank(i, cols)), [(sv, 1)])
        else:
            P.emit("vector", lambda e, dst=dst, cols=cols:
                   e.memset(dst[:, 0:cols], 0.5), [(sv, 1)])

    for b in range(NB if DBG_NB is None else DBG_NB):
        for s in range(NSEG if DBG_NSEG is None else DBG_NSEG):

            def flush_tr(pas, t, pend_tr):
                if t not in pend_tr:
                    return None
                par, g, ttb_done, app_done = pend_tr.pop(t)
                P.wait("tensor", sv, ttb_done)
                P.wait("tensor", ss, app_done)
                tr_done, trb = tr_part(bb[par], par, "trev")
                if pas == 0:
                    dst3 = qtseg.ap().rearrange("p (d c) -> p d c", c=SEG)[
                        :, :, 128 * t:128 * (t + 1)]
                    sc, bi = qkcol[:, 0:1], qkcol[:, 1:2]
                else:
                    dst3 = ktall.ap().rearrange("p (d c) -> p d c", c=T)[
                        :, :, 128 * g:128 * (g + 1)]
                    sc, bi = qkcol[:, 2:3], qkcol[:, 3:4]
                return lambda: ev_part(dst3, trb, tr_done, par, "trev", sc, bi)

            for t in range(NTC):
                g = NTC * s + t
                par = t % 2
                P.wait("sync", sv, EV.get(("xfree", par), 0))
                P.wait("sync", sop[par], EV.get(("ofree", par), 0))
                P.emit("sync", lambda e, b=b, g=g, par=par: e.dma_start(
                    out=xf[par][:], in_=dx[b, 128 * g:128 * (g + 1), :]),
                    [(sxp[par], 16)])
                x_ld = sxp[par].n
                P.wait("vector", sxp[par], x_ld)
                P.wait("vector", st_, EV.get(("bb_rd", par), 0))
                P.wait("vector", st_, EV.get("aot_rd", 0))
                P.emit("vector", lambda e, par=par: e.tensor_copy(
                    bb[par][:], xf[par][:]), [(sv, 1)])
                EV[("xfree", par)] = sv.n
                P.wait("tensor", sv)
                tr_evac(bb[par],
                        scrab.ap().rearrange("p (d c) -> p d c", c=SEG)[
                            :, :, 128 * t:128 * (t + 1)], par, "trev")
                EV[("xt", t)] = ss.n

            def stage2a(tp):
                if tp < 0:
                    return
                par = tp % 2
                P.wait("scalar", sv, EV[("var", tp)])
                P.emit("scalar", lambda e, par=par: e.activation(
                    st16p[par]["sq"][:], st16p[par]["var"][:], AF.Ln,
                    bias=e64_col[:, 0:1]), [(ss, 1)])
                P.emit("scalar", lambda e, par=par: e.activation(
                    st16p[par]["rstd"][:], st16p[par]["sq"][:], AF.Exp,
                    scale=-0.5, bias=ln8_col[:, 0:1]), [(ss, 1)])
                P.wait("vector", ss)
                P.emit("vector", lambda e, par=par: e.scalar_tensor_tensor(
                    st16p[par]["nmr"][:], st16p[par]["sum"][:], -1.0 / HD,
                    st16p[par]["rstd"][:], ALU.mult, ALU.mult), [(sv, 1)])
                EV[("nmr", tp)] = sv.n
                VH = H - SH_A
                P.wait("vector", st_, EV.get(("bb_rd", par), 0))
                P.emit("vector", lambda e, par=par, VH=VH: e.tensor_tensor(
                    sqf2[par].ap()[:, HD * SH_A:].rearrange(
                        "p (h c) -> p h c", c=HD),
                    nscr[par].ap()[:, HD * SH_A:].rearrange(
                        "p (h c) -> p h c", c=HD),
                    st16p[par]["rstd"].ap()[:, SH_A:].rearrange(
                        "p (h o) -> p h o", o=1).to_broadcast([128, VH, HD]),
                    ALU.mult), [(sv, 1)])
                P.emit("vector", lambda e, par=par, VH=VH: e.tensor_tensor(
                    bb[par][:, HD * SH_A:].rearrange("p (h c) -> p h c", c=HD),
                    sqf2[par].ap()[:, HD * SH_A:].rearrange(
                        "p (h c) -> p h c", c=HD),
                    st16p[par]["nmr"].ap()[:, SH_A:].rearrange(
                        "p (h o) -> p h o", o=1).to_broadcast([128, VH, HD]),
                    ALU.add), [(sv, 1)])
                EV[("ttb", tp)] = sv.n
                EV[("nscr_free", par)] = sv.n
                EV[("sqfA_free", par)] = sv.n

            def stage2b(tp, pend_tr):
                if tp < 0:
                    return
                par = tp % 2
                g = NTC * s + tp
                P.wait("scalar", sv, EV[("nmr", tp)])
                P.wait("scalar", st_, EV.get(("bb_rd", par), 0))
                for h in range(SH_A):
                    P.emit("scalar", lambda e, h=h, par=par: e.activation(
                        bb[par][:, HD * h:HD * (h + 1)],
                        nscr[par][:, HD * h:HD * (h + 1)],
                        AF.Identity, bias=st16p[par]["nmr"][:, h:h + 1],
                        scale=st16p[par]["rstd"][:, h:h + 1]),
                        [(ss, 1)] if h == SH_A - 1 else ())
                pend_tr[tp] = (par, g, EV[("ttb", tp)], ss.n)

            wplan = [("wq", cwq, 0), ("wk", cwk, 1), ("wv", cwv, 0),
                     ("wo", cwo, 1)]
            wrdy = {}

            def load_w(idx):
                if idx >= len(wplan):
                    return
                nm, wsrc, slot = wplan[idx]
                P.wait("sync", st_, EV.get(("wfree", slot), 0))
                P.wait("sync", sgp, cast_done[nm])
                P.emit("sync", lambda e, wsrc=wsrc, slot=slot: e.dma_start(
                    out=wcur[slot][:], in_=wsrc), [(sw, 16)])
                wrdy[idx] = sw.n

            load_w(0)
            for pas, (wnm, wdram) in enumerate(
                    [("wq", cwq), ("wk", cwk), ("wv", cwv)][:DBG_A_PASSES]):
                wslot = pas % 2
                pend_tr = {}
                w_ready = wrdy[pas]
                for t in range(NTC):
                    if t == 1:
                        load_w(pas + 1)
                    g = NTC * s + t
                    par = t % 2
                    qb0 = 2 * par
                    P.wait("tensor", sw, w_ready)
                    P.wait("tensor", ss, EV[("xt", t)])
                    fr = EV.get(("qkvfree", par), (0, 0))
                    P.wait("tensor", ss, fr[0])
                    P.wait("tensor", sv, fr[1])
                    for d_ in range(NDC):
                        for nh in range(2):
                            P.emit("tensor",
                                   lambda e, t=t, d_=d_, nh=nh, qb0=qb0, ws=wslot:
                                   e.matmul(bank(qb0 + nh), xTv(t, d_),
                                            wcur[ws][:, D * d_ + 512 * nh:
                                                     D * d_ + 512 * (nh + 1)],
                                            start=(d_ == 0), stop=(d_ == NDC - 1)),
                                   [(st_, 1)] if (d_ == NDC - 1 and nh == 1)
                                   else ())
                    mm_done = st_.n
                    pso = ps[:, 512 * qb0:512 * qb0 + 1024]
                    if pas < 2 and DBG_A_LN > 0:
                        stage2a(t - 1)
                        ev_fn = flush_tr(pas, t - 2, pend_tr)
                        P.wait("scalar", st_, mm_done)
                        P.wait("scalar", sv, EV.get(("nscr_free", par), 0))
                        P.emit("scalar", lambda e, pso=pso, par=par: e.copy(
                            nscr[par][:], pso), [(ss, 1)])
                        EV[("qkvfree", par)] = (ss.n, 0)
                        P.wait("scalar", sv, EV.get(("sqfA_free", par), 0))
                        P.emit("scalar", lambda e, par=par: e.activation(
                            sqf2[par][:], nscr[par][:], AF.Square), [(ss, 1)])
                        sq_done = ss.n
                        P.wait("vector", ss, sq_done - 1)
                        P.emit("vector", lambda e, par=par: e.reduce_sum(
                            st16p[par]["sum"][:],
                            nscr[par].ap().rearrange("p (h c) -> p h c", c=HD),
                            axis=AX.X), [(sv, 1)])
                        P.wait("vector", ss, sq_done)
                        P.emit("vector", lambda e, par=par: e.reduce_sum(
                            st16p[par]["sq"][:],
                            sqf2[par].ap().rearrange("p (h c) -> p h c", c=HD),
                            axis=AX.X), [(sv, 1)])
                        P.emit("vector", lambda e, par=par: e.scalar_tensor_tensor(
                            st16p[par]["mean"][:], st16p[par]["sum"][:], -1.0 / HD,
                            st16p[par]["sum"][:], ALU.mult, ALU.mult), [(sv, 1)])
                        P.emit("vector", lambda e, par=par: e.tensor_tensor(
                            st16p[par]["var"][:], st16p[par]["mean"][:],
                            st16p[par]["sq"][:], ALU.add), [(sv, 1)])
                        EV[("var", t)] = sv.n
                        if ev_fn is not None:
                            ev_fn()
                        stage2b(t - 1, pend_tr)
                    elif pas == 2:
                        P.wait("scalar", st_, mm_done)
                        P.emit("scalar", lambda e, g=g, pso=pso: e.copy(
                            vsb[:, 1040 * g:1040 * (g + 1)]
                            .rearrange("p (h c) -> p h c", c=65)[:, :, 0:64],
                            pso.rearrange("p (h c) -> p h c", c=HD)), [(ss, 1)])
                        EV[("qkvfree", par)] = (ss.n, sv.n)

                EV[("wfree", wslot)] = st_.n
                if pas < 2:
                    stage2a(NTC - 1)
                    stage2b(NTC - 1, pend_tr)
                    for tt_ in (NTC - 2, NTC - 1):
                        ev_fn = flush_tr(pas, tt_, pend_tr)
                        if ev_fn is not None:
                            ev_fn()

            if "b" not in DBG_PHASES:
                continue
            wo_ready = wrdy[3]
            P.wait("tensor", sgp, 192)

            for qt in range(NQT):
                Q = NQT * s + qt
                jmax = 4 * (Q + 1)
                P.wait("tensor", ss)
                P.wait("tensor", sv)
                exp_at = {}
                pv_red_at = 0
                ao_start_guard = sv.n

                def kw(bi):
                    if not KEEP_WARM:
                        return
                    P.emit("tensor", lambda e, bi=bi: e.matmul(
                        ps[0:128, 512 * bi:512 * (bi + 1)], ones_b[0:1, :],
                        tri_b[0:1, 0:512], start=True, stop=True))

                def flush_norm(i):
                    pvb = (4, 6, 7)[i % 3]
                    P.wait("tensor", ss, EV.get(("rr", i), 0))
                    P.wait("tensor", sv, EV.get("bc_evac", 0))
                    kw(5)
                    hb = QTL * (i % 2)
                    P.emit("tensor", lambda e, hb=hb: e.matmul(
                        ps[0:64, 512 * 5:512 * 6], ones_b[0:1, 0:64],
                        rrow_b[0:1, hb:hb + QTL],
                        start=True, stop=True), [(st_, 1)])
                    P.wait("vector", st_)
                    P.emit("vector", lambda e, hb=hb: e.tensor_copy(
                        bc_sb[:, hb:hb + QTL], ps[0:64, 512 * 5:512 * 6]),
                        [(sv, 1)])
                    EV["bc_evac"] = sv.n
                    P.emit("vector", lambda e, i=i, pvb=pvb, hb=hb:
                           e.tensor_tensor(
                               aot[0:128, QTL * (i // 2):QTL * (i // 2 + 1)]
                               [64 * (i % 2):64 * (i % 2) + 64, :],
                               ps[0:64, 512 * pvb:512 * pvb + QTL],
                               bc_sb[:, hb:hb + QTL], ALU.mult), [(sv, 1)])
                    EV[("aoTT", i)] = sv.n

                def emit_pv(i, js):
                    pvb = (4, 6, 7)[i % 3]
                    for j in js:
                        if j == 0:
                            P.wait("tensor", sv, EV.get(("aoTT", i - 3), 0))
                            kw(pvb)
                        P.wait("tensor", ss, exp_at.get((i, j // 2), 0))
                        if j >= 4 * Q:
                            P.wait("tensor", sv,
                                   EV.get(("mask", i, j - 4 * Q), 0))
                        P.emit("tensor", lambda e, j=j, i=i, pvb=pvb: e.matmul(
                            ps[0:65, 512 * pvb:512 * pvb + QTL],
                            vsb[:, 1040 * j + 65 * i:1040 * j + 65 * (i + 1)],
                            scrab[:, 512 * j:512 * (j + 1)],
                            start=(j == 0), stop=(j == jmax - 1)),
                            [(st_, 1)] if j == js[-1] else ())
                    if js and js[-1] == jmax - 1:
                        EV[("pvdone", i)] = st_.n

                exp_at = {}
                for i in range(DBG_B_NH + 1):
                    if i >= 3:
                        flush_norm(i - 3)
                    if i < DBG_B_NH:
                        hc, hp = i // 2, 64 * (i % 2)
                        npair = (jmax + 1) // 2
                        for jp in range(npair):
                            if jp % 2 == 0 and i >= 1:
                                q4 = jp // 2
                                emit_pv(i - 1, list(range(4 * q4,
                                                          min(4 * q4 + 4,
                                                              jmax))))
                            js = [2 * jp] + ([2 * jp + 1]
                                             if 2 * jp + 1 < jmax else [])
                            pb0 = (jp % 2) * 2
                            P.wait("tensor", ss, EV.get(("exp_bank", pb0), 0))
                            if i == 0 and jp == 0:
                                P.wait("tensor", ss)
                                P.wait("tensor", sv)
                            kw(pb0)
                            for idx, j in enumerate(js):
                                P.emit("tensor",
                                       lambda e, j=j, idx=idx, pb0=pb0, hc=hc,
                                       hp=hp, qt=qt: e.matmul(
                                           bank(pb0 + idx),
                                           ktall[hp:hp + HD,
                                                 T * hc + 128 * j:
                                                 T * hc + 128 * (j + 1)],
                                           qtseg[hp:hp + HD,
                                                 SEG * hc + QTL * qt:
                                                 SEG * hc + QTL * (qt + 1)],
                                           start=True, stop=True),
                                       [(st_, 1)] if idx == len(js) - 1 else ())
                            P.wait("scalar", st_)
                            ncols = 512 * len(js)
                            P.emit("scalar",
                                   lambda e, jp=jp, pb0=pb0, ncols=ncols:
                                   e.activation(
                                       scrab[:, 1024 * jp:1024 * jp + ncols],
                                       ps[:, 512 * pb0:512 * pb0 + ncols],
                                       AF.Exp, scale=0.125), [(ss, 1)])
                            exp_at[(i, jp)] = ss.n
                            EV[("exp_bank", pb0)] = ss.n
                        EV[("expdone", i)] = ss.n
                        P.wait("vector", ss, EV[("expdone", i)])
                        moff = [0, 128, 384, 768]
                        for di in range(4):
                            j = 4 * Q + di
                            w = 128 * (di + 1)
                            P.emit("vector", lambda e, j=j, di=di, w=w:
                                   e.tensor_tensor(
                                       scrab[:, 512 * j:512 * j + w],
                                       scrab[:, 512 * j:512 * j + w],
                                       tri_b[:, moff[di]:moff[di] + w],
                                       ALU.mult), [(sv, 1)])
                            EV[("mask", i, di)] = sv.n
                        EV[("masks", i)] = sv.n
                    else:
                        emit_pv(i - 1, list(range(jmax)))
                    if i >= 1 and ("rr", i - 1) not in EV:
                        pvb1 = (4, 6, 7)[(i - 1) % 3]
                        P.wait("scalar", st_, EV[("pvdone", i - 1)])
                        hb1 = QTL * ((i - 1) % 2)
                        P.emit("scalar", lambda e, pvb1=pvb1: e.activation(
                            rrow_f[0:1, :],
                            ps[64:65, 512 * pvb1:512 * pvb1 + QTL],
                            AF.Ln), [(ss, 1)])
                        P.emit("scalar", lambda e, hb1=hb1: e.activation(
                            rrow_b[0:1, hb1:hb1 + QTL],
                            rrow_f[0:1, :], AF.Exp, scale=-1.0), [(ss, 1)])
                        EV[("rr", i - 1)] = ss.n
                flush_norm(DBG_B_NH - 2)
                flush_norm(DBG_B_NH - 1)
                for i in range(DBG_B_NH + 1):
                    EV.pop(("rr", i), None)
                    EV.pop(("masks", i), None)
                    EV.pop(("expdone", i), None)
                ao_done = sv.n

                for tc in range(4 if DBG_B_LVL >= 6 else 0):
                    t = 4 * qt + tc
                    g = NTC * s + t
                    par = t % 2
                    P.wait("sync", sv, EV.get(("xfree", par), 0))
                    P.wait("sync", sop[par], EV.get(("ofree", par), 0))
                    P.emit("sync", lambda e, b=b, g=g, par=par: e.dma_start(
                        out=xf[par][:], in_=dx[b, 128 * g:128 * (g + 1), :]),
                        [(sxp[par], 16)])
                    xr_ld = sxp[par].n
                    P.wait("tensor", sv, ao_done)
                    P.wait("tensor", sw, wo_ready)
                    P.wait("tensor", sv, EV.get("oev", 0))
                    for c in range(NDC):
                        for nh in range(2):
                            P.emit("tensor", lambda e, c=c, nh=nh, tc=tc:
                                   e.matmul(bank(6 + nh),
                                            aot[:, QTL * c + 128 * tc:
                                                QTL * c + 128 * (tc + 1)],
                                            wcur[1][:, D * c + 512 * nh:
                                                    D * c + 512 * (nh + 1)],
                                            start=(c == 0), stop=False))
                    for nh in range(2):
                        P.emit("tensor", lambda e, nh=nh: e.matmul(
                            bank(6 + nh), ones_b[0:1, :],
                            bo_row_b[0:1, 512 * nh:512 * (nh + 1)],
                            start=False, stop=True),
                            [(st_, 1)] if nh == 1 else ())
                    o_done = st_.n
                    P.wait("vector", st_, o_done)
                    P.wait("vector", sxp[par], xr_ld)
                    P.emit("vector", lambda e, par=par: e.scalar_tensor_tensor(
                        nscr[par][:], ps[:, 512 * 6:512 * 8], 1.0, xf[par][:],
                        ALU.mult, ALU.add, accum_out=st1["sum"][:, 0:1]),
                        [(sv, 1)])
                    EV[("xfree", par)] = sv.n
                    EV["oev"] = sv.n
                    P.emit("vector", lambda e, par=par: e.scalar_tensor_tensor(
                        sqf[:], nscr[par][:], 1.0, nscr[par][:],
                        ALU.mult, ALU.mult, accum_out=st1["sq"][:, 0:1]),
                        [(sv, 1)])
                    P.emit("vector", lambda e: e.tensor_scalar_mul(
                        st1["mean"][:], st1["sum"][:], 1.0 / D), [(sv, 1)])
                    P.emit("vector", lambda e: e.tensor_tensor(
                        st1["var"][:], st1["mean"][:], st1["mean"][:], ALU.mult),
                        [(sv, 1)])
                    P.emit("vector", lambda e: e.scalar_tensor_tensor(
                        st1["var"][:], st1["sq"][:], 1.0 / D, st1["var"][:],
                        ALU.mult, ALU.subtract), [(sv, 1)])
                    P.wait("scalar", sv)
                    P.emit("scalar", lambda e: e.activation(
                        st1["sq"][:], st1["var"][:], AF.Ln,
                        bias=eps_col[:, 0:1]), [(ss, 1)])
                    P.emit("scalar", lambda e: e.activation(
                        st1["rstd"][:], st1["sq"][:], AF.Exp, scale=-0.5),
                        [(ss, 1)])
                    P.wait("vector", ss)
                    P.emit("vector", lambda e, t=t, par=par: e.tensor_scalar(
                        h1f[:, D * t:D * (t + 1)], nscr[par][:],
                        st1["mean"][:, 0:1], st1["rstd"][:, 0:1],
                        ALU.subtract, ALU.mult), [(sv, 1)])
                    EV[("h1", t)] = sv.n

            EV["ps47_dve"] = sems["sv"].n
            EV[("wfree", 1)] = st_.n
            EV["aot_rd"] = st_.n

            if "c" not in DBG_PHASES:
                continue

            def unit_srcs(u):
                if u < NE:
                    return (cgw[u], cuw[u], cdw[u], True)
                qd = u - NE
                return (csg[qd], csu[qd], csd[qd], False)

            def load_unit_gu(u):
                gsrc, usrc, dsrc, _ = unit_srcs(u)
                slot = u % 2
                P.wait("sync", sgp, cast_done["uw" if u < NE else "su"])
                P.emit("sync", lambda e, gsrc=gsrc, slot=slot: e.dma_start(
                    out=gub[slot][:, 0:4096], in_=gsrc), [(sgu[slot], 16)])
                P.emit("sync", lambda e, usrc=usrc, slot=slot: e.dma_start(
                    out=gub[slot][:, 4096:8192], in_=usrc), [(sgu[slot], 16)])
                return sgu[slot].n

            def load_unit_dn(u):
                gsrc, usrc, dsrc, _ = unit_srcs(u)
                P.wait("sync", sgp, cast_done["dw" if u < NE else "sd"])
                P.emit("sync", lambda e, dsrc=dsrc: e.dma_start(
                    out=down_sb.ap()[:, :], in_=dsrc), [(sdn, 16)])
                return sdn.n

            pend_c = {}

            def flush_c(t):
                if t not in pend_c:
                    return
                par, cast_done_c = pend_c.pop(t)
                P.wait("tensor", sv, cast_done_c)
                tr_evac(bb[par],
                        wcur[0].ap().rearrange("p (d c) -> p d c", c=SEG)[
                            :, :, 128 * t:128 * (t + 1)], par, "trev")
                EV[("h1t", t)] = ss.n

            for t in range(NTC):
                par = t % 2
                P.wait("vector", sv, EV[("h1", t)])
                P.wait("vector", st_, EV.get(("bb_rd", par), 0))
                P.wait("vector", st_, EV.get("aot_rd", 0))
                P.emit("vector", lambda e, t=t, par=par: e.tensor_copy(
                    bb[par][:], h1f[:, D * t:D * (t + 1)]), [(sv, 1)])
                pend_c[t] = (par, sv.n)
                flush_c(t - 1)
                if t < NTC - 1:
                    continue
                flush_c(t)
            P.wait("tensor", sgp, 192)
            P.wait("tensor", sv, EV.get("router_read", 0))
            for t in range(NTC):
                P.wait("tensor", ss, EV[("h1t", t)])
                for d_ in range(NDC):
                    P.emit("tensor", lambda e, t=t, d_=d_: e.matmul(
                        ps[:, NE * t:NE * (t + 1)],
                        h1T[:, SEG * d_ + 128 * t:SEG * d_ + 128 * (t + 1)],
                        rw_sb[:, NE * d_:NE * (d_ + 1)],
                        start=(d_ == 0), stop=(d_ == NDC - 1)),
                        [(st_, 1)] if d_ == NDC - 1 else ())
            rt_done = st_.n

            def r3(t_):
                return t_.ap().rearrange("p (t e) -> p t e", e=NE)

            def c3(t_):
                return t_.ap().rearrange("p (t o) -> p t o", o=1).to_broadcast(
                    [128, NTC, NE])

            P.wait("vector", st_, rt_done)
            P.emit("vector", lambda e: e.tensor_copy(
                r8["logit"][:], bank(0, NTC * NE)), [(sv, 1)])
            EV["router_read"] = sv.n
            P.emit("vector", lambda e: e.reduce_max(
                rcol["nmax"][:], r3(r8["logit"]), axis=AX.X, negate=True),
                [(sv, 1)])
            P.emit("vector", lambda e: e.tensor_tensor(
                r3(r8["logit"]), r3(r8["logit"]), c3(rcol["nmax"]), ALU.add),
                [(sv, 1)])
            P.wait("scalar", sv)
            P.emit("scalar", lambda e: e.activation(
                r8["score"][:], r8["logit"][:], AF.Exp), [(ss, 1)])
            P.wait("vector", ss)
            P.emit("vector", lambda e: e.reduce_sum(
                rcol["den"][:], r3(r8["score"]), axis=AX.X), [(sv, 1)])
            P.emit("vector", lambda e: e.reciprocal(
                rcol["rden"][:], rcol["den"][:]), [(sv, 1)])
            P.emit("vector", lambda e: e.tensor_tensor(
                r3(r8["score"]), r3(r8["score"]), c3(rcol["rden"]), ALU.mult),
                [(sv, 1)])
            P.emit("vector", lambda e: e.tensor_scalar_add(
                r8["corr"][:], r8["score"][:], rbias_col[:, 0:1]), [(sv, 1)])
            for t in range(NTC):
                P.emit("vector", lambda e, t=t: e.max(
                    out=m8w[:, NE * t:NE * (t + 1)],
                    in_=r8["corr"][:, NE * t:NE * (t + 1)]), [(sv, 1)],
                    nodrain=(t > 0))
            thr3 = m8w.ap().rearrange("p (t e) -> p t e", e=NE)[
                :, :, 3:4].to_broadcast([128, NTC, NE])
            P.emit("vector", lambda e, thr3=thr3: e.tensor_tensor(
                r3(r8["sel"]), r3(r8["corr"]), thr3, ALU.is_ge), [(sv, 1)])
            P.emit("vector", lambda e: e.tensor_mul(
                r8["msk"][:], r8["sel"][:], r8["score"][:]), [(sv, 1)])
            P.emit("vector", lambda e: e.reduce_sum(
                rcol["den4"][:], r3(r8["msk"]), axis=AX.X), [(sv, 1)])
            P.emit("vector", lambda e: e.tensor_scalar_add(
                rcol["den4"][:], rcol["den4"][:], 1e-20), [(sv, 1)])
            P.emit("vector", lambda e: e.reciprocal(
                rcol["rden4"][:], rcol["den4"][:]), [(sv, 1)])
            P.emit("vector", lambda e: e.tensor_tensor(
                w_seg.ap().rearrange("p (t e) -> p t e", e=NE),
                r3(r8["msk"]), c3(rcol["rden4"]), ALU.mult), [(sv, 1)])
            for t in range(NTC):
                EV[("w", t)] = sv.n

            P.wait("sync", st_)
            u_gu = {0: load_unit_gu(0)}
            u_dn = {0: load_unit_dn(0)}
            EV["pipeline_prev"] = None

            def emit_down(u, tq, act_done, routed):
                tc0, tc1 = 2 * tq, 2 * tq + 1
                apar = 1024 * (tq % 2)
                P.wait("tensor", sv, act_done)
                P.wait("tensor", sdn, u_dn[u])
                P.wait("tensor", sv, EV.get("dadd_last", 0))
                for tci in range(2):
                    for m in range(4):
                        for nh in range(2):
                            P.emit("tensor",
                                   lambda e, tci=tci, m=m, nh=nh, apar=apar:
                                   e.matmul(
                                       bank(4 + 2 * tci + nh),
                                       actw[:, apar + 256 * m + 128 * tci:
                                            apar + 256 * m + 128 * tci + 128],
                                       down_sb[:, D * m + 512 * nh:
                                               D * m + 512 * (nh + 1)],
                                       start=(m == 0), stop=(m == 3)),
                                   [(st_, 1)]
                                   if (m == 3 and nh == 1 and tci == 1)
                                   else ())
                dn_done = st_.n
                EV[("act_read_t", tq % 2)] = dn_done
                P.wait("vector", st_, dn_done)
                for tci, tcg in enumerate((tc0, tc1)):
                    pd = 512 * (4 + 2 * tci)
                    if routed:
                        P.emit("vector", lambda e, tcg=tcg, pd=pd, u=u:
                               e.scalar_tensor_tensor(
                                   h1f[:, D * tcg:D * (tcg + 1)],
                                   ps[:, pd:pd + 1024],
                                   w_seg[:, NE * tcg + u:NE * tcg + u + 1],
                                   h1f[:, D * tcg:D * (tcg + 1)],
                                   ALU.mult, ALU.add), [(sv, 1)])
                    else:
                        P.emit("vector", lambda e, tcg=tcg, pd=pd: e.tensor_add(
                            h1f[:, D * tcg:D * (tcg + 1)],
                            h1f[:, D * tcg:D * (tcg + 1)],
                            ps[:, pd:pd + 1024]), [(sv, 1)])
                EV["dadd_last"] = sv.n
                EV[("dadd", (u, tq))] = sv.n
            for u in range(NU):
                slot = u % 2
                routed = u < NE
                for tq in range(NTQ):
                    tc0, tc1 = 2 * tq, 2 * tq + 1
                    P.wait("tensor", sgu[u % 2], u_gu[u])
                    P.wait("tensor", sv, EV.get("router_read", 0))
                    P.wait("tensor", ss, EV.get("guev_s", 0))
                    P.wait("tensor", sv, EV.get("guev_v", 0))
                    for mat in range(2):
                        for m in range(4):
                            for d_ in range(NDC):
                                P.emit("tensor",
                                       lambda e, mat=mat, m=m, d_=d_, tq=tq,
                                       slot=slot: e.matmul(
                                           ps[:, 1024 * mat + 256 * m:
                                              1024 * mat + 256 * (m + 1)],
                                           gub[slot][:, 4096 * mat + IE * d_ +
                                                     128 * m:
                                                     4096 * mat + IE * d_ +
                                                     128 * (m + 1)],
                                           h1T[:, SEG * d_ + TOKQ * tq:
                                               SEG * d_ + TOKQ * (tq + 1)],
                                           start=(d_ == 0),
                                           stop=(d_ == NDC - 1)),
                                       [(st_, 1)]
                                       if (d_ == NDC - 1 and m == 3) else ())
                    gu_done = st_.n
                    if tq == 0 and u + 1 < NU:
                        P.wait("sync", st_)
                        u_gu[u + 1] = load_unit_gu(u + 1)
                    P.wait("scalar", st_, gu_done)
                    P.wait("scalar", sv, EV.get("act_read_v", 0))
                    P.emit("scalar", lambda e: e.activation(
                        silu_sb[:, 0:1024], ps[:, 0:1024], AF.Silu), [(ss, 1)])
                    EV["guev_s"] = ss.n
                    P.wait("vector", st_, gu_done)
                    P.wait("vector", ss)
                    P.emit("vector", lambda e: e.tensor_copy(
                        silu_sb[:, 1024:2048], ps[:, 1024:2048]), [(sv, 1)])
                    EV["guev_v"] = sv.n
                    P.wait("vector", st_, EV.get(("act_read_t", tq % 2), 0))
                    apar = 1024 * (tq % 2)
                    P.emit("vector", lambda e, apar=apar: e.tensor_tensor(
                        actw[:, apar:apar + 1024], silu_sb[:, 0:1024],
                        silu_sb[:, 1024:2048], ALU.mult), [(sv, 1)])
                    act_done = sv.n
                    EV["act_read_v"] = act_done
                    prev = EV.get("pipeline_prev")
                    EV["pipeline_prev"] = (u, tq, act_done, routed)
                    if prev is not None:
                        emit_down(*prev)
                        if prev[1] == NTQ - 1 and u not in u_dn:
                            P.wait("sync", st_)
                            u_dn[u] = load_unit_dn(u)


            if EV.get("pipeline_prev") is not None:
                emit_down(*EV["pipeline_prev"])
                EV["pipeline_prev"] = None

            EV["ps47_dve"] = sems["sv"].n
            EV[("wfree", 0)] = st_.n
            for t in range(NTC):
                par = t % 2
                P.wait("vector", sv, EV[("dadd", (NU - 1, t // 2))])
                P.emit("vector", lambda e, t=t: e.scalar_tensor_tensor(
                    sqf[:], h1f[:, D * t:D * (t + 1)], 1.0,
                    h1f[:, D * t:D * (t + 1)], ALU.mult, ALU.mult,
                    accum_out=st1["sq"][:, 0:1]), [(sv, 1)])
                P.emit("vector", lambda e, t=t: e.reduce_sum(
                    st1["sum"][:], h1f[:, D * t:D * (t + 1)], axis=AX.X),
                    [(sv, 1)])
                P.emit("vector", lambda e: e.tensor_scalar_mul(
                    st1["mean"][:], st1["sum"][:], 1.0 / D), [(sv, 1)])
                P.emit("vector", lambda e: e.tensor_tensor(
                    st1["var"][:], st1["mean"][:], st1["mean"][:], ALU.mult),
                    [(sv, 1)])
                P.emit("vector", lambda e: e.scalar_tensor_tensor(
                    st1["var"][:], st1["sq"][:], 1.0 / D, st1["var"][:],
                    ALU.mult, ALU.subtract), [(sv, 1)])
                P.wait("scalar", sv)
                P.emit("scalar", lambda e: e.activation(
                    st1["sq"][:], st1["var"][:], AF.Ln, bias=eps_col[:, 0:1]),
                    [(ss, 1)])
                P.emit("scalar", lambda e: e.activation(
                    st1["rstd"][:], st1["sq"][:], AF.Exp, scale=-0.5), [(ss, 1)])
                P.wait("vector", ss)
                P.wait("vector", sop[par], EV.get(("ofree", par), 0))
                P.emit("vector", lambda e, t=t, par=par: e.tensor_scalar(
                    xf[par][:], h1f[:, D * t:D * (t + 1)],
                    st1["mean"][:, 0:1], st1["rstd"][:, 0:1],
                    ALU.subtract, ALU.mult), [(sv, 1)])
                apply_done = sv.n
                P.wait("sync", sv, apply_done)
                g = NTC * s + t
                P.emit("sync", lambda e, b=b, g=g, par=par: e.dma_start(
                    out=dout[b, 128 * g:128 * (g + 1), :], in_=xf[par][:]),
                    [(sop[par], 16)])
                EV[("ofree", par)] = sop[par].n

    P.wait("sync", sop[0])
    P.wait("sync", sop[1])

    with nc.Block() as block:
        @block.sync
        def _(e):
            P.run("sync", e)

        @block.gpsimd
        def _(e):
            P.run("gpsimd", e)

        @block.tensor
        def _(e):
            P.run("tensor", e)

        @block.scalar
        def _(e):
            P.run("scalar", e)

        @block.vector
        def _(e):
            P.run("vector", e)

    es.close()
    return nc


_NC_CACHE = None


def prep_common(L0_attn_Wk, L0_attn_Wq, L0_attn_Wv, L0_attn_Wo, L0_attn_bo,
                L0_attn_kln_w, L0_attn_kln_b, L0_attn_qln_w, L0_attn_qln_b,
                L0_ffn_router_w, L0_ffn_router_bias, L0_ffn_gate_w,
                L0_ffn_up_w, L0_ffn_down_w, L0_ffn_s_gate_w, L0_ffn_s_up_w,
                L0_ffn_s_down_w):
    f32 = lambda a: np.ascontiguousarray(np.asarray(a, dtype=np.float32))
    def lay2(w):
        w = np.asarray(w, np.float32)
        return np.ascontiguousarray(
            w.reshape(-1, 128, w.shape[-1]).transpose(1, 0, 2))

    def lay3(w):
        w = np.asarray(w, np.float32)
        return np.ascontiguousarray(
            w.reshape(w.shape[0], -1, 128, w.shape[-1]).transpose(0, 2, 1, 3))

    sg_q = np.stack([f32(L0_ffn_s_gate_w)[:, 512 * q:512 * (q + 1)]
                     for q in range(4)])
    su_q = np.stack([f32(L0_ffn_s_up_w)[:, 512 * q:512 * (q + 1)]
                     for q in range(4)])
    sd_q = np.stack([f32(L0_ffn_s_down_w)[512 * q:512 * (q + 1), :]
                     for q in range(4)])
    common = dict(
        wq=lay2(L0_attn_Wq), wk=lay2(L0_attn_Wk), wv=lay2(L0_attn_Wv),
        wo=lay2(L0_attn_Wo), bo=f32(L0_attn_bo).reshape(1, D),
        klw=f32(L0_attn_kln_w).reshape(1, HD),
        klb=f32(L0_attn_kln_b).reshape(1, HD),
        qlw=f32(L0_attn_qln_w).reshape(1, HD),
        qlb=f32(L0_attn_qln_b).reshape(1, HD),
        rw=lay2(L0_ffn_router_w), rb=f32(L0_ffn_router_bias).reshape(1, 1),
        gw=lay3(L0_ffn_gate_w), uw=lay3(L0_ffn_up_w), dw=lay3(L0_ffn_down_w),
        sg=lay3(sg_q), su=lay3(su_q), sd=lay3(sd_q),
        ident=np.eye(128, dtype=ml_dtypes.bfloat16),
        tri=_diag_masks(),
    )
    return common


def kernel(L0_attn_x, L0_attn_Wk, L0_attn_Wq, L0_attn_Wv, L0_attn_Wo,
           L0_attn_bo, L0_attn_kln_w, L0_attn_kln_b, L0_attn_qln_w,
           L0_attn_qln_b, L0_ffn_router_w, L0_ffn_router_bias, L0_ffn_gate_w,
           L0_ffn_up_w, L0_ffn_down_w, L0_ffn_s_gate_w, L0_ffn_s_up_w,
           L0_ffn_s_down_w):
    global _NC_CACHE
    x = np.ascontiguousarray(np.asarray(L0_attn_x, dtype=np.float32))
    common = prep_common(L0_attn_Wk, L0_attn_Wq, L0_attn_Wv, L0_attn_Wo,
                         L0_attn_bo, L0_attn_kln_w, L0_attn_kln_b,
                         L0_attn_qln_w, L0_attn_qln_b, L0_ffn_router_w,
                         L0_ffn_router_bias, L0_ffn_gate_w, L0_ffn_up_w,
                         L0_ffn_down_w, L0_ffn_s_gate_w, L0_ffn_s_up_w,
                         L0_ffn_s_down_w)
    in_maps = []
    for c in range(NCORES):
        m = dict(common)
        m["x"] = np.ascontiguousarray(x[NB * c:NB * (c + 1)])
        in_maps.append(m)
    if _NC_CACHE is None:
        _NC_CACHE = build_core()
    res = run_bass_kernel_spmd(_NC_CACHE, in_maps, core_ids=list(range(NCORES)))
    return np.concatenate([np.asarray(r["out"]) for r in res.results], axis=0)


def _diag_masks():
    parts = []
    for i in range(4):
        w = 128 * (i + 1)
        q = np.arange(w)[None, :]
        k = np.arange(128)[:, None]
        parts.append((q >= 128 * i + k).astype(np.float32))
    return np.concatenate(parts, axis=1).astype(ml_dtypes.bfloat16)


if __name__ == "__main__":
    nc = build_core()
    print("built ok; instructions:", nc.n_instructions()
          if hasattr(nc, "n_instructions") else "?")


# version 21
# speedup vs baseline: 1.1283x; 1.0283x over previous
import contextlib
import numpy as np
import ml_dtypes

import concourse.bass as bass
import concourse.mybir as mybir
from concourse.bass_utils import run_bass_kernel_spmd

F32 = mybir.dt.float32
BF16 = mybir.dt.bfloat16
AF = mybir.ActivationFunctionType
ALU = mybir.AluOpType
AX = mybir.AxisListType

NCORES = 8
NB = 4
T = 2048
D = 1024
H = 16
HD = 64
NDC = 8
SEG = 1024
NSEG = 2
NTC = 8
QTL = 512
NQT = 2
TOKQ = 256
NTQ = 4
NE = 8
IE = 512
NU = 12
DBG_NB = None
DBG_NSEG = None
DBG_PHASES = "abc"
DBG_A_PASSES = 3
DBG_A_LN = 9
DBG_B_NH = 16
DBG_B_LVL = 9
DBG_PV_BANK = 4
DBG_PRO_BCAST = 1
EPS = 1e-5

ENGINES = ("tensor", "scalar", "vector", "sync", "gpsimd")
SAFE_SYNC = False
SH_A = 4
KEEP_WARM = False


class Sem:
    def __init__(self, handle, name):
        self.h = handle
        self.name = name
        self.n = 0


class Plan:

    def __init__(self):
        self.items = {e: [] for e in ENGINES}

    def emit(self, eng, fn, incs=(), nodrain=False):
        incs = tuple(incs)
        if (eng == "vector" and not nodrain) or (eng == "scalar" and SAFE_SYNC):
            inner = fn
            sep = "drain"
            if eng == "scalar":
                sep = "drain"

            def fn(v, _inner=inner, sep=sep):
                getattr(v, sep)()
                return _inner(v)
        self.items[eng].append(("op", fn, incs))
        for sem, k in incs:
            sem.n += k

    def wait(self, eng, sem, val=None):
        v = sem.n if val is None else val
        if v > 0:
            self.items[eng].append(("wait", sem, v))

    def run(self, eng_name, eng):
        for it in self.items[eng_name]:
            if it[0] == "op":
                _, fn, incs = it
                ins = fn(eng)
                for sem, k in incs:
                    ins.then_inc(sem.h, k)
            else:
                _, sem, v = it
                eng.wait_ge(sem.h, v)


def build_core():
    nc = bass.Bass("TRN2", target_bir_lowering=False, debug=False)

    dx = nc.dram_tensor("x", [NB, T, D], F32, kind="ExternalInput").ap()
    dwq = nc.dram_tensor("wq", [128, NDC, D], F32, kind="ExternalInput").ap()
    dwk = nc.dram_tensor("wk", [128, NDC, D], F32, kind="ExternalInput").ap()
    dwv = nc.dram_tensor("wv", [128, NDC, D], F32, kind="ExternalInput").ap()
    dwo = nc.dram_tensor("wo", [128, NDC, D], F32, kind="ExternalInput").ap()
    dbo = nc.dram_tensor("bo", [1, D], F32, kind="ExternalInput").ap()
    dklw = nc.dram_tensor("klw", [1, HD], F32, kind="ExternalInput").ap()
    dklb = nc.dram_tensor("klb", [1, HD], F32, kind="ExternalInput").ap()
    dqlw = nc.dram_tensor("qlw", [1, HD], F32, kind="ExternalInput").ap()
    dqlb = nc.dram_tensor("qlb", [1, HD], F32, kind="ExternalInput").ap()
    drw = nc.dram_tensor("rw", [128, NDC, NE], F32, kind="ExternalInput").ap()
    drb = nc.dram_tensor("rb", [1, 1], F32, kind="ExternalInput").ap()
    dgw = nc.dram_tensor("gw", [NE, 128, NDC, IE], F32, kind="ExternalInput").ap()
    duw = nc.dram_tensor("uw", [NE, 128, NDC, IE], F32, kind="ExternalInput").ap()
    ddw = nc.dram_tensor("dw", [NE, 128, 4, D], F32, kind="ExternalInput").ap()
    dsg = nc.dram_tensor("sg", [4, 128, NDC, IE], F32, kind="ExternalInput").ap()
    dsu = nc.dram_tensor("su", [4, 128, NDC, IE], F32, kind="ExternalInput").ap()
    dsd = nc.dram_tensor("sd", [4, 128, 4, D], F32, kind="ExternalInput").ap()
    cwq = nc.dram_tensor("cwq", [128, NDC * D], BF16, kind="Internal").ap()
    cwk = nc.dram_tensor("cwk", [128, NDC * D], BF16, kind="Internal").ap()
    cwv = nc.dram_tensor("cwv", [128, NDC * D], BF16, kind="Internal").ap()
    cwo = nc.dram_tensor("cwo", [128, NDC * D], BF16, kind="Internal").ap()
    cgw = nc.dram_tensor("cgw", [NE, 128, NDC * IE], BF16, kind="Internal").ap()
    cuw = nc.dram_tensor("cuw", [NE, 128, NDC * IE], BF16, kind="Internal").ap()
    cdw = nc.dram_tensor("cdw", [NE, 128, 4 * D], BF16, kind="Internal").ap()
    csg = nc.dram_tensor("csg", [4, 128, NDC * IE], BF16, kind="Internal").ap()
    csu = nc.dram_tensor("csu", [4, 128, NDC * IE], BF16, kind="Internal").ap()
    csd = nc.dram_tensor("csd", [4, 128, 4 * D], BF16, kind="Internal").ap()
    dident = nc.dram_tensor("ident", [128, 128], BF16, kind="ExternalInput").ap()
    dtri = nc.dram_tensor("tri", [128, 1280], BF16, kind="ExternalInput").ap()
    dout = nc.dram_tensor("out", [NB, T, D], F32, kind="ExternalOutput").ap()

    es = contextlib.ExitStack()
    S = lambda n, sh, dt: es.enter_context(nc.sbuf_tensor(n, sh, dt))

    ident_b = S("ident_b", [128, 128], BF16)
    tri_b = S("tri_b", [128, 1280], BF16)
    ones_b = S("ones_b", [1, 128], BF16)
    ones_f = S("ones_f", [1, 128], F32)
    bo_row_b = S("bo_row_b", [1, D], BF16)
    lnrows = S("lnrows", [1, 4 * HD], F32)
    klw_r = lnrows.ap()[:, 0:HD]
    klb_r = lnrows.ap()[:, HD:2 * HD]
    qlw_r = lnrows.ap()[:, 2 * HD:3 * HD]
    qlb_r = lnrows.ap()[:, 3 * HD:4 * HD]
    rb11 = S("rb11", [1, 1], F32)
    rbias_col = S("rbias_col", [128, 1], F32)
    eps_col = S("eps_col", [128, 1], F32)
    qkcol = S("qkcol", [128, 4], F32)
    e64_col = S("e64_col", [128, 1], F32)
    ln8_col = S("ln8_col", [128, 1], F32)
    rw_sb = S("rw_sb", [128, NDC * NE], BF16)
    ktall = S("ktall", [128, NDC * T], BF16)
    vsb = S("vsb", [128, 16 * (H * 65)], BF16)
    h1f = S("h1f", [128, NTC * D], F32)
    xf = [S("xf0", [128, D], F32), S("xf1", [128, D], F32)]
    sqf2 = [S("sqf0", [128, D], F32), S("sqf1", [128, D], F32)]
    sqf = sqf2[0]
    st16p = [{k: S(f"st16_{k}{i}", [128, H], F32)
              for k in ("sum", "sq", "mean", "var", "rstd", "nmr")}
             for i in range(2)]
    st1 = {k: S("st1_" + k, [128, 1], F32)
           for k in ("sum", "sq", "mean", "var", "rstd")}
    w_seg = S("w_seg", [128, NTC * NE], F32)
    rrow_f = S("rrow_f", [1, QTL], F32)
    rrow_b = S("rrow_b", [1, 2 * QTL], BF16)
    bc_sb = S("bc_sb", [64, 2 * QTL], BF16)
    r8 = {k: S("r8_" + k, [128, NTC * NE], F32)
          for k in ("logit", "score", "sel", "msk")}
    r8["corr"] = w_seg
    m8w = r8["logit"]
    rcol = {k: S("rcol_" + k, [128, NTC], F32)
            for k in ("nmax", "den", "rden", "den4", "rden4")}

    wcur = [S("wcur0", [128, NDC * D], BF16), S("wcur1", [128, NDC * D], BF16)]
    scrab = S("scrab", [128, 8192], BF16)
    qtseg = S("qtseg", [128, NDC * SEG], BF16)
    aot = S("aot", [128, NDC * QTL], BF16)
    bb = [aot.ap()[:, 0:D], aot.ap()[:, D:2 * D]]
    nscr = [S("nscr0", [128, D], F32), S("nscr1", [128, D], F32)]

    h1T = wcur[0]
    gub = [qtseg, scrab]
    down_sb = aot
    silu_sb = nscr[0].ap().bitcast(BF16)
    actw = nscr[1].ap().bitcast(BF16)

    ps = es.enter_context(nc.psum_tensor("ps", [128, 4096], F32))

    def bank(b, w=512):
        return ps[:, 512 * b: 512 * b + w]

    def bank_b16(b, off, cols):
        return ps[:, 512 * b + off // 2: 512 * b + (off + cols) // 2].bitcast(BF16)

    sems = {}
    for name in ("scs", "sx0", "sx1", "sw", "sgu0", "sgu1", "sdn", "sgp",
                 "so0", "so1", "st", "ss", "sv"):
        sems[name] = Sem(es.enter_context(nc.semaphore(name)), name)
    scs = sems["scs"]
    sxp = [sems["sx0"], sems["sx1"]]
    sw = sems["sw"]
    sgu = [sems["sgu0"], sems["sgu1"]]
    sdn = sems["sdn"]
    sgp = sems["sgp"]
    sop = [sems["so0"], sems["so1"]]
    st_, ss, sv = sems["st"], sems["ss"], sems["sv"]

    P = Plan()
    EV = {}

    def xTv(t, d):
        return scrab[:, SEG * d + 128 * t: SEG * d + 128 * (t + 1)]

    def tr_part(src_bb, par, producer_key):
        trb = 4 + par
        P.wait("tensor", ss, EV.get((producer_key, par), 0))
        P.wait("tensor", sv, EV.get("ps47_dve", 0))
        for d_ in range(NDC):
            P.emit("tensor", lambda e, d_=d_, trb=trb, src_bb=src_bb:
                   e.transpose(bank_b16(trb, 128 * d_, 128),
                               src_bb[:, 128 * d_:128 * (d_ + 1)], ident_b[:]),
                   [(st_, 1)] if d_ == NDC - 1 else ())
        EV[("bb_rd", par)] = st_.n
        return st_.n, trb

    def ev_part(dst3, trb, tr_done, par, producer_key, scale=None, bias=None):
        P.wait("scalar", st_, tr_done)
        src3 = bank_b16(trb, 0, 1024).rearrange("p (d c) -> p d c", c=128)
        if scale is None:
            P.emit("scalar", lambda e, dst3=dst3, src3=src3:
                   e.copy(dst3, src3), [(ss, 1)])
        else:
            P.emit("scalar", lambda e, dst3=dst3, src3=src3, scale=scale,
                   bias=bias: e.activation(dst3, src3, AF.Identity,
                                           scale=scale, bias=bias), [(ss, 1)])
        EV[(producer_key, par)] = ss.n

    def tr_evac(src_bb, dst3, par, producer_key, scale=None, bias=None):
        tr_done, trb = tr_part(src_bb, par, producer_key)
        ev_part(dst3, trb, tr_done, par, producer_key, scale, bias)

    for ap_in, tile in [(dident, ident_b[:]), (dtri, tri_b[:]),
                        (dklw, klw_r), (dklb, klb_r), (dqlw, qlw_r),
                        (dqlb, qlb_r), (drb, rb11[:])]:
        P.emit("sync", lambda e, a=ap_in, t=tile: e.dma_start(out=t, in_=a),
               [(scs, 16)])
    for j, srow in enumerate([dqlw, dqlb, dklw, dklb]):
        for half in range(2):
            P.emit("sync", lambda e, srow=srow, j=j, half=half: e.dma_start(
                out=qkcol[64 * half:64 * (half + 1), j:j + 1], in_=srow),
                [(scs, 16)])
    P.wait("scalar", scs)
    P.emit("gpsimd", lambda e: e.dma_start(
        out=rw_sb.ap().rearrange("p (c e) -> p c e", e=NE), in_=drw),
        [(sgp, 16)])
    P.emit("gpsimd", lambda e: e.dma_start(out=bo_row_b[:], in_=dbo), [(sgp, 16)])
    cast_done = {}
    for nm, csrc, cdst in [("wq", dwq, cwq), ("wk", dwk, cwk), ("wv", dwv, cwv),
                           ("wo", dwo, cwo), ("gw", dgw, cgw), ("uw", duw, cuw),
                           ("dw", ddw, cdw), ("sg", dsg, csg), ("su", dsu, csu),
                           ("sd", dsd, csd)]:
        P.emit("gpsimd", lambda e, a=csrc, b=cdst: e.dma_start(
            out=b, in_=a.rearrange("... p c f -> ... p (c f)")
            if len(a.shape) == 4 else a.rearrange("p c f -> p (c f)")),
            [(sgp, 16)])
    for nm in ("wq", "wk", "wv", "wo", "gw", "uw", "dw", "sg", "su", "sd"):
        cast_done[nm] = sgp.n
    P.wait("vector", scs)
    P.wait("vector", sgp)
    P.emit("vector", lambda e: e.memset(ones_b[:], 1.0), [(sv, 1)])
    P.emit("vector", lambda e: e.memset(ones_f[:], 1.0), [(sv, 1)])
    P.emit("vector", lambda e: e.memset(eps_col[:], EPS), [(sv, 1)])
    P.emit("vector", lambda e: e.memset(e64_col[:], HD * EPS), [(sv, 1)])
    P.emit("vector", lambda e: e.memset(ln8_col[:], 2.0794415416798357), [(sv, 1)])
    P.emit("vector", lambda e: e.memset(
        vsb.ap().rearrange("p (g h c) -> p g h c", h=H, c=65)[:, :, :, 64:65],
        1.0), [(sv, 1)])
    P.wait("tensor", sv)
    for i, (dst, srow, cols) in enumerate([(rbias_col, rb11, 1)]):
        if DBG_PRO_BCAST:
            P.emit("tensor", lambda e, srow=srow, cols=cols, i=i:
                   e.matmul(bank(i, cols), ones_f[0:1, :], srow[0:1, :],
                            start=True, stop=True), [(st_, 1)])
            P.wait("vector", st_)
            P.emit("vector", lambda e, dst=dst, cols=cols, i=i:
                   e.tensor_copy(dst[:, 0:cols], bank(i, cols)), [(sv, 1)])
        else:
            P.emit("vector", lambda e, dst=dst, cols=cols:
                   e.memset(dst[:, 0:cols], 0.5), [(sv, 1)])

    for b in range(NB if DBG_NB is None else DBG_NB):
        for s in range(NSEG if DBG_NSEG is None else DBG_NSEG):

            def flush_tr(pas, t, pend_tr):
                if t not in pend_tr:
                    return None
                par, g, ttb_done, app_done = pend_tr.pop(t)
                P.wait("tensor", sv, ttb_done)
                P.wait("tensor", ss, app_done)
                tr_done, trb = tr_part(bb[par], par, "trev")
                if pas == 0:
                    dst3 = qtseg.ap().rearrange("p (d c) -> p d c", c=SEG)[
                        :, :, 128 * t:128 * (t + 1)]
                    sc, bi = qkcol[:, 0:1], qkcol[:, 1:2]
                else:
                    dst3 = ktall.ap().rearrange("p (d c) -> p d c", c=T)[
                        :, :, 128 * g:128 * (g + 1)]
                    sc, bi = qkcol[:, 2:3], qkcol[:, 3:4]
                return lambda: ev_part(dst3, trb, tr_done, par, "trev", sc, bi)

            for t in range(NTC):
                g = NTC * s + t
                par = t % 2
                P.wait("sync", sv, EV.get(("xfree", par), 0))
                P.wait("sync", sop[par], EV.get(("ofree", par), 0))
                P.emit("sync", lambda e, b=b, g=g, par=par: e.dma_start(
                    out=xf[par][:], in_=dx[b, 128 * g:128 * (g + 1), :]),
                    [(sxp[par], 16)])
                x_ld = sxp[par].n
                P.wait("vector", sxp[par], x_ld)
                P.wait("vector", st_, EV.get(("bb_rd", par), 0))
                P.wait("vector", st_, EV.get("aot_rd", 0))
                P.emit("vector", lambda e, par=par: e.tensor_copy(
                    bb[par][:], xf[par][:]), [(sv, 1)])
                EV[("xfree", par)] = sv.n
                P.wait("tensor", sv)
                tr_evac(bb[par],
                        scrab.ap().rearrange("p (d c) -> p d c", c=SEG)[
                            :, :, 128 * t:128 * (t + 1)], par, "trev")
                EV[("xt", t)] = ss.n

            def stage2a(tp):
                if tp < 0:
                    return
                par = tp % 2
                P.wait("scalar", sv, EV[("var", tp)])
                P.emit("scalar", lambda e, par=par: e.activation(
                    st16p[par]["sq"][:], st16p[par]["var"][:], AF.Ln,
                    bias=e64_col[:, 0:1]), [(ss, 1)])
                P.emit("scalar", lambda e, par=par: e.activation(
                    st16p[par]["rstd"][:], st16p[par]["sq"][:], AF.Exp,
                    scale=-0.5, bias=ln8_col[:, 0:1]), [(ss, 1)])
                P.wait("vector", ss)
                P.emit("vector", lambda e, par=par: e.scalar_tensor_tensor(
                    st16p[par]["nmr"][:], st16p[par]["sum"][:], -1.0 / HD,
                    st16p[par]["rstd"][:], ALU.mult, ALU.mult), [(sv, 1)],
                    nodrain=True)
                EV[("nmr", tp)] = sv.n
                VH = H - SH_A
                P.wait("vector", st_, EV.get(("bb_rd", par), 0))
                P.emit("vector", lambda e, par=par, VH=VH: e.tensor_tensor(
                    sqf2[par].ap()[:, HD * SH_A:].rearrange(
                        "p (h c) -> p h c", c=HD),
                    nscr[par].ap()[:, HD * SH_A:].rearrange(
                        "p (h c) -> p h c", c=HD),
                    st16p[par]["rstd"].ap()[:, SH_A:].rearrange(
                        "p (h o) -> p h o", o=1).to_broadcast([128, VH, HD]),
                    ALU.mult), [(sv, 1)], nodrain=True)
                P.emit("vector", lambda e, par=par, VH=VH: e.tensor_tensor(
                    bb[par][:, HD * SH_A:].rearrange("p (h c) -> p h c", c=HD),
                    sqf2[par].ap()[:, HD * SH_A:].rearrange(
                        "p (h c) -> p h c", c=HD),
                    st16p[par]["nmr"].ap()[:, SH_A:].rearrange(
                        "p (h o) -> p h o", o=1).to_broadcast([128, VH, HD]),
                    ALU.add), [(sv, 1)])
                EV[("ttb", tp)] = sv.n
                EV[("nscr_free", par)] = sv.n
                EV[("sqfA_free", par)] = sv.n

            def stage2b(tp, pend_tr):
                if tp < 0:
                    return
                par = tp % 2
                g = NTC * s + tp
                P.wait("scalar", sv, EV[("nmr", tp)])
                P.wait("scalar", st_, EV.get(("bb_rd", par), 0))
                for h in range(SH_A):
                    P.emit("scalar", lambda e, h=h, par=par: e.activation(
                        bb[par][:, HD * h:HD * (h + 1)],
                        nscr[par][:, HD * h:HD * (h + 1)],
                        AF.Identity, bias=st16p[par]["nmr"][:, h:h + 1],
                        scale=st16p[par]["rstd"][:, h:h + 1]),
                        [(ss, 1)] if h == SH_A - 1 else ())
                pend_tr[tp] = (par, g, EV[("ttb", tp)], ss.n)

            wplan = [("wq", cwq, 0), ("wk", cwk, 1), ("wv", cwv, 0),
                     ("wo", cwo, 1)]
            wrdy = {}

            def load_w(idx):
                if idx >= len(wplan):
                    return
                nm, wsrc, slot = wplan[idx]
                P.wait("sync", st_, EV.get(("wfree", slot), 0))
                P.wait("sync", sgp, cast_done[nm])
                P.emit("sync", lambda e, wsrc=wsrc, slot=slot: e.dma_start(
                    out=wcur[slot][:], in_=wsrc), [(sw, 16)])
                wrdy[idx] = sw.n

            load_w(0)
            for pas, (wnm, wdram) in enumerate(
                    [("wq", cwq), ("wk", cwk), ("wv", cwv)][:DBG_A_PASSES]):
                wslot = pas % 2
                pend_tr = {}
                w_ready = wrdy[pas]
                for t in range(NTC):
                    if t == 1:
                        load_w(pas + 1)
                    g = NTC * s + t
                    par = t % 2
                    qb0 = 2 * par
                    P.wait("tensor", sw, w_ready)
                    P.wait("tensor", ss, EV[("xt", t)])
                    fr = EV.get(("qkvfree", par), (0, 0))
                    P.wait("tensor", ss, fr[0])
                    P.wait("tensor", sv, fr[1])
                    for d_ in range(NDC):
                        for nh in range(2):
                            P.emit("tensor",
                                   lambda e, t=t, d_=d_, nh=nh, qb0=qb0, ws=wslot:
                                   e.matmul(bank(qb0 + nh), xTv(t, d_),
                                            wcur[ws][:, D * d_ + 512 * nh:
                                                     D * d_ + 512 * (nh + 1)],
                                            start=(d_ == 0), stop=(d_ == NDC - 1)),
                                   [(st_, 1)] if (d_ == NDC - 1 and nh == 1)
                                   else ())
                    mm_done = st_.n
                    pso = ps[:, 512 * qb0:512 * qb0 + 1024]
                    if pas < 2 and DBG_A_LN > 0:
                        stage2a(t - 1)
                        ev_fn = flush_tr(pas, t - 2, pend_tr)
                        P.wait("scalar", st_, mm_done)
                        P.wait("scalar", sv, EV.get(("nscr_free", par), 0))
                        P.emit("scalar", lambda e, pso=pso, par=par: e.copy(
                            nscr[par][:], pso), [(ss, 1)])
                        EV[("qkvfree", par)] = (ss.n, 0)
                        P.wait("scalar", sv, EV.get(("sqfA_free", par), 0))
                        P.emit("scalar", lambda e, par=par: e.activation(
                            sqf2[par][:], nscr[par][:], AF.Square), [(ss, 1)])
                        sq_done = ss.n
                        P.wait("vector", ss, sq_done - 1)
                        P.emit("vector", lambda e, par=par: e.reduce_sum(
                            st16p[par]["sum"][:],
                            nscr[par].ap().rearrange("p (h c) -> p h c", c=HD),
                            axis=AX.X), [(sv, 1)], nodrain=True)
                        P.wait("vector", ss, sq_done)
                        P.emit("vector", lambda e, par=par: e.reduce_sum(
                            st16p[par]["sq"][:],
                            sqf2[par].ap().rearrange("p (h c) -> p h c", c=HD),
                            axis=AX.X), [(sv, 1)], nodrain=True)
                        P.emit("vector", lambda e, par=par: e.scalar_tensor_tensor(
                            st16p[par]["mean"][:], st16p[par]["sum"][:], -1.0 / HD,
                            st16p[par]["sum"][:], ALU.mult, ALU.mult), [(sv, 1)])
                        P.emit("vector", lambda e, par=par: e.tensor_tensor(
                            st16p[par]["var"][:], st16p[par]["mean"][:],
                            st16p[par]["sq"][:], ALU.add), [(sv, 1)])
                        EV[("var", t)] = sv.n
                        if ev_fn is not None:
                            ev_fn()
                        stage2b(t - 1, pend_tr)
                    elif pas == 2:
                        P.wait("scalar", st_, mm_done)
                        P.emit("scalar", lambda e, g=g, pso=pso: e.copy(
                            vsb[:, 1040 * g:1040 * (g + 1)]
                            .rearrange("p (h c) -> p h c", c=65)[:, :, 0:64],
                            pso.rearrange("p (h c) -> p h c", c=HD)), [(ss, 1)])
                        EV[("qkvfree", par)] = (ss.n, sv.n)

                EV[("wfree", wslot)] = st_.n
                if pas < 2:
                    stage2a(NTC - 1)
                    stage2b(NTC - 1, pend_tr)
                    for tt_ in (NTC - 2, NTC - 1):
                        ev_fn = flush_tr(pas, tt_, pend_tr)
                        if ev_fn is not None:
                            ev_fn()

            if "b" not in DBG_PHASES:
                continue
            wo_ready = wrdy[3]
            P.wait("tensor", sgp, 192)

            for qt in range(NQT):
                Q = NQT * s + qt
                jmax = 4 * (Q + 1)
                P.wait("tensor", ss)
                P.wait("tensor", sv)
                exp_at = {}
                pv_red_at = 0
                ao_start_guard = sv.n

                def kw(bi):
                    if not KEEP_WARM:
                        return
                    P.emit("tensor", lambda e, bi=bi: e.matmul(
                        ps[0:128, 512 * bi:512 * (bi + 1)], ones_b[0:1, :],
                        tri_b[0:1, 0:512], start=True, stop=True))

                def flush_norm(i):
                    pvb = (4, 6, 7)[i % 3]
                    P.wait("tensor", ss, EV.get(("rr", i), 0))
                    P.wait("tensor", sv, EV.get("bc_evac", 0))
                    kw(5)
                    hb = QTL * (i % 2)
                    P.emit("tensor", lambda e, hb=hb: e.matmul(
                        ps[0:64, 512 * 5:512 * 6], ones_b[0:1, 0:64],
                        rrow_b[0:1, hb:hb + QTL],
                        start=True, stop=True), [(st_, 1)])
                    P.wait("vector", st_)
                    P.emit("vector", lambda e, hb=hb: e.tensor_copy(
                        bc_sb[:, hb:hb + QTL], ps[0:64, 512 * 5:512 * 6]),
                        [(sv, 1)])
                    EV["bc_evac"] = sv.n
                    P.emit("vector", lambda e, i=i, pvb=pvb, hb=hb:
                           e.tensor_tensor(
                               aot[0:128, QTL * (i // 2):QTL * (i // 2 + 1)]
                               [64 * (i % 2):64 * (i % 2) + 64, :],
                               ps[0:64, 512 * pvb:512 * pvb + QTL],
                               bc_sb[:, hb:hb + QTL], ALU.mult), [(sv, 1)])
                    EV[("aoTT", i)] = sv.n

                def emit_pv(i, js):
                    pvb = (4, 6, 7)[i % 3]
                    for j in js:
                        if j == 0:
                            P.wait("tensor", sv, EV.get(("aoTT", i - 3), 0))
                            kw(pvb)
                        P.wait("tensor", ss, exp_at.get((i, j // 2), 0))
                        if j >= 4 * Q:
                            P.wait("tensor", sv,
                                   EV.get(("mask", i, j - 4 * Q), 0))
                        P.emit("tensor", lambda e, j=j, i=i, pvb=pvb: e.matmul(
                            ps[0:65, 512 * pvb:512 * pvb + QTL],
                            vsb[:, 1040 * j + 65 * i:1040 * j + 65 * (i + 1)],
                            scrab[:, 512 * j:512 * (j + 1)],
                            start=(j == 0), stop=(j == jmax - 1)),
                            [(st_, 1)] if j == js[-1] else ())
                    if js and js[-1] == jmax - 1:
                        EV[("pvdone", i)] = st_.n

                exp_at = {}
                for i in range(DBG_B_NH + 1):
                    if i >= 3:
                        flush_norm(i - 3)
                    if i < DBG_B_NH:
                        hc, hp = i // 2, 64 * (i % 2)
                        npair = (jmax + 1) // 2
                        for jp in range(npair):
                            if jp % 2 == 0 and i >= 1:
                                q4 = jp // 2
                                emit_pv(i - 1, list(range(4 * q4,
                                                          min(4 * q4 + 4,
                                                              jmax))))
                            js = [2 * jp] + ([2 * jp + 1]
                                             if 2 * jp + 1 < jmax else [])
                            pb0 = (jp % 2) * 2
                            P.wait("tensor", ss, EV.get(("exp_bank", pb0), 0))
                            if i == 0 and jp == 0:
                                P.wait("tensor", ss)
                                P.wait("tensor", sv)
                            kw(pb0)
                            for idx, j in enumerate(js):
                                P.emit("tensor",
                                       lambda e, j=j, idx=idx, pb0=pb0, hc=hc,
                                       hp=hp, qt=qt: e.matmul(
                                           bank(pb0 + idx),
                                           ktall[hp:hp + HD,
                                                 T * hc + 128 * j:
                                                 T * hc + 128 * (j + 1)],
                                           qtseg[hp:hp + HD,
                                                 SEG * hc + QTL * qt:
                                                 SEG * hc + QTL * (qt + 1)],
                                           start=True, stop=True),
                                       [(st_, 1)] if idx == len(js) - 1 else ())
                            P.wait("scalar", st_)
                            ncols = 512 * len(js)
                            P.emit("scalar",
                                   lambda e, jp=jp, pb0=pb0, ncols=ncols:
                                   e.activation(
                                       scrab[:, 1024 * jp:1024 * jp + ncols],
                                       ps[:, 512 * pb0:512 * pb0 + ncols],
                                       AF.Exp, scale=0.125), [(ss, 1)])
                            exp_at[(i, jp)] = ss.n
                            EV[("exp_bank", pb0)] = ss.n
                        EV[("expdone", i)] = ss.n
                        P.wait("vector", ss, EV[("expdone", i)])
                        moff = [0, 128, 384, 768]
                        for di in range(4):
                            j = 4 * Q + di
                            w = 128 * (di + 1)
                            P.emit("vector", lambda e, j=j, di=di, w=w:
                                   e.tensor_tensor(
                                       scrab[:, 512 * j:512 * j + w],
                                       scrab[:, 512 * j:512 * j + w],
                                       tri_b[:, moff[di]:moff[di] + w],
                                       ALU.mult), [(sv, 1)],
                                   nodrain=(di > 0))
                            EV[("mask", i, di)] = sv.n
                        EV[("masks", i)] = sv.n
                    else:
                        emit_pv(i - 1, list(range(jmax)))
                    if i >= 1 and ("rr", i - 1) not in EV:
                        pvb1 = (4, 6, 7)[(i - 1) % 3]
                        P.wait("scalar", st_, EV[("pvdone", i - 1)])
                        hb1 = QTL * ((i - 1) % 2)
                        P.emit("scalar", lambda e, pvb1=pvb1: e.activation(
                            rrow_f[0:1, :],
                            ps[64:65, 512 * pvb1:512 * pvb1 + QTL],
                            AF.Ln), [(ss, 1)])
                        P.emit("scalar", lambda e, hb1=hb1: e.activation(
                            rrow_b[0:1, hb1:hb1 + QTL],
                            rrow_f[0:1, :], AF.Exp, scale=-1.0), [(ss, 1)])
                        EV[("rr", i - 1)] = ss.n
                flush_norm(DBG_B_NH - 2)
                flush_norm(DBG_B_NH - 1)
                for i in range(DBG_B_NH + 1):
                    EV.pop(("rr", i), None)
                    EV.pop(("masks", i), None)
                    EV.pop(("expdone", i), None)
                ao_done = sv.n

                for tc in range(4 if DBG_B_LVL >= 6 else 0):
                    t = 4 * qt + tc
                    g = NTC * s + t
                    par = t % 2
                    P.wait("sync", sv, EV.get(("xfree", par), 0))
                    P.wait("sync", sop[par], EV.get(("ofree", par), 0))
                    P.emit("sync", lambda e, b=b, g=g, par=par: e.dma_start(
                        out=xf[par][:], in_=dx[b, 128 * g:128 * (g + 1), :]),
                        [(sxp[par], 16)])
                    xr_ld = sxp[par].n
                    P.wait("tensor", sv, ao_done)
                    P.wait("tensor", sw, wo_ready)
                    P.wait("tensor", sv, EV.get("oev", 0))
                    for c in range(NDC):
                        for nh in range(2):
                            P.emit("tensor", lambda e, c=c, nh=nh, tc=tc:
                                   e.matmul(bank(6 + nh),
                                            aot[:, QTL * c + 128 * tc:
                                                QTL * c + 128 * (tc + 1)],
                                            wcur[1][:, D * c + 512 * nh:
                                                    D * c + 512 * (nh + 1)],
                                            start=(c == 0), stop=False))
                    for nh in range(2):
                        P.emit("tensor", lambda e, nh=nh: e.matmul(
                            bank(6 + nh), ones_b[0:1, :],
                            bo_row_b[0:1, 512 * nh:512 * (nh + 1)],
                            start=False, stop=True),
                            [(st_, 1)] if nh == 1 else ())
                    o_done = st_.n
                    P.wait("vector", st_, o_done)
                    P.wait("vector", sxp[par], xr_ld)
                    P.emit("vector", lambda e, par=par: e.scalar_tensor_tensor(
                        nscr[par][:], ps[:, 512 * 6:512 * 8], 1.0, xf[par][:],
                        ALU.mult, ALU.add, accum_out=st1["sum"][:, 0:1]),
                        [(sv, 1)])
                    EV[("xfree", par)] = sv.n
                    EV["oev"] = sv.n
                    P.emit("vector", lambda e, par=par: e.scalar_tensor_tensor(
                        sqf[:], nscr[par][:], 1.0, nscr[par][:],
                        ALU.mult, ALU.mult, accum_out=st1["sq"][:, 0:1]),
                        [(sv, 1)])
                    P.emit("vector", lambda e: e.tensor_scalar_mul(
                        st1["mean"][:], st1["sum"][:], 1.0 / D), [(sv, 1)])
                    P.emit("vector", lambda e: e.tensor_tensor(
                        st1["var"][:], st1["mean"][:], st1["mean"][:], ALU.mult),
                        [(sv, 1)])
                    P.emit("vector", lambda e: e.scalar_tensor_tensor(
                        st1["var"][:], st1["sq"][:], 1.0 / D, st1["var"][:],
                        ALU.mult, ALU.subtract), [(sv, 1)])
                    P.wait("scalar", sv)
                    P.emit("scalar", lambda e: e.activation(
                        st1["sq"][:], st1["var"][:], AF.Ln,
                        bias=eps_col[:, 0:1]), [(ss, 1)])
                    P.emit("scalar", lambda e: e.activation(
                        st1["rstd"][:], st1["sq"][:], AF.Exp, scale=-0.5),
                        [(ss, 1)])
                    P.wait("vector", ss)
                    P.emit("vector", lambda e, t=t, par=par: e.tensor_scalar(
                        h1f[:, D * t:D * (t + 1)], nscr[par][:],
                        st1["mean"][:, 0:1], st1["rstd"][:, 0:1],
                        ALU.subtract, ALU.mult), [(sv, 1)])
                    EV[("h1", t)] = sv.n

            EV["ps47_dve"] = sems["sv"].n
            EV[("wfree", 1)] = st_.n
            EV["aot_rd"] = st_.n

            if "c" not in DBG_PHASES:
                continue

            def unit_srcs(u):
                if u < NE:
                    return (cgw[u], cuw[u], cdw[u], True)
                qd = u - NE
                return (csg[qd], csu[qd], csd[qd], False)

            def load_unit_gu(u):
                gsrc, usrc, dsrc, _ = unit_srcs(u)
                slot = u % 2
                P.wait("sync", sgp, cast_done["uw" if u < NE else "su"])
                P.emit("sync", lambda e, gsrc=gsrc, slot=slot: e.dma_start(
                    out=gub[slot][:, 0:4096], in_=gsrc), [(sgu[slot], 16)])
                P.emit("sync", lambda e, usrc=usrc, slot=slot: e.dma_start(
                    out=gub[slot][:, 4096:8192], in_=usrc), [(sgu[slot], 16)])
                return sgu[slot].n

            def load_unit_dn(u):
                gsrc, usrc, dsrc, _ = unit_srcs(u)
                P.wait("sync", sgp, cast_done["dw" if u < NE else "sd"])
                P.emit("sync", lambda e, dsrc=dsrc: e.dma_start(
                    out=down_sb.ap()[:, :], in_=dsrc), [(sdn, 16)])
                return sdn.n

            pend_c = {}

            def flush_c(t):
                if t not in pend_c:
                    return
                par, cast_done_c = pend_c.pop(t)
                P.wait("tensor", sv, cast_done_c)
                tr_evac(bb[par],
                        wcur[0].ap().rearrange("p (d c) -> p d c", c=SEG)[
                            :, :, 128 * t:128 * (t + 1)], par, "trev")
                EV[("h1t", t)] = ss.n

            for t in range(NTC):
                par = t % 2
                P.wait("vector", sv, EV[("h1", t)])
                P.wait("vector", st_, EV.get(("bb_rd", par), 0))
                P.wait("vector", st_, EV.get("aot_rd", 0))
                P.emit("vector", lambda e, t=t, par=par: e.tensor_copy(
                    bb[par][:], h1f[:, D * t:D * (t + 1)]), [(sv, 1)])
                pend_c[t] = (par, sv.n)
                flush_c(t - 1)
                if t < NTC - 1:
                    continue
                flush_c(t)
            P.wait("tensor", sgp, 192)
            P.wait("tensor", sv, EV.get("router_read", 0))
            for t in range(NTC):
                P.wait("tensor", ss, EV[("h1t", t)])
                for d_ in range(NDC):
                    P.emit("tensor", lambda e, t=t, d_=d_: e.matmul(
                        ps[:, NE * t:NE * (t + 1)],
                        h1T[:, SEG * d_ + 128 * t:SEG * d_ + 128 * (t + 1)],
                        rw_sb[:, NE * d_:NE * (d_ + 1)],
                        start=(d_ == 0), stop=(d_ == NDC - 1)),
                        [(st_, 1)] if d_ == NDC - 1 else ())
            rt_done = st_.n

            def r3(t_):
                return t_.ap().rearrange("p (t e) -> p t e", e=NE)

            def c3(t_):
                return t_.ap().rearrange("p (t o) -> p t o", o=1).to_broadcast(
                    [128, NTC, NE])

            P.wait("vector", st_, rt_done)
            P.emit("vector", lambda e: e.tensor_copy(
                r8["logit"][:], bank(0, NTC * NE)), [(sv, 1)])
            EV["router_read"] = sv.n
            P.emit("vector", lambda e: e.reduce_max(
                rcol["nmax"][:], r3(r8["logit"]), axis=AX.X, negate=True),
                [(sv, 1)])
            P.emit("vector", lambda e: e.tensor_tensor(
                r3(r8["logit"]), r3(r8["logit"]), c3(rcol["nmax"]), ALU.add),
                [(sv, 1)])
            P.wait("scalar", sv)
            P.emit("scalar", lambda e: e.activation(
                r8["score"][:], r8["logit"][:], AF.Exp), [(ss, 1)])
            P.wait("vector", ss)
            P.emit("vector", lambda e: e.reduce_sum(
                rcol["den"][:], r3(r8["score"]), axis=AX.X), [(sv, 1)])
            P.emit("vector", lambda e: e.reciprocal(
                rcol["rden"][:], rcol["den"][:]), [(sv, 1)])
            P.emit("vector", lambda e: e.tensor_tensor(
                r3(r8["score"]), r3(r8["score"]), c3(rcol["rden"]), ALU.mult),
                [(sv, 1)])
            P.emit("vector", lambda e: e.tensor_scalar_add(
                r8["corr"][:], r8["score"][:], rbias_col[:, 0:1]), [(sv, 1)])
            for t in range(NTC):
                P.emit("vector", lambda e, t=t: e.max(
                    out=m8w[:, NE * t:NE * (t + 1)],
                    in_=r8["corr"][:, NE * t:NE * (t + 1)]), [(sv, 1)],
                    nodrain=(t > 0))
            thr3 = m8w.ap().rearrange("p (t e) -> p t e", e=NE)[
                :, :, 3:4].to_broadcast([128, NTC, NE])
            P.emit("vector", lambda e, thr3=thr3: e.tensor_tensor(
                r3(r8["sel"]), r3(r8["corr"]), thr3, ALU.is_ge), [(sv, 1)])
            P.emit("vector", lambda e: e.tensor_mul(
                r8["msk"][:], r8["sel"][:], r8["score"][:]), [(sv, 1)])
            P.emit("vector", lambda e: e.reduce_sum(
                rcol["den4"][:], r3(r8["msk"]), axis=AX.X), [(sv, 1)])
            P.emit("vector", lambda e: e.tensor_scalar_add(
                rcol["den4"][:], rcol["den4"][:], 1e-20), [(sv, 1)])
            P.emit("vector", lambda e: e.reciprocal(
                rcol["rden4"][:], rcol["den4"][:]), [(sv, 1)])
            P.emit("vector", lambda e: e.tensor_tensor(
                w_seg.ap().rearrange("p (t e) -> p t e", e=NE),
                r3(r8["msk"]), c3(rcol["rden4"]), ALU.mult), [(sv, 1)])
            for t in range(NTC):
                EV[("w", t)] = sv.n

            P.wait("sync", st_)
            u_gu = {0: load_unit_gu(0)}
            u_dn = {0: load_unit_dn(0)}
            EV["pipeline_prev"] = None

            def emit_down(u, tq, act_done, routed):
                tc0, tc1 = 2 * tq, 2 * tq + 1
                apar = 1024 * (tq % 2)
                P.wait("tensor", sv, act_done)
                P.wait("tensor", sdn, u_dn[u])
                P.wait("tensor", sv, EV.get("dadd_last", 0))
                for tci in range(2):
                    for m in range(4):
                        for nh in range(2):
                            P.emit("tensor",
                                   lambda e, tci=tci, m=m, nh=nh, apar=apar:
                                   e.matmul(
                                       bank(4 + 2 * tci + nh),
                                       actw[:, apar + 256 * m + 128 * tci:
                                            apar + 256 * m + 128 * tci + 128],
                                       down_sb[:, D * m + 512 * nh:
                                               D * m + 512 * (nh + 1)],
                                       start=(m == 0), stop=(m == 3)),
                                   [(st_, 1)]
                                   if (m == 3 and nh == 1 and tci == 1)
                                   else ())
                dn_done = st_.n
                EV[("act_read_t", tq % 2)] = dn_done
                P.wait("vector", st_, dn_done)
                for tci, tcg in enumerate((tc0, tc1)):
                    pd = 512 * (4 + 2 * tci)
                    if routed:
                        P.emit("vector", lambda e, tcg=tcg, pd=pd, u=u:
                               e.scalar_tensor_tensor(
                                   h1f[:, D * tcg:D * (tcg + 1)],
                                   ps[:, pd:pd + 1024],
                                   w_seg[:, NE * tcg + u:NE * tcg + u + 1],
                                   h1f[:, D * tcg:D * (tcg + 1)],
                                   ALU.mult, ALU.add), [(sv, 1)])
                    else:
                        P.emit("vector", lambda e, tcg=tcg, pd=pd: e.tensor_add(
                            h1f[:, D * tcg:D * (tcg + 1)],
                            h1f[:, D * tcg:D * (tcg + 1)],
                            ps[:, pd:pd + 1024]), [(sv, 1)])
                EV["dadd_last"] = sv.n
                EV[("dadd", (u, tq))] = sv.n
            for u in range(NU):
                slot = u % 2
                routed = u < NE
                for tq in range(NTQ):
                    tc0, tc1 = 2 * tq, 2 * tq + 1
                    P.wait("tensor", sgu[u % 2], u_gu[u])
                    P.wait("tensor", sv, EV.get("router_read", 0))
                    P.wait("tensor", ss, EV.get("guev_s", 0))
                    P.wait("tensor", sv, EV.get("guev_v", 0))
                    for mat in range(2):
                        for m in range(4):
                            for d_ in range(NDC):
                                P.emit("tensor",
                                       lambda e, mat=mat, m=m, d_=d_, tq=tq,
                                       slot=slot: e.matmul(
                                           ps[:, 1024 * mat + 256 * m:
                                              1024 * mat + 256 * (m + 1)],
                                           gub[slot][:, 4096 * mat + IE * d_ +
                                                     128 * m:
                                                     4096 * mat + IE * d_ +
                                                     128 * (m + 1)],
                                           h1T[:, SEG * d_ + TOKQ * tq:
                                               SEG * d_ + TOKQ * (tq + 1)],
                                           start=(d_ == 0),
                                           stop=(d_ == NDC - 1)),
                                       [(st_, 1)]
                                       if (d_ == NDC - 1 and m == 3) else ())
                    gu_done = st_.n
                    if tq == 0 and u + 1 < NU:
                        P.wait("sync", st_)
                        u_gu[u + 1] = load_unit_gu(u + 1)
                    P.wait("scalar", st_, gu_done)
                    P.wait("scalar", sv, EV.get("act_read_v", 0))
                    P.emit("scalar", lambda e: e.activation(
                        silu_sb[:, 0:1024], ps[:, 0:1024], AF.Silu), [(ss, 1)])
                    EV["guev_s"] = ss.n
                    P.wait("vector", st_, gu_done)
                    P.wait("vector", ss)
                    P.emit("vector", lambda e: e.tensor_copy(
                        silu_sb[:, 1024:2048], ps[:, 1024:2048]), [(sv, 1)],
                        nodrain=True)
                    EV["guev_v"] = sv.n
                    P.wait("vector", st_, EV.get(("act_read_t", tq % 2), 0))
                    apar = 1024 * (tq % 2)
                    P.emit("vector", lambda e, apar=apar: e.tensor_tensor(
                        actw[:, apar:apar + 1024], silu_sb[:, 0:1024],
                        silu_sb[:, 1024:2048], ALU.mult), [(sv, 1)])
                    act_done = sv.n
                    EV["act_read_v"] = act_done
                    prev = EV.get("pipeline_prev")
                    EV["pipeline_prev"] = (u, tq, act_done, routed)
                    if prev is not None:
                        emit_down(*prev)
                        if prev[1] == NTQ - 1 and u not in u_dn:
                            P.wait("sync", st_)
                            u_dn[u] = load_unit_dn(u)


            if EV.get("pipeline_prev") is not None:
                emit_down(*EV["pipeline_prev"])
                EV["pipeline_prev"] = None

            EV["ps47_dve"] = sems["sv"].n
            EV[("wfree", 0)] = st_.n
            for t in range(NTC):
                par = t % 2
                P.wait("vector", sv, EV[("dadd", (NU - 1, t // 2))])
                P.emit("vector", lambda e, t=t: e.scalar_tensor_tensor(
                    sqf[:], h1f[:, D * t:D * (t + 1)], 1.0,
                    h1f[:, D * t:D * (t + 1)], ALU.mult, ALU.mult,
                    accum_out=st1["sq"][:, 0:1]), [(sv, 1)])
                P.emit("vector", lambda e, t=t: e.reduce_sum(
                    st1["sum"][:], h1f[:, D * t:D * (t + 1)], axis=AX.X),
                    [(sv, 1)])
                P.emit("vector", lambda e: e.tensor_scalar_mul(
                    st1["mean"][:], st1["sum"][:], 1.0 / D), [(sv, 1)])
                P.emit("vector", lambda e: e.tensor_tensor(
                    st1["var"][:], st1["mean"][:], st1["mean"][:], ALU.mult),
                    [(sv, 1)])
                P.emit("vector", lambda e: e.scalar_tensor_tensor(
                    st1["var"][:], st1["sq"][:], 1.0 / D, st1["var"][:],
                    ALU.mult, ALU.subtract), [(sv, 1)])
                P.wait("scalar", sv)
                P.emit("scalar", lambda e: e.activation(
                    st1["sq"][:], st1["var"][:], AF.Ln, bias=eps_col[:, 0:1]),
                    [(ss, 1)])
                P.emit("scalar", lambda e: e.activation(
                    st1["rstd"][:], st1["sq"][:], AF.Exp, scale=-0.5), [(ss, 1)])
                P.wait("vector", ss)
                P.wait("vector", sop[par], EV.get(("ofree", par), 0))
                P.emit("vector", lambda e, t=t, par=par: e.tensor_scalar(
                    xf[par][:], h1f[:, D * t:D * (t + 1)],
                    st1["mean"][:, 0:1], st1["rstd"][:, 0:1],
                    ALU.subtract, ALU.mult), [(sv, 1)])
                apply_done = sv.n
                P.wait("sync", sv, apply_done)
                g = NTC * s + t
                P.emit("sync", lambda e, b=b, g=g, par=par: e.dma_start(
                    out=dout[b, 128 * g:128 * (g + 1), :], in_=xf[par][:]),
                    [(sop[par], 16)])
                EV[("ofree", par)] = sop[par].n

    P.wait("sync", sop[0])
    P.wait("sync", sop[1])

    with nc.Block() as block:
        @block.sync
        def _(e):
            P.run("sync", e)

        @block.gpsimd
        def _(e):
            P.run("gpsimd", e)

        @block.tensor
        def _(e):
            P.run("tensor", e)

        @block.scalar
        def _(e):
            P.run("scalar", e)

        @block.vector
        def _(e):
            P.run("vector", e)

    es.close()
    return nc


_NC_CACHE = None


def prep_common(L0_attn_Wk, L0_attn_Wq, L0_attn_Wv, L0_attn_Wo, L0_attn_bo,
                L0_attn_kln_w, L0_attn_kln_b, L0_attn_qln_w, L0_attn_qln_b,
                L0_ffn_router_w, L0_ffn_router_bias, L0_ffn_gate_w,
                L0_ffn_up_w, L0_ffn_down_w, L0_ffn_s_gate_w, L0_ffn_s_up_w,
                L0_ffn_s_down_w):
    f32 = lambda a: np.ascontiguousarray(np.asarray(a, dtype=np.float32))
    def lay2(w):
        w = np.asarray(w, np.float32)
        return np.ascontiguousarray(
            w.reshape(-1, 128, w.shape[-1]).transpose(1, 0, 2))

    def lay3(w):
        w = np.asarray(w, np.float32)
        return np.ascontiguousarray(
            w.reshape(w.shape[0], -1, 128, w.shape[-1]).transpose(0, 2, 1, 3))

    sg_q = np.stack([f32(L0_ffn_s_gate_w)[:, 512 * q:512 * (q + 1)]
                     for q in range(4)])
    su_q = np.stack([f32(L0_ffn_s_up_w)[:, 512 * q:512 * (q + 1)]
                     for q in range(4)])
    sd_q = np.stack([f32(L0_ffn_s_down_w)[512 * q:512 * (q + 1), :]
                     for q in range(4)])
    common = dict(
        wq=lay2(L0_attn_Wq), wk=lay2(L0_attn_Wk), wv=lay2(L0_attn_Wv),
        wo=lay2(L0_attn_Wo), bo=f32(L0_attn_bo).reshape(1, D),
        klw=f32(L0_attn_kln_w).reshape(1, HD),
        klb=f32(L0_attn_kln_b).reshape(1, HD),
        qlw=f32(L0_attn_qln_w).reshape(1, HD),
        qlb=f32(L0_attn_qln_b).reshape(1, HD),
        rw=lay2(L0_ffn_router_w), rb=f32(L0_ffn_router_bias).reshape(1, 1),
        gw=lay3(L0_ffn_gate_w), uw=lay3(L0_ffn_up_w), dw=lay3(L0_ffn_down_w),
        sg=lay3(sg_q), su=lay3(su_q), sd=lay3(sd_q),
        ident=np.eye(128, dtype=ml_dtypes.bfloat16),
        tri=_diag_masks(),
    )
    return common


def kernel(L0_attn_x, L0_attn_Wk, L0_attn_Wq, L0_attn_Wv, L0_attn_Wo,
           L0_attn_bo, L0_attn_kln_w, L0_attn_kln_b, L0_attn_qln_w,
           L0_attn_qln_b, L0_ffn_router_w, L0_ffn_router_bias, L0_ffn_gate_w,
           L0_ffn_up_w, L0_ffn_down_w, L0_ffn_s_gate_w, L0_ffn_s_up_w,
           L0_ffn_s_down_w):
    global _NC_CACHE
    x = np.ascontiguousarray(np.asarray(L0_attn_x, dtype=np.float32))
    common = prep_common(L0_attn_Wk, L0_attn_Wq, L0_attn_Wv, L0_attn_Wo,
                         L0_attn_bo, L0_attn_kln_w, L0_attn_kln_b,
                         L0_attn_qln_w, L0_attn_qln_b, L0_ffn_router_w,
                         L0_ffn_router_bias, L0_ffn_gate_w, L0_ffn_up_w,
                         L0_ffn_down_w, L0_ffn_s_gate_w, L0_ffn_s_up_w,
                         L0_ffn_s_down_w)
    in_maps = []
    for c in range(NCORES):
        m = dict(common)
        m["x"] = np.ascontiguousarray(x[NB * c:NB * (c + 1)])
        in_maps.append(m)
    if _NC_CACHE is None:
        _NC_CACHE = build_core()
    res = run_bass_kernel_spmd(_NC_CACHE, in_maps, core_ids=list(range(NCORES)))
    return np.concatenate([np.asarray(r["out"]) for r in res.results], axis=0)


def _diag_masks():
    parts = []
    for i in range(4):
        w = 128 * (i + 1)
        q = np.arange(w)[None, :]
        k = np.arange(128)[:, None]
        parts.append((q >= 128 * i + k).astype(np.float32))
    return np.concatenate(parts, axis=1).astype(ml_dtypes.bfloat16)


if __name__ == "__main__":
    nc = build_core()
    print("built ok; instructions:", nc.n_instructions()
          if hasattr(nc, "n_instructions") else "?")
